# version 24
# baseline (speedup 1.0000x reference)
import contextlib
import numpy as np

import concourse.bass as bass
import concourse.mybir as mybir
from concourse.bass_utils import run_bass_kernel_spmd

F32 = mybir.dt.float32
BF16 = mybir.dt.bfloat16
ALU = mybir.AluOpType
ACTF = mybir.ActivationFunctionType
AX = mybir.AxisListType

C = 1024
NCH = 8
R = 64
I = 3072
NI = 24
INV_QMAX = 1.0 / 32767.0
INV_WQMAX = 1.0 / 7.0
U16_CAP = 65535.4921875
NEGBIG = -1.0e30


class Sem:
    def __init__(self, ctx, nc, name):
        self.h = ctx.enter_context(nc.semaphore(name))
        self.n = 0

    def inc(self, inst, k=1):
        inst.then_inc(self.h, k)
        self.n += k
        return self.n


def build_nc(T=4096):
    NT = T // 128
    NQ = T // 512
    TQ = T // 4
    NTQ = TQ // 128
    nc = bass.Bass("TRN2", target_bir_lowering=False, debug=False)
    ctx = contextlib.ExitStack()

    x_in = nc.dram_tensor("x_in", [T, C], F32, kind="ExternalInput").ap()
    Wb = nc.dram_tensor("Wb", [R, C], F32, kind="ExternalInput").ap()
    Wv = nc.dram_tensor("Wv", [C, C], F32, kind="ExternalInput").ap()
    bv = nc.dram_tensor("bv", [1, C], F32, kind="ExternalInput").ap()
    Wo = nc.dram_tensor("Wo", [C, C], F32, kind="ExternalInput").ap()
    bo = nc.dram_tensor("bo", [1, C], F32, kind="ExternalInput").ap()
    Wg = nc.dram_tensor("Wg", [I, C], F32, kind="ExternalInput").ap()
    Wu = nc.dram_tensor("Wu", [I, C], F32, kind="ExternalInput").ap()
    Wd = nc.dram_tensor("Wd", [C, I], F32, kind="ExternalInput").ap()
    iden_in = nc.dram_tensor("iden_in", [128, 128], F32, kind="ExternalInput").ap()
    maskneg_in = nc.dram_tensor("maskneg_in", [128, 128], F32, kind="ExternalInput").ap()
    ones_in = nc.dram_tensor("ones_in", [1, 128], F32, kind="ExternalInput").ap()
    out = nc.dram_tensor("out", [T, C], F32, kind="ExternalOutput").ap()

    x2_hbm = nc.dram_tensor("x2_hbm", [T, C], F32, kind="Internal").ap()
    U_hbm = nc.dram_tensor("U_hbm", [I, T], BF16, kind="Internal").ap()
    G_hbm = nc.dram_tensor("G_hbm", [I, T], BF16, kind="Internal").ap()
    QQ_hbm = nc.dram_tensor("QQ_hbm", [I, T], F32, kind="Internal").ap()
    WdnT_hbm = nc.dram_tensor("WdnT_hbm", [128, (I // 128) * C], BF16, kind="Internal").ap()
    cc_in, cc_out = {}, {}
    for bn in ("X", "A", "B", "Cc"):
        cc_in[bn] = nc.dram_tensor(f"cc_in_{bn}", [1, 8], F32, kind="Internal").ap()
        cc_out[bn] = nc.dram_tensor(f"cc_out_{bn}", [1, 8], F32, kind="Internal",
                                    addr_space="Shared").ap()

    def sb(name, shape, dt, stack=ctx, side=None):
        if side is None:
            return stack.enter_context(nc.sbuf_tensor(name, shape, dt))
        return stack.enter_context(nc.sbuf_tensor(name, shape, dt, side=side))

    iden_f = sb("iden_f", [128, 128], F32)
    iden_b = sb("iden_b", [128, 128], BF16)
    ones_f = sb("ones_f", [1, 128], F32)
    ones_b = sb("ones_b", [1, 128], BF16)
    mcol = sb("mcol", [128, 1], F32)
    nmcol = sb("nmcol", [128, 1], F32)
    tmpc = sb("tmpc", [128, 4], F32)
    scols = sb("scols", [128, 16], F32)
    SX, ISX, SG, ISG, SU, ISU, SGQ, ISB, ISCQ, SC = range(10)
    srow = sb("srow", [1, 16], F32)
    s1t = sb("s1t", [128, NT], F32)
    s2t = sb("s2t", [128, NT], F32)
    rstdt = sb("rstdt", [128, NT], F32)
    nmbt = sb("nmbt", [128, NT], F32)
    tmpstat = sb("tmpstat", [128, NT], F32)
    statx = sb("statx", [128, NT], F32)
    ustat = sb("ustat", [128, 2 * NI], F32)
    gstat = sb("gstat", [128, 2 * NI], F32)
    gstat2 = sb("gstat2", [128, 2 * NI], F32)
    qstat = sb("qstat", [128, NI], F32)
    padt = sb("padt", [128, 128], F32)
    mparts = sb("mparts", [128, 16], F32)
    zparts = sb("zparts", [128, 16], F32)
    nbias = sb("nbias", [128, 8], F32)
    zcol = sb("zcol", [128, 8], F32)
    rzcol = sb("rzcol", [128, 8], F32)

    x2t = [sb(f"x2t{i}", [128, C], F32) for i in range(2)]
    xlt = [sb(f"xlt{i}", [128, C], BF16) for i in range(2)]
    wkt = [sb(f"wkt{i}", [128, C], F32) for i in range(2)]

    psl = lambda name: ctx.enter_context(nc.psum_tensor(name, [128, 512], F32))
    pA, pB, pC, pD, pE, pF = psl("pA"), psl("pB"), psl("pC"), psl("pD"), psl("pE"), psl("pF")
    pT = ctx.enter_context(nc.psum_tensor("pT", [128, 1024], BF16))
    pT2 = ctx.enter_context(nc.psum_tensor("pT2", [128, 1024], BF16))

    stk1 = contextlib.ExitStack()
    stk2a = contextlib.ExitStack()
    stk2b = contextlib.ExitStack()
    stk3 = contextlib.ExitStack()
    stk4o = contextlib.ExitStack()
    stk4a = contextlib.ExitStack()
    stk4b = contextlib.ExitStack()

    stk0 = contextlib.ExitStack()
    xlnT = sb("xlnT", [128, NCH * T], BF16, stk1, side="right")
    WvT = sb("WvT", [128, NCH * C], BF16, stk1, side="right")
    WbT = sb("WbT", [128, NCH * R], BF16, stk1, side="right")
    wtr = [sb(f"wtr{i}", [128, C], F32, stk0, side="right") for i in range(2)]
    wtb = [sb(f"wtb{i}", [128, C], BF16, stk0, side="right") for i in range(2)]
    wbraw = sb("wbraw", [64, C], F32, stk0, side="right")
    bv_f = sb("bv_f", [1, C], F32, stk0, side="right")
    bo_f = sb("bo_f", [1, C], F32, stk0, side="right")
    WoT = sb("WoT", [128, NCH * C], BF16, stk2a)
    xsq = [sb(f"xsq{i}", [128, C], F32, stk2a) for i in range(2)]
    LT = sb("LT", [64, T], BF16, stk2a)
    bo_bc = sb("bo_bc", [128, C], F32, stk2a)
    bv_b = sb("bv_b", [1, C], BF16, stk2a)

    class _RecInst:
        __slots__ = ("incs",)
        def __init__(self):
            self.incs = []
        def then_inc(self, sem, v):
            self.incs.append((sem, v))
            return self

    class _RecEngine:
        def __init__(self):
            self.ops = []
        def __getattr__(self, attr):
            ops = self.ops
            def call(*a, **k):
                inst = _RecInst()
                ops.append((attr, a, k, inst))
                return inst
            return call

    class _RecBlock:
        def __init__(self):
            self.eng = {n: _RecEngine() for n in
                        ("sync", "scalar", "vector", "tensor", "gpsimd")}
        def __getattr__(self, name):
            e = self.eng[name]
            return lambda fn: fn(e)

    block = _RecBlock()
    if True:
        mk = lambda name: Sem(ctx, nc, name)
        lc = mk("lc")
        lx = [mk("lx0"), mk("lx1")]
        lw = [mk("lw0"), mk("lw1")]
        lr = [mk("lr0"), mk("lr1")]
        lg = [mk("lg0"), mk("lg1")]
        sp = [mk("sp0"), mk("sp1")]
        lu = mk("lu")
        wds = mk("wds")
        sq = [mk("sq0"), mk("sq1")]
        wl = mk("wl")
        so = [mk("so0"), mk("so1")]
        ts = mk("ts")
        vs = mk("vs")
        ss = mk("ss")
        gs = mk("gs")
        ptg = [0]
        ptgv = [0]
        pt2g = [0]

        def rnd(vector, dst, src):
            return vector.tensor_scalar(dst, src, mcol[:], nmcol[:], ALU.add, ALU.add)

        def _(sync):
            for dst, src in ((iden_f[:], iden_in),
                             (ones_f[:], ones_in), (bv_f[:], bv), (bo_f[:], bo),
                             (wbraw[:], Wb)):
                lc.inc(sync.dma_start(out=dst, in_=src), 16)
        block.sync(_)

        def _(vector):
            vector.wait_ge(lc.h, lc.n)
            def ch(inst):
                vs.inc(inst); vector.wait_ge(vs.h, vs.n)
            ch(vector.memset(tmpc[:, 0:1], 0.75))
            ch(vector.reciprocal(tmpc[:, 0:1], tmpc[:, 0:1]))
            ch(vector.tensor_scalar(tmpc[:, 0:1], tmpc[:, 0:1], -1.0, 3.0, ALU.add, ALU.mult))
            ch(vector.tensor_scalar(tmpc[:, 0:1], tmpc[:, 0:1], -1.0, None, ALU.add))
            ch(vector.tensor_scalar(tmpc[:, 1:2], tmpc[:, 0:1], -1.0, None, ALU.mult))
            ch(vector.tensor_tensor(tmpc[:, 0:1], tmpc[:, 0:1], tmpc[:, 1:2], ALU.max))
            ch(vector.reciprocal(tmpc[:, 0:1], tmpc[:, 0:1]))
            ch(vector.tensor_scalar(mcol[:], tmpc[:, 0:1], 1.5, None, ALU.mult))
            ch(vector.tensor_scalar(nmcol[:], mcol[:], -1.0, None, ALU.mult))
            vector.memset(zparts[:], 0.0)
            vector.memset(srow[:], 0.0)
            vector.tensor_copy(iden_b[:], iden_f[:])
            vector.tensor_copy(ones_b[:], ones_f[:])
            vector.tensor_copy(wtb[0][0:64, :], wbraw[:])
            vs.inc(vector.tensor_copy(bv_b[:], bv_f[:]))
        block.vector(_)
        P0_V = vs.n

        def _(tensor):
            tensor.wait_ge(vs.h, P0_V)
            tensor.matmul(pA[:], ones_f[:], bo_f[:, 0:512], start=True, stop=True)
            ts.inc(tensor.matmul(pB[:], ones_f[:], bo_f[:, 512:1024], start=True, stop=True))
        block.tensor(_)
        bo_ts = ts.n

        def _(vector):
            vector.wait_ge(ts.h, bo_ts)
            vector.tensor_copy(bo_bc[:, 0:512], pA[:])
            vs.inc(vector.tensor_copy(bo_bc[:, 512:1024], pB[:]))
        block.vector(_)
        bo_v = vs.n

        def _(tensor):
            for ci in range(NCH):
                inst = tensor.transpose(pT2[:, ci * R:(ci + 1) * R],
                                        wtb[0][0:64, ci * 128:(ci + 1) * 128],
                                        iden_b[0:64, 0:64])
            ts.inc(inst)
        block.tensor(_)
        wb_ts = ts.n

        def _(scalar):
            scalar.wait_ge(ts.h, wb_ts)
            ss.inc(scalar.copy(WbT[:], pT2[:, 0:NCH * R]))
            pt2g[0] = ss.n
        block.scalar(_)

        attn_cast = []
        attn_tr = [ts.n, ts.n]
        def prep_attn_weight(Wsrc, dst):
            cast_ms, tr_ms = attn_cast, attn_tr
            for rt in range(NCH):
                slot = (len(cast_ms)) % 2
                def _(sync, rt=rt, slot=slot):
                    if len(cast_ms) >= 2:
                        sync.wait_ge(vs.h, cast_ms[-2])
                    lw[slot].inc(sync.dma_start(out=wtr[slot][:],
                                                in_=Wsrc[rt * 128:(rt + 1) * 128, :]), 16)
                block.sync(_)
                def _(vector, rt=rt, slot=slot):
                    vector.wait_ge(lw[slot].h, lw[slot].n)
                    vector.wait_ge(ts.h, tr_ms[-2])
                    vs.inc(vector.tensor_copy(wtb[slot][:], wtr[slot][:]))
                    cast_ms.append(vs.n)
                block.vector(_)
                def _(tensor, rt=rt, slot=slot):
                    tensor.wait_ge(vs.h, cast_ms[-1])
                    tensor.wait_ge(ss.h, pt2g[0])
                    for ci in range(NCH):
                        inst = tensor.transpose(pT2[:, ci * 128:(ci + 1) * 128],
                                                wtb[slot][:, ci * 128:(ci + 1) * 128], iden_b[:])
                    ts.inc(inst)
                    tr_ms.append(ts.n)
                block.tensor(_)
                def _(scalar, rt=rt):
                    scalar.wait_ge(ts.h, tr_ms[-1])
                    for ci in range(NCH):
                        inst = scalar.copy(dst[:, ci * C + rt * 128: ci * C + rt * 128 + 128],
                                           pT2[:, ci * 128:(ci + 1) * 128])
                    ss.inc(inst)
                    pt2g[0] = ss.n
                block.scalar(_)

        prep_attn_weight(Wv, WvT)
        prep_attn_weight(Wo, WoT)
        P0_SS_END = ss.n
        stk0.close()

        red_ms, sq_ms = [], []
        for t in range(NT):
            slot = t % 2
            def _(sync, t=t, slot=slot):
                if t >= 2:
                    sync.wait_ge(vs.h, red_ms[t - 2])
                    sync.wait_ge(ss.h, sq_ms[t - 2])
                lx[slot].inc(sync.dma_start(out=x2t[slot][:],
                                            in_=x_in[t * 128:(t + 1) * 128, :]), 16)
            block.sync(_)
            def _(scalar, t=t, slot=slot):
                scalar.wait_ge(lx[slot].h, lx[slot].n)
                if t >= 2:
                    scalar.wait_ge(ss.h, sq_ms[t - 2])
                ss.inc(scalar.activation(xsq[slot][:], x2t[slot][:], ACTF.Square,
                                         accum_out=s2t[:, t:t + 1]))
                sq_ms.append(ss.n)
            block.scalar(_)
            def _(vector, t=t, slot=slot):
                vector.wait_ge(lx[slot].h, lx[slot].n)
                vs.inc(vector.reduce_sum(s1t[:, t:t + 1], x2t[slot][:], axis=AX.X))
                red_ms.append(vs.n)
            block.vector(_)

        def emit_ln_chain(pre_v):
            def _(vector):
                vector.wait_ge(vs.h, pre_v)
                def ch(inst):
                    vs.inc(inst); vector.wait_ge(vs.h, vs.n)
                invC = 1.0 / C
                ch(vector.tensor_scalar(s1t[:], s1t[:], invC, None, ALU.mult))
                ch(vector.tensor_tensor(tmpstat[:], s1t[:], s1t[:], ALU.mult))
                ch(vector.scalar_tensor_tensor(s2t[:], s2t[:], invC, tmpstat[:],
                                               ALU.mult, ALU.subtract))
                ch(vector.tensor_scalar(s2t[:], s2t[:], 1.0e-5, None, ALU.add))
            block.vector(_)
            mid_v = vs.n
            def _(scalar):
                scalar.wait_ge(vs.h, mid_v)
                ss.inc(scalar.activation(rstdt[:], s2t[:], ACTF.Sqrt))
            block.scalar(_)
            mid_s = ss.n
            def _(vector):
                vector.wait_ge(ss.h, mid_s)
                def ch(inst):
                    vs.inc(inst); vector.wait_ge(vs.h, vs.n)
                ch(vector.reciprocal(rstdt[:], rstdt[:]))
                ch(vector.tensor_tensor(tmpstat[:], s2t[:], rstdt[:], ALU.mult))
                ch(vector.tensor_tensor(tmpstat[:], tmpstat[:], rstdt[:], ALU.mult))
                ch(vector.tensor_scalar(tmpstat[:], tmpstat[:], -0.5, 1.5, ALU.mult, ALU.add))
                ch(vector.tensor_tensor(rstdt[:], rstdt[:], tmpstat[:], ALU.mult))
                ch(vector.tensor_tensor(nmbt[:], s1t[:], rstdt[:], ALU.mult))
                ch(vector.tensor_scalar(nmbt[:], nmbt[:], -1.0, None, ALU.mult))
            block.vector(_)
            return vs.n

        def _(vector):
            vector.wait_ge(ss.h, sq_ms[-1])
        block.vector(_)
        ln1_v = emit_ln_chain(red_ms[-1])

        norm_ms = []
        xl_tr = []
        for t in range(NT):
            slot = t % 2
            def _(sync, t=t, slot=slot):
                if t >= 2:
                    sync.wait_ge(ss.h, norm_ms[t - 2])
                else:
                    sync.wait_ge(vs.h, red_ms[NT - 2 + t])
                    sync.wait_ge(ss.h, sq_ms[NT - 2 + t])
                lx[slot].inc(sync.dma_start(out=x2t[slot][:],
                                            in_=x_in[t * 128:(t + 1) * 128, :]), 16)
            block.sync(_)
            def _(scalar, t=t, slot=slot):
                scalar.wait_ge(lx[slot].h, lx[slot].n)
                scalar.wait_ge(vs.h, ln1_v)
                if t >= 2:
                    scalar.wait_ge(ts.h, xl_tr[t - 2])
                ss.inc(scalar.activation(xlt[slot][:], x2t[slot][:], ACTF.Identity,
                                         bias=nmbt[:, t:t + 1], scale=rstdt[:, t:t + 1]))
                norm_ms.append(ss.n)
            block.scalar(_)
            def _(tensor, t=t, slot=slot):
                tensor.wait_ge(ss.h, norm_ms[t])
                tensor.wait_ge(ss.h, ptg[0])
                tensor.wait_ge(vs.h, ptgv[0])
                for ci in range(NCH):
                    inst = tensor.transpose(pT[:, ci * 128:(ci + 1) * 128],
                                            xlt[slot][:, ci * 128:(ci + 1) * 128], iden_b[:])
                ts.inc(inst)
                xl_tr.append(ts.n)
            block.tensor(_)
            trm = ts.n
            def _(scalar, t=t):
                scalar.wait_ge(ts.h, trm)
                for ci in range(NCH):
                    inst = scalar.copy(xlnT[:, ci * T + t * 128: ci * T + t * 128 + 128],
                                       pT[:, ci * 128:(ci + 1) * 128])
                ss.inc(inst)
                ptg[0] = ss.n
            block.scalar(_)
        P1_SS = ss.n

        lat_ms = []
        for f in range(NQ):
            bank = pA if f % 2 == 0 else pB
            def _(tensor, f=f, bank=bank):
                if f < 2:
                    tensor.wait_ge(ss.h, P1_SS)
                    tensor.wait_ge(vs.h, bo_v)
                else:
                    tensor.wait_ge(ss.h, lat_ms[f - 2])
                for ci in range(NCH):
                    inst = tensor.matmul(bank[0:R, :],
                                         WbT[:, ci * R:(ci + 1) * R],
                                         xlnT[:, ci * T + f * 512: ci * T + (f + 1) * 512],
                                         start=(ci == 0), stop=(ci == NCH - 1))
                ts.inc(inst)
            block.tensor(_)
            mm = ts.n
            def _(scalar, f=f, bank=bank):
                scalar.wait_ge(ts.h, mm)
                ss.inc(scalar.copy(LT[:, f * 512:(f + 1) * 512], bank[0:R, :]))
                lat_ms.append(ss.n)
            block.scalar(_)

        v_sb = sb("v_sb", [128, NT * C], BF16, stk2a)
        v_ms = []
        for t in range(NT):
            banks = (pC, pD) if t % 2 == 0 else (pE, pF)
            for h in range(2):
                bank = banks[h]
                def _(tensor, t=t, h=h, bank=bank):
                    if t >= 2:
                        tensor.wait_ge(ss.h, v_ms[(t - 2) * 2 + h])
                    for ci in range(NCH):
                        tensor.matmul(bank[:],
                                      xlnT[:, ci * T + t * 128: ci * T + t * 128 + 128],
                                      WvT[:, ci * C + h * 512: ci * C + (h + 1) * 512],
                                      start=(ci == 0), stop=False)
                    inst = tensor.matmul(bank[:], ones_b[:], bv_b[:, h * 512:(h + 1) * 512],
                                         start=False, stop=True)
                    ts.inc(inst)
                block.tensor(_)
                mm = ts.n
                def _(scalar, t=t, h=h, bank=bank):
                    scalar.wait_ge(ts.h, mm)
                    ss.inc(scalar.copy(v_sb[:, t * C + h * 512: t * C + (h + 1) * 512], bank[:]))
                    v_ms.append(ss.n)
                block.scalar(_)
        P2_SS = ss.n
        stk1.close()

        pTs = sb("pTs", [128, NT * 512], BF16, stk2b)
        p_row = sb("p_row", [128, T], BF16, stk2b)
        yT = sb("yT", [128, NCH * 512], BF16, stk2b)
        maskneg = sb("maskneg", [128, 128], F32, stk2b)
        smsk = sb("smsk", [128, 128], F32, stk2b)
        ymm_last = [0]
        ptz = [0]
        def _(sync):
            sync.wait_ge(ss.h, P2_SS)
            lc.inc(sync.dma_start(out=maskneg[:], in_=maskneg_in), 16)
        block.sync(_)
        MSK_LC = lc.n
        last_exp_ss = [P2_SS]
        last_yev = [P2_SS]
        epi_v = []
        epi_sq = []
        x2st_ms = [0, 0]

        def emit_pass1(qc, qt):
            qp = (qc % 2) * 4 + qt
            g = qc * 4 + qt
            k_hi = (g + 1) * 128
            nkc = (k_hi + 511) // 512
            sbank = (pA, pB)
            red1 = []
            for kc in range(nkc):
                N = min(512, k_hi - kc * 512)
                def _(tensor, kc=kc, N=N, g=g):
                    if kc >= 2:
                        tensor.wait_ge(vs.h, red1[kc - 2])
                    else:
                        tensor.wait_ge(vs.h, p1_mv[-1])
                    ts.inc(tensor.matmul(sbank[kc % 2][:, 0:N],
                                         LT[:, g * 128:(g + 1) * 128],
                                         LT[:, kc * 512: kc * 512 + N],
                                         start=True, stop=True))
                block.tensor(_)
                mm = ts.n
                def _(vector, kc=kc, N=N, nkc=nkc):
                    vector.wait_ge(ts.h, mm)
                    if kc == nkc - 1:
                        vector.wait_ge(lc.h, MSK_LC)
                        off = N - 128
                        vs.inc(vector.tensor_tensor(smsk[:], sbank[kc % 2][:, off:off + 128],
                                                    maskneg[:], ALU.add))
                        vector.wait_ge(vs.h, vs.n)
                        if off > 0:
                            vector.reduce_max(mparts[:, kc:kc + 1],
                                              sbank[kc % 2][:, 0:off], axis=AX.X)
                        else:
                            vector.memset(mparts[:, kc:kc + 1], NEGBIG)
                        vs.inc(vector.reduce_max(mparts[:, kc + 1:kc + 2], smsk[:], axis=AX.X))
                    else:
                        vs.inc(vector.reduce_max(mparts[:, kc:kc + 1],
                                                 sbank[kc % 2][:, 0:N], axis=AX.X))
                    red1.append(vs.n)
                block.vector(_)
            def _(vector, qt=qt, nkc=nkc, qp=qp):
                vector.wait_ge(vs.h, red1[-1])
                vs.inc(vector.reduce_max(tmpc[:, 2:3], mparts[:, 0:nkc + 1], axis=AX.X))
                vector.wait_ge(vs.h, vs.n)
                vs.inc(vector.tensor_scalar(nbias[:, qp:qp + 1], tmpc[:, 2:3],
                                            -0.125, None, ALU.mult))
            block.vector(_)
            p1_mv.append(vs.n)
            return vs.n

        def emit_pass2(qc, qt, m_v):
            qp = (qc % 2) * 4 + qt
            g = qc * 4 + qt
            k_hi = (g + 1) * 128
            nkc = (k_hi + 511) // 512
            sbank = (pD, pE)
            exp_ms = []
            for kc in range(nkc):
                N = min(512, k_hi - kc * 512)
                def _(tensor, kc=kc, N=N, g=g, m_v=m_v):
                    if kc >= 2:
                        tensor.wait_ge(ss.h, exp_ms[kc - 2])
                    else:
                        tensor.wait_ge(vs.h, m_v)
                        tensor.wait_ge(ss.h, last_exp_ss[0])
                        tensor.wait_ge(ss.h, last_yev[0])
                    ts.inc(tensor.matmul(sbank[kc % 2][:, 0:N],
                                         LT[:, g * 128:(g + 1) * 128],
                                         LT[:, kc * 512: kc * 512 + N],
                                         start=True, stop=True))
                block.tensor(_)
                mm2 = ts.n
                last = (kc == nkc - 1)
                if last:
                    off = N - 128
                    def _(vector, kc=kc, N=N, off=off, m_v=m_v):
                        vector.wait_ge(ts.h, mm2)
                        vector.wait_ge(vs.h, m_v)
                        if off == 0:
                            vector.memset(zparts[:, kc:kc + 1], 0.0)
                        vs.inc(vector.tensor_tensor(smsk2[:], sbank[kc % 2][:, off:off + 128],
                                                    maskneg[:], ALU.add))
                    block.vector(_)
                    mk_v = vs.n
                    def _(scalar, kc=kc, N=N, off=off, qp=qp):
                        scalar.wait_ge(vs.h, mk_v)
                        if off > 0:
                            scalar.activation(p_row[:, kc * 512: kc * 512 + off],
                                              sbank[kc % 2][:, 0:off], ACTF.Exp,
                                              bias=nbias[:, qp:qp + 1], scale=0.125,
                                              accum_out=zparts[:, kc:kc + 1])
                        ss.inc(scalar.activation(p_row[:, kc * 512 + off: kc * 512 + N],
                                                 smsk2[:], ACTF.Exp,
                                                 bias=nbias[:, qp:qp + 1], scale=0.125,
                                                 accum_out=zparts[:, kc + 1:kc + 2]))
                        exp_ms.append(ss.n)
                    block.scalar(_)
                else:
                    def _(scalar, kc=kc, N=N, qp=qp, m_v=m_v):
                        scalar.wait_ge(ts.h, mm2)
                        scalar.wait_ge(vs.h, m_v)
                        ss.inc(scalar.activation(p_row[:, kc * 512: kc * 512 + N],
                                                 sbank[kc % 2][:, 0:N], ACTF.Exp,
                                                 bias=nbias[:, qp:qp + 1], scale=0.125,
                                                 accum_out=zparts[:, kc:kc + 1]))
                        exp_ms.append(ss.n)
                    block.scalar(_)
            last_exp_ss[0] = exp_ms[-1]
            def _(vector, qp=qp, nkc=nkc):
                vector.wait_ge(ss.h, exp_ms[-1])
                vs.inc(vector.reduce_sum(zcol[:, qp:qp + 1], zparts[:, 0:nkc + 1], axis=AX.X))
                vector.wait_ge(vs.h, vs.n)
                vs.inc(vector.reciprocal(rzcol[:, qp:qp + 1], zcol[:, qp:qp + 1]))
            block.vector(_)
            return exp_ms[-1]

        def emit_transposes(qc, qt, exp_last):
            g = qc * 4 + qt
            k_hi = (g + 1) * 128
            nblk = k_hi // 128
            for b0 in range(0, nblk, 8):
                bn = min(8, nblk - b0)
                def _(tensor, b0=b0, bn=bn, exp_last=exp_last):
                    tensor.wait_ge(ss.h, exp_last)
                    tensor.wait_ge(ss.h, ptg[0])
                    tensor.wait_ge(vs.h, ptgv[0])
                    for j in range(bn):
                        inst = tensor.transpose(pT[:, j * 128:(j + 1) * 128],
                                                p_row[:, (b0 + j) * 128:(b0 + j + 1) * 128],
                                                iden_b[:])
                    ts.inc(inst)
                block.tensor(_)
                trm = ts.n
                def _(scalar, b0=b0, bn=bn, qt=qt):
                    scalar.wait_ge(ts.h, trm)
                    for j in range(bn):
                        kb = b0 + j
                        inst = scalar.copy(
                            pTs[:, kb * 512 + qt * 128: kb * 512 + qt * 128 + 128],
                            pT[:, j * 128:(j + 1) * 128])
                    ss.inc(inst)
                    ptg[0] = ss.n
                block.scalar(_)

        p1_mv = [0]
        smsk2 = sb("smsk2", [128, 128], F32, stk2b)

        def emit_sm_memset(qc):
            def _(vector, qc=qc):
                vector.wait_ge(ts.h, ymm_last[0])
                if qc == 0:
                    vector.wait_ge(ss.h, P2_SS)
                inst = None
                for j in range(1, 4):
                    kb = qc * 4 + j
                    inst = vector.memset(pTs[:, kb * 512: kb * 512 + j * 128], 0.0)
                if inst is not None:
                    vs.inc(inst)
                    ptz[0] = vs.n
            block.vector(_)

        def emit_softmax_chunk(qc):
            emit_sm_memset(qc)
            mv_next = emit_pass1(qc, 0)
            for qt in range(4):
                e_last = emit_pass2(qc, qt, mv_next)
                if qt < 3:
                    mv_next = emit_pass1(qc, qt + 1)
                emit_transposes(qc, qt, e_last)

        def emit_yacc_group(qc, d, yev_ms, ptr_ss):
            nkb = (qc + 1) * 4
            bank = pC if d % 2 == 0 else pF
            def _(tensor, d=d, bank=bank, nkb=nkb, ptr_ss=ptr_ss):
                tensor.wait_ge(ss.h, ptr_ss)
                tensor.wait_ge(vs.h, ptz[0])
                if d >= 2:
                    tensor.wait_ge(ss.h, yev_ms[d - 2])
                elif epi_v:
                    tensor.wait_ge(vs.h, epi_v[-1])
                for kb in range(nkb):
                    inst = tensor.matmul(bank[:],
                                         v_sb[:, kb * C + d * 128: kb * C + d * 128 + 128],
                                         pTs[:, kb * 512:(kb + 1) * 512],
                                         start=(kb == 0), stop=(kb == nkb - 1))
                ts.inc(inst)
                if d == NCH - 1:
                    ymm_last[0] = ts.n
            block.tensor(_)
            ym = ts.n
            def _(scalar, d=d, bank=bank):
                scalar.wait_ge(ts.h, ym)
                ss.inc(scalar.copy(yT[:, d * 512:(d + 1) * 512], bank[:]))
                yev_ms.append(ss.n)
            block.scalar(_)
            if d == NCH - 1:
                last_yev[0] = yev_ms[-1]

        def emit_oproj_qt(qc, qt, yev_ms):
            if True:
                g = qc * 4 + qt
                slot = g % 2
                def _(sync, g=g, slot=slot):
                    if len(epi_sq) >= 2:
                        sync.wait_ge(ss.h, epi_sq[-2])
                        sync.wait_ge(sp[slot].h, x2st_ms[slot])
                    else:
                        sync.wait_ge(ss.h, norm_ms[NT - 2 + g])
                    lx[slot].inc(sync.dma_start(out=x2t[slot][:],
                                                in_=x_in[g * 128:(g + 1) * 128, :]), 16)
                block.sync(_)
                obanks = (pC, pF)
                def _(tensor, qt=qt):
                    tensor.wait_ge(ss.h, yev_ms[-1])
                    if epi_v:
                        tensor.wait_ge(vs.h, epi_v[-1])
                    for h in range(2):
                        for d in range(NCH):
                            inst = tensor.matmul(
                                obanks[h][:],
                                yT[:, d * 512 + qt * 128: d * 512 + qt * 128 + 128],
                                WoT[:, d * C + h * 512: d * C + (h + 1) * 512],
                                start=(d == 0), stop=(d == NCH - 1))
                    ts.inc(inst)
                block.tensor(_)
                om = ts.n
                def _(vector, qt=qt, g=g, slot=slot, qc=qc):
                    qp = (qc % 2) * 4 + qt
                    vector.wait_ge(ts.h, om)
                    vector.wait_ge(lx[slot].h, lx[slot].n)
                    def ch(inst):
                        vs.inc(inst); vector.wait_ge(vs.h, vs.n)
                    ch(vector.tensor_tensor(x2t[slot][:], x2t[slot][:], bo_bc[:], ALU.add))
                    ch(vector.scalar_tensor_tensor(x2t[slot][:, 0:512], obanks[0][:],
                                                   rzcol[:, qp:qp + 1], x2t[slot][:, 0:512],
                                                   ALU.mult, ALU.add))
                    ch(vector.scalar_tensor_tensor(x2t[slot][:, 512:1024], obanks[1][:],
                                                   rzcol[:, qp:qp + 1], x2t[slot][:, 512:1024],
                                                   ALU.mult, ALU.add))
                    vs.inc(vector.reduce_sum(s1t[:, g:g + 1], x2t[slot][:], axis=AX.X))
                    epi_v.append(vs.n)
                block.vector(_)
                def _(scalar, g=g, slot=slot):
                    scalar.wait_ge(vs.h, epi_v[-1])
                    if len(epi_sq) >= 2:
                        scalar.wait_ge(ss.h, epi_sq[-2])
                    elif sq_ms:
                        scalar.wait_ge(ss.h, sq_ms[-2 + len(epi_sq)])
                    ss.inc(scalar.activation(xsq[slot][:], x2t[slot][:], ACTF.Square,
                                             accum_out=s2t[:, g:g + 1]))
                    epi_sq.append(ss.n)
                block.scalar(_)
                def _(vector, g=g, slot=slot):
                    def ch(inst):
                        vs.inc(inst); vector.wait_ge(vs.h, vs.n)
                    vector.wait_ge(vs.h, vs.n)
                    ch(vector.tensor_scalar(tmpc[:, 3:4], s1t[:, g:g + 1], 1.0 / C,
                                            None, ALU.mult))
                    ch(vector.tensor_scalar(tmpc[:, 3:4], tmpc[:, 3:4], -1.0, None, ALU.mult))
                    vector.wait_ge(ss.h, epi_sq[-1])
                    ch(vector.tensor_scalar(xsq[slot][:], x2t[slot][:], tmpc[:, 3:4],
                                            None, ALU.add))
                    vs.inc(vector.reduce_max(statx[:, g:g + 1], xsq[slot][:], axis=AX.X,
                                             apply_absolute_value=True))
                    epi_v.append(vs.n)
                block.vector(_)
                def _(gpsimd, g=g, slot=slot):
                    gpsimd.wait_ge(vs.h, epi_v[-1])
                    gpsimd.wait_ge(ss.h, epi_sq[-1])
                    sp[slot].inc(gpsimd.dma_start(out=x2_hbm[g * 128:(g + 1) * 128, :],
                                                  in_=x2t[slot][:]), 16)
                    x2st_ms[slot] = sp[slot].n
                block.gpsimd(_)

        stk4w = contextlib.ExitStack()
        wrd = [sb("wrd0", [128, I], F32, stk4w)] * 2
        wqd = sb("wqd", [128, I], BF16, stk4w)
        blkd = sb("blkd", [128, 96], F32, stk4w)
        bind = sb("bind", [128, 96], F32, stk4w)
        wstg = sb("wstg", [128, I], BF16, stk4w)
        P3_TS0 = ts.n
        wd_cast, wd_tr = [], []
        def emit_wd_head(rt):
            slot = rt % 2
            def _(sync, rt=rt, slot=slot):
                if wd_cast:
                    sync.wait_ge(vs.h, wd_cast[-1])
                else:
                    sync.wait_ge(ts.h, P3_TS0)
                lw[slot].inc(sync.dma_start(out=wrd[slot][:],
                                            in_=Wd[rt * 128:(rt + 1) * 128, :]), 16)
            block.sync(_)
            def _(vector, rt=rt, slot=slot):
                vector.wait_ge(lw[slot].h, lw[slot].n)
                if wd_tr:
                    vector.wait_ge(ts.h, wd_tr[-1])
                def ch(inst):
                    vs.inc(inst); vector.wait_ge(vs.h, vs.n)
                w3 = wrd[slot][:].rearrange("p (b k) -> p b k", k=32)
                ch(vector.reduce_max(blkd[:], w3, axis=AX.X, apply_absolute_value=True))
                ch(vector.tensor_scalar(blkd[:], blkd[:], INV_WQMAX, 1.0e-12,
                                        ALU.mult, ALU.max))
                ch(vector.reciprocal(bind[:], blkd[:]))
                bmr = bind[:].unsqueeze(2).broadcast_to([128, 96, 32])
                ch(vector.tensor_tensor(w3, w3, bmr, ALU.mult))
                ch(rnd(vector, wrd[slot][:], wrd[slot][:]))
                bms = blkd[:].unsqueeze(2).broadcast_to([128, 96, 32])
                ch(vector.tensor_tensor(wqd[:].rearrange("p (b k) -> p b k", k=32),
                                        w3, bms, ALU.mult))
                wd_cast.append(vs.n)
            block.vector(_)
        def emit_wd_tr(rt):
            for hb in range(3):
                def _(tensor, rt=rt, hb=hb):
                    tensor.wait_ge(vs.h, wd_cast[rt])
                    tensor.wait_ge(ss.h, pt2g[0])
                    for j in range(8):
                        ii = hb * 8 + j
                        inst = tensor.transpose(pT2[:, j * 128:(j + 1) * 128],
                                                wqd[:, ii * 128:(ii + 1) * 128], iden_b[:])
                    ts.inc(inst)
                    wd_tr.append(ts.n)
                block.tensor(_)
                def _(scalar, rt=rt, hb=hb):
                    scalar.wait_ge(ts.h, wd_tr[-1])
                    if hb == 0:
                        scalar.wait_ge(wds.h, 16 * rt)
                    for j in range(8):
                        ii = hb * 8 + j
                        inst = scalar.copy(wstg[:, ii * 128:(ii + 1) * 128],
                                           pT2[:, j * 128:(j + 1) * 128])
                    ss.inc(inst)
                    pt2g[0] = ss.n
                block.scalar(_)
            def _(sync, rt=rt):
                sync.wait_ge(ss.h, pt2g[0])
                wds.inc(sync.dma_start(
                    out=WdnT_hbm.rearrange("p (ii c) -> p ii c", c=C)[:, :,
                                                                     rt * 128:rt * 128 + 128],
                    in_=wstg[:].rearrange("p (ii c) -> p ii c", c=128)), 16)
            block.sync(_)

        emit_softmax_chunk(0)
        wd_emitted = 0
        for qc in range(NQ):
            nxt = qc + 1 < NQ
            ptr_ss = ss.n
            yev = []
            if nxt:
                emit_sm_memset(qc + 1)
                mv = emit_pass1(qc + 1, 0)
            for d in range(4):
                emit_yacc_group(qc, d, yev, ptr_ss)
            if nxt:
                e_last = emit_pass2(qc + 1, 0, mv)
            for d in range(4, 8):
                emit_yacc_group(qc, d, yev, ptr_ss)
            if nxt:
                mv = emit_pass1(qc + 1, 1)
                emit_transposes(qc + 1, 0, e_last)
            for qt in range(4):
                emit_oproj_qt(qc, qt, yev)
                if nxt and qt < 3:
                    e_last = emit_pass2(qc + 1, qt + 1, mv)
                    if qt < 2:
                        mv = emit_pass1(qc + 1, qt + 2)
                    emit_transposes(qc + 1, qt + 1, e_last)
            if wd_emitted < 8:
                if wd_emitted >= 1:
                    emit_wd_tr(wd_emitted - 1)
                emit_wd_head(wd_emitted)
                wd_emitted += 1
        while wd_emitted < 8:
            if wd_emitted >= 1:
                emit_wd_tr(wd_emitted - 1)
            emit_wd_head(wd_emitted)
            wd_emitted += 1
        emit_wd_tr(7)
        stk4w.close()
        X2_SP = (sp[0].n, sp[1].n)
        stk2b.close()
        stk2a.close()

        def _(vector):
            vector.wait_ge(ss.h, epi_sq[-1])
        block.vector(_)
        ln2_v = emit_ln_chain(epi_v[-1])

        def _(vector):
            vector.wait_ge(vs.h, ln2_v)
            vs.inc(vector.tensor_tensor(statx[:], statx[:], rstdt[:], ALU.mult))
        block.vector(_)
        amax_v = vs.n
        pa_done = ss.n

        def fold_stat(stat_ap, col, pre_v, use_abs=True):
            def _(vector):
                if pre_v is not None:
                    vector.wait_ge(vs.h, pre_v)
                vs.inc(vector.memset(padt[:], NEGBIG if not use_abs else 0.0))
                vector.wait_ge(vs.h, vs.n)
                vs.inc(vector.reduce_max(padt[:, 0:1], stat_ap, axis=AX.X,
                                         apply_absolute_value=use_abs))
            block.vector(_)
            fv = vs.n
            def _(tensor):
                tensor.wait_ge(vs.h, fv)
                ts.inc(tensor.transpose(pA[:, 0:128], padt[:], iden_f[:]))
            block.tensor(_)
            ft = ts.n
            def _(vector):
                vector.wait_ge(ts.h, ft)
                vs.inc(vector.reduce_max(srow[0:1, col:col + 1], pA[0:1, 0:128], axis=AX.X))
            block.vector(_)
            return vs.n

        def barrier(tag, pre_v):
            def _(sync):
                sync.wait_ge(vs.h, pre_v)
                lc.inc(sync.dma_start(out=cc_in[tag], in_=srow[0:1, 0:8]), 16)
            block.sync(_)
            ldn = lc.n
            def _(gpsimd):
                gpsimd.wait_ge(lc.h, ldn)
                gs.inc(gpsimd.collective_compute(
                    "AllReduce", ALU.max, replica_groups=[list(range(8))],
                    ins=[cc_in[tag]], outs=[cc_out[tag]]))
            block.gpsimd(_)
            gn = gs.n
            def _(sync):
                sync.wait_ge(gs.h, gn)
                lc.inc(sync.dma_start(out=srow[0:1, 0:8], in_=cc_out[tag]), 16)
            block.sync(_)
            ldn2 = lc.n
            def _(vector):
                vector.wait_ge(lc.h, ldn2)
                vs.inc(vector.nop())
            block.vector(_)
            return vs.n

        def bcast_scols(pairs, pre_v, pre_extra=None):
            def _(tensor):
                tensor.wait_ge(vs.h, pre_v)
                if pre_extra is not None:
                    tensor.wait_ge(vs.h, pre_extra)
                ts.inc(tensor.matmul(pB[:, 0:16], ones_f[:], srow[:],
                                     start=True, stop=True))
            block.tensor(_)
            bt = ts.n
            def _(vector):
                vector.wait_ge(ts.h, bt)
                for (rc, cc_) in pairs:
                    inst = vector.tensor_copy(scols[:, cc_:cc_ + 1], pB[:, rc:rc + 1])
                vs.inc(inst)
            block.vector(_)
            return vs.n

        fx = fold_stat(statx[:, 0:NT], 0, amax_v)
        bx = barrier("X", fx)
        def _(vector):
            vector.wait_ge(vs.h, bx)
            def ch(inst):
                vs.inc(inst); vector.wait_ge(vs.h, vs.n)
            ch(vector.tensor_scalar(srow[0:1, 1:2], srow[0:1, 0:1], INV_QMAX, 1.0e-12,
                                    ALU.mult, ALU.max))
            ch(vector.reciprocal(srow[0:1, 2:3], srow[0:1, 1:2]))
        block.vector(_)
        bcx = bcast_scols([(1, SX), (2, ISX)], vs.n)
        rsdx = sb("rsdx", [128, NT], F32)
        nmbx = sb("nmbx", [128, NT], F32)
        def _(vector):
            vector.wait_ge(vs.h, bcx)
            def ch(inst):
                vs.inc(inst); vector.wait_ge(vs.h, vs.n)
            ch(vector.tensor_scalar(rsdx[:], rstdt[:], scols[:, ISX:ISX + 1], None, ALU.mult))
            ch(vector.tensor_scalar(nmbx[:], nmbt[:], scols[:, ISX:ISX + 1], None, ALU.mult))
        block.vector(_)
        bcx2 = vs.n

        xqT = sb("xqT", [128, NCH * T], BF16, stk3)
        WupT = sb("WupT", [128, NCH * I], BF16, stk3)
        WgT = sb("WgT", [128, NCH * I], BF16, stk3)
        wrs = [sb(f"wrs{i}", [128, C], F32, stk3) for i in range(2)]
        wqb = [sb(f"wqb{i}", [128, C], BF16, stk3) for i in range(2)]
        blkmx = [sb(f"blkmx{i}", [128, 32], F32, stk3) for i in range(2)]
        binx = sb("binx", [128, 32], F32, stk3)

        wp_cast, wp_tr = [], []
        wp_i = [0]
        def emit_wprep_tile(Wsrc, dstT, rt):
            slot = wp_i[0] % 2
            wp_i[0] += 1
            def _(sync, rt=rt, slot=slot):
                if len(wp_cast) >= 2:
                    sync.wait_ge(vs.h, wp_cast[-2])
                elif not wp_cast:
                    sync.wait_ge(vs.h, bcx)
                    sync.wait_ge(wds.h, 128)
                lw[slot].inc(sync.dma_start(out=wrs[slot][:],
                                            in_=Wsrc[rt * 128:(rt + 1) * 128, :]), 16)
            block.sync(_)
            def _(vector, rt=rt, slot=slot):
                vector.wait_ge(lw[slot].h, lw[slot].n)
                if len(wp_tr) >= 2:
                    vector.wait_ge(ts.h, wp_tr[-2])
                def ch(inst):
                    vs.inc(inst); vector.wait_ge(vs.h, vs.n)
                w3 = wrs[slot][:].rearrange("p (b k) -> p b k", k=32)
                ch(vector.reduce_max(blkmx[slot][:], w3, axis=AX.X, apply_absolute_value=True))
                ch(vector.tensor_scalar(blkmx[slot][:], blkmx[slot][:], INV_WQMAX, 1.0e-12,
                                        ALU.mult, ALU.max))
                ch(vector.reciprocal(binx[:], blkmx[slot][:]))
                bmr = binx[:].unsqueeze(2).broadcast_to([128, 32, 32])
                ch(vector.tensor_tensor(w3, w3, bmr, ALU.mult))
                ch(rnd(vector, wrs[slot][:], wrs[slot][:]))
                bms = blkmx[slot][:].unsqueeze(2).broadcast_to([128, 32, 32])
                ch(vector.tensor_tensor(wqb[slot][:].rearrange("p (b k) -> p b k", k=32),
                                        w3, bms, ALU.mult))
                wp_cast.append(vs.n)
            block.vector(_)
            def _(tensor, rt=rt, slot=slot):
                tensor.wait_ge(vs.h, wp_cast[-1])
                tensor.wait_ge(ss.h, pt2g[0])
                for ci in range(NCH):
                    inst = tensor.transpose(pT2[:, ci * 128:(ci + 1) * 128],
                                            wqb[slot][:, ci * 128:(ci + 1) * 128], iden_b[:])
                ts.inc(inst)
                wp_tr.append(ts.n)
            block.tensor(_)
            def _(scalar, rt=rt, first=(wp_i[0] == 1)):
                scalar.wait_ge(ts.h, wp_tr[-1])
                if first:
                    scalar.wait_ge(wds.h, 128)
                for ci in range(NCH):
                    inst = scalar.copy(dstT[:, ci * I + rt * 128: ci * I + rt * 128 + 128],
                                       pT2[:, ci * 128:(ci + 1) * 128])
                ss.inc(inst)
                pt2g[0] = ss.n
            block.scalar(_)

        qdq_ev = []
        wtiles = [(Wu, WupT, rt) for rt in range(NI)] + [(Wg, WgT, rt) for rt in range(NI)]
        wi = 0
        for t in range(NT):
            slot = t % 2
            def _(sync, t=t, slot=slot):
                if t >= 2:
                    sync.wait_ge(ss.h, qdq_ev[-1])
                    sync.wait_ge(ss.h, ptg[0])
                else:
                    sync.wait_ge(sp[slot].h, X2_SP[slot])
                lx[slot].inc(sync.dma_start(out=x2t[slot][:],
                                            in_=x2_hbm[t * 128:(t + 1) * 128, :]), 16)
            block.sync(_)
            def _(scalar, t=t, slot=slot):
                scalar.wait_ge(lx[slot].h, lx[slot].n)
                scalar.wait_ge(vs.h, bcx2)
                if len(qdq_ev) >= 2:
                    scalar.wait_ge(ss.h, qdq_ev[-2])
                ss.inc(scalar.activation(wkt[slot][:], x2t[slot][:], ACTF.Identity,
                                         bias=nmbx[:, t:t + 1], scale=rsdx[:, t:t + 1]))
            block.scalar(_)
            y_s = ss.n
            def _(vector, t=t, slot=slot):
                vector.wait_ge(ss.h, y_s)
                vs.inc(rnd(vector, wkt[slot][:], wkt[slot][:]))
            block.vector(_)
            k_v = vs.n
            def _(scalar, t=t, slot=slot):
                scalar.wait_ge(vs.h, k_v)
                ss.inc(scalar.activation(xlt[slot][:], wkt[slot][:], ACTF.Copy,
                                         scale=scols[:, SX:SX + 1]))
            block.scalar(_)
            xq_s = ss.n
            def _(tensor, t=t, slot=slot):
                tensor.wait_ge(ss.h, xq_s)
                tensor.wait_ge(ss.h, ptg[0])
                tensor.wait_ge(vs.h, ptgv[0])
                for ci in range(NCH):
                    inst = tensor.transpose(pT[:, ci * 128:(ci + 1) * 128],
                                            xlt[slot][:, ci * 128:(ci + 1) * 128], iden_b[:])
                ts.inc(inst)
            block.tensor(_)
            trm = ts.n
            def _(scalar, t=t):
                scalar.wait_ge(ts.h, trm)
                if t == 0:
                    scalar.wait_ge(wds.h, 128)
                for ci in range(NCH):
                    inst = scalar.copy(xqT[:, ci * T + t * 128: ci * T + t * 128 + 128],
                                       pT[:, ci * 128:(ci + 1) * 128])
                ss.inc(inst)
                ptg[0] = ss.n
            block.scalar(_)
            qdq_ev.append(ss.n)
            nw = 2 if (t % 2 == 0) else 1
            for _k in range(nw):
                if wi < len(wtiles):
                    Ws, dT, rt = wtiles[wi]
                    emit_wprep_tile(Ws, dT, rt)
                    wi += 1
        while wi < len(wtiles):
            Ws, dT, rt = wtiles[wi]
            emit_wprep_tile(Ws, dT, rt)
            wi += 1
        P4_done = ss.n
        P4_wv = vs.n

        ustg = [sb("ustg0", [128, T], BF16, stk3)] * 2
        f1_ev = []
        f1_st = [0, 0]
        f1_stat_v = []

        f1_lo = []
        f1_hi = []
        def emit_big_mm(dstT_hbm, WT, stat, first, stat2=None):
            H = T // 2
            for ic in range(NI):
                slot = ic % 2
                banks = (pA, pB) if ic % 2 == 0 else (pC, pD)
                ev_here = []
                for f in range(NQ):
                    bank = banks[f % 2]
                    def _(tensor, ic=ic, f=f, bank=bank):
                        if first and ic == 0 and f == 0:
                            tensor.wait_ge(ss.h, P4_done)
                            tensor.wait_ge(vs.h, bcx)
                        if f >= 2:
                            tensor.wait_ge(ss.h, ev_here[f - 2])
                        elif f1_ev:
                            tensor.wait_ge(ss.h, f1_ev[-1])
                        for ci in range(NCH):
                            inst = tensor.matmul(
                                bank[:],
                                WT[:, ci * I + ic * 128: ci * I + ic * 128 + 128],
                                xqT[:, ci * T + f * 512: ci * T + (f + 1) * 512],
                                start=(ci == 0), stop=(ci == NCH - 1))
                        ts.inc(inst)
                    block.tensor(_)
                    mm = ts.n
                    def _(scalar, ic=ic, f=f, bank=bank, slot=slot):
                        scalar.wait_ge(ts.h, mm)
                        if f == 0 and NQ >= 2 and f1_lo:
                            psl, pv, pvs = f1_lo[-1]
                            scalar.wait_ge(sq[psl].h, pv)
                            scalar.wait_ge(vs.h, pvs)
                        if f == 0 and NQ < 2 and f1_hi:
                            psl, pv, pvs = f1_hi[-1]
                            scalar.wait_ge(sp[psl].h, pv)
                            scalar.wait_ge(vs.h, pvs)
                        if NQ >= 2 and f == NQ // 2 and f1_hi:
                            psl, pv, pvs = f1_hi[-1]
                            scalar.wait_ge(sp[psl].h, pv)
                            scalar.wait_ge(vs.h, pvs)
                        ss.inc(scalar.copy(ustg[slot][:, f * 512:(f + 1) * 512], bank[:]))
                        ev_here.append(ss.n)
                    block.scalar(_)
                    if NQ >= 2 and f == NQ // 2 - 1:
                        def _(vector, ic=ic, slot=slot):
                            vector.wait_ge(ss.h, ev_here[-1])
                            if stat2 is not None:
                                vector.reduce_max(stat2[:, 2 * ic:2 * ic + 1],
                                                  ustg[slot][:, 0:H], axis=AX.X)
                            vs.inc(vector.reduce_max(stat[:, 2 * ic:2 * ic + 1],
                                                     ustg[slot][:, 0:H], axis=AX.X,
                                                     apply_absolute_value=True))
                        block.vector(_)
                        vlo = vs.n
                        def _(gpsimd, ic=ic, slot=slot, evv=ev_here[-1]):
                            gpsimd.wait_ge(ss.h, evv)
                            sq[slot].inc(gpsimd.dma_start(
                                out=dstT_hbm[ic * 128:(ic + 1) * 128, 0:H],
                                in_=ustg[slot][:, 0:H]), 16)
                        block.gpsimd(_)
                        f1_lo.append((slot, sq[slot].n, vlo))
                f1_ev.append(ev_here[-1])
                lo = 0 if NQ >= 2 else None
                def _(vector, ic=ic, slot=slot, lo=lo):
                    vector.wait_ge(ss.h, ev_here[-1])
                    if lo is not None:
                        src = ustg[slot][:, H:T]
                        c0 = 2 * ic + 1
                    else:
                        src = ustg[slot][:]
                        c0 = 2 * ic
                    if stat2 is not None:
                        vector.reduce_max(stat2[:, c0:c0 + 1], src, axis=AX.X)
                        if lo is None:
                            vector.reduce_max(stat2[:, c0 + 1:c0 + 2], src, axis=AX.X)
                    if lo is None:
                        vector.reduce_max(stat[:, c0 + 1:c0 + 2], src, axis=AX.X,
                                          apply_absolute_value=True)
                    vs.inc(vector.reduce_max(stat[:, c0:c0 + 1], src, axis=AX.X,
                                             apply_absolute_value=True))
                    f1_stat_v.append(vs.n)
                block.vector(_)
                def _(gpsimd, ic=ic, slot=slot, lo=lo):
                    gpsimd.wait_ge(ss.h, ev_here[-1])
                    if lo is not None:
                        sp[slot].inc(gpsimd.dma_start(
                            out=dstT_hbm[ic * 128:(ic + 1) * 128, H:T],
                            in_=ustg[slot][:, H:T]), 16)
                    else:
                        sp[slot].inc(gpsimd.dma_start(
                            out=dstT_hbm[ic * 128:(ic + 1) * 128, :],
                            in_=ustg[slot][:]), 16)
                    f1_st[slot] = sp[slot].n
                block.gpsimd(_)
                f1_hi.append((slot, sp[slot].n, f1_stat_v[-1]))

        emit_big_mm(U_hbm, WupT, ustat, True)
        emit_big_mm(G_hbm, WgT, gstat, False, stat2=gstat2)
        F1_V = vs.n
        F1_SP = (sp[0].n, sp[1].n)
        F1_SQ = (sq[0].n, sq[1].n)

        fa1 = fold_stat(ustat[:, 0:2 * NI], 0, F1_V)
        fa2 = fold_stat(gstat[:, 0:2 * NI], 1, None)
        fa3 = fold_stat(gstat2[:, 0:2 * NI], 2, None, use_abs=False)
        ba = barrier("A", fa3)
        def _(vector):
            vector.wait_ge(vs.h, ba)
            def ch(inst):
                vs.inc(inst); vector.wait_ge(vs.h, vs.n)
            ch(vector.tensor_scalar(srow[0:1, 8:9], srow[0:1, 0:1], INV_QMAX, 1.0e-12,
                                    ALU.mult, ALU.max))
            ch(vector.reciprocal(srow[0:1, 9:10], srow[0:1, 8:9]))
            ch(vector.tensor_scalar(srow[0:1, 10:11], srow[0:1, 1:2], INV_QMAX, 1.0e-12,
                                    ALU.mult, ALU.max))
            ch(vector.reciprocal(srow[0:1, 11:12], srow[0:1, 10:11]))
            ch(vector.tensor_scalar(srow[0:1, 12:13], srow[0:1, 10:11], 1.0 / 65536.0,
                                    None, ALU.mult))
            ch(vector.tensor_tensor(srow[0:1, 3:4], srow[0:1, 2:3], srow[0:1, 11:12],
                                    ALU.mult))
            ch(vector.tensor_scalar(srow[0:1, 3:4], srow[0:1, 3:4], mcol[0:1, :],
                                    nmcol[0:1, :], ALU.add, ALU.add))
            ch(vector.tensor_tensor(srow[0:1, 3:4], srow[0:1, 3:4], srow[0:1, 10:11],
                                    ALU.mult))
        block.vector(_)
        smA = vs.n
        def _(scalar):
            scalar.wait_ge(vs.h, smA)
            ss.inc(scalar.activation(srow[0:1, 6:7], srow[0:1, 3:4], ACTF.Sigmoid))
        block.scalar(_)
        smAs = ss.n
        def _(vector):
            vector.wait_ge(ss.h, smAs)
            def ch(inst):
                vs.inc(inst); vector.wait_ge(vs.h, vs.n)
            ch(vector.tensor_scalar(srow[0:1, 6:7], srow[0:1, 6:7], 65536.0, U16_CAP,
                                    ALU.mult, ALU.min))
            ch(vector.tensor_scalar(srow[0:1, 6:7], srow[0:1, 6:7], mcol[0:1, :],
                                    nmcol[0:1, :], ALU.add, ALU.add))
            ch(vector.tensor_tensor(srow[0:1, 6:7], srow[0:1, 6:7], srow[0:1, 3:4],
                                    ALU.mult))
            ch(vector.tensor_scalar(srow[0:1, 6:7], srow[0:1, 6:7], 1.0 / 65536.0,
                                    None, ALU.mult))
            ch(vector.tensor_scalar(srow[0:1, 4:5], srow[0:1, 6:7], INV_QMAX, 1.0e-12,
                                    ALU.mult, ALU.max))
            ch(vector.reciprocal(srow[0:1, 5:6], srow[0:1, 4:5]))
            ch(vector.tensor_tensor(srow[0:1, 7:8], srow[0:1, 12:13], srow[0:1, 5:6],
                                    ALU.mult))
        block.vector(_)
        bca = bcast_scols([(8, SU), (9, ISU), (10, SG), (11, ISG), (12, SGQ),
                           (7, ISB)], vs.n)

        stk3.close()

        WdnT = sb("WdnT", [128, NI * C], BF16, stk4o)
        def _(sync):
            sync.wait_ge(sp[0].h, F1_SP[0])
            sync.wait_ge(sp[1].h, F1_SP[1])
            sync.wait_ge(sq[0].h, F1_SQ[0])
            sync.wait_ge(sq[1].h, F1_SQ[1])
            sync.wait_ge(wds.h, 16 * 8)
            for q in range(4):
                wl.inc(sync.dma_start(
                    out=WdnT[:, q * (NI * C // 4):(q + 1) * (NI * C // 4)],
                    in_=WdnT_hbm[:, q * (NI * C // 4):(q + 1) * (NI * C // 4)]), 16)
        block.sync(_)
        gbuf = [sb("gbuf0", [128, T], BF16, stk4a)] * 2
        ubuf = [sb("ubuf0", [128, T], BF16, stk4a)] * 2
        A1 = [sb(f"A1{i}", [128, T], F32, stk4a) for i in range(3)]
        A2 = [sb(f"A2{i}", [128, T], F32, stk4a) for i in range(2)]
        A3 = [sb(f"A3{i}", [128, T], F32, stk4a) for i in range(2)]
        f23_v, f23_pp, f23_yg, f23_sg, f23_yu = [], [], [], [], []
        qq_v = {}
        spq = {}
        def emit_t1(ic):
            slot = ic % 2
            s3 = ic % 3
            def _(vector, ic=ic, slot=slot, s3=s3):
                vector.wait_ge(ss.h, f23_sg[ic])
                def ch(inst):
                    vs.inc(inst); vector.wait_ge(vs.h, vs.n)
                ch(vector.tensor_scalar(A1[s3][:], A1[s3][:], 65536.0, U16_CAP,
                                        ALU.mult, ALU.min))
                ch(rnd(vector, A1[s3][:], A1[s3][:]))
                if ic >= 2:
                    vector.wait_ge(sp[slot].h, spq[ic - 2])
                ch(vector.tensor_tensor(A3[slot][:], A2[slot][:], A1[s3][:], ALU.mult))
                f23_pp.append(vs.n)
            block.vector(_)
        def emit_ta(ic):
            slot = ic % 2
            s3 = ic % 3
            def _(scalar, ic=ic, slot=slot, s3=s3):
                scalar.wait_ge(vs.h, f23_pp[ic])
                scalar.wait_ge(lu.h, 16 * (ic + 1))
                scalar.activation(A2[slot][:], ubuf[slot][:], ACTF.Copy,
                                  scale=scols[:, ISU:ISU + 1])
                ss.inc(scalar.activation(A1[s3][:], A3[slot][:], ACTF.Copy,
                                         scale=scols[:, ISB:ISB + 1]))
                f23_yu.append(ss.n)
            block.scalar(_)
        def emit_t2(ic):
            slot = ic % 2
            s3 = ic % 3
            def _(vector, ic=ic, slot=slot, s3=s3):
                vector.wait_ge(ss.h, f23_yu[ic])
                def ch(inst):
                    vs.inc(inst); vector.wait_ge(vs.h, vs.n)
                ch(rnd(vector, A2[slot][:], A2[slot][:]))
                ch(rnd(vector, A1[s3][:], A1[s3][:]))
                ch(vector.tensor_tensor(A3[slot][:], A1[s3][:], A2[slot][:], ALU.mult))
                qq_v[ic] = vs.n
                vs.inc(vector.reduce_max(qstat[:, ic:ic + 1], A3[slot][:], axis=AX.X,
                                         apply_absolute_value=True))
                f23_v.append(vs.n)
            block.vector(_)
            def _(gpsimd, ic=ic, slot=slot):
                gpsimd.wait_ge(vs.h, f23_v[-1])
                sp[slot].inc(gpsimd.dma_start(out=QQ_hbm[ic * 128:(ic + 1) * 128, :],
                                              in_=A3[slot][:]), 16)
                spq[ic] = sp[slot].n
            block.gpsimd(_)

        for ic in range(NI):
            slot = ic % 2
            if ic >= 2:
                emit_t2(ic - 2)
            def _(sync, ic=ic, slot=slot):
                if ic < 2:
                    sync.wait_ge(sp[0].h, F1_SP[0])
                    sync.wait_ge(sp[1].h, F1_SP[1])
                    sync.wait_ge(sq[0].h, F1_SQ[0])
                    sync.wait_ge(sq[1].h, F1_SQ[1])
                if f23_yg:
                    sync.wait_ge(ss.h, f23_yg[-1])
                lr[slot].inc(sync.dma_start(out=gbuf[slot][:],
                                            in_=G_hbm[ic * 128:(ic + 1) * 128, :]), 16)
            block.sync(_)
            s3 = ic % 3
            def _(scalar, ic=ic, slot=slot, s3=s3):
                scalar.wait_ge(lr[slot].h, lr[slot].n)
                scalar.wait_ge(vs.h, bca)
                if ic >= 3:
                    scalar.wait_ge(vs.h, qq_v[ic - 3])
                ss.inc(scalar.activation(A1[s3][:], gbuf[slot][:], ACTF.Copy,
                                         scale=scols[:, ISG:ISG + 1]))
                f23_yg.append(ss.n)
            block.scalar(_)
            yg_s = ss.n
            def _(vector, ic=ic, slot=slot, s3=s3):
                vector.wait_ge(ss.h, yg_s)
                if ic >= 2:
                    vector.wait_ge(vs.h, qq_v[ic - 2])
                vs.inc(rnd(vector, A2[slot][:], A1[s3][:]))
            block.vector(_)
            kg_v = vs.n
            def _(scalar, ic=ic, slot=slot, s3=s3):
                scalar.wait_ge(vs.h, kg_v)
                ss.inc(scalar.activation(A1[s3][:], A2[slot][:], ACTF.Sigmoid,
                                         scale=scols[:, SG:SG + 1]))
                f23_sg.append(ss.n)
            block.scalar(_)
            if ic >= 1:
                emit_t1(ic - 1)
                emit_ta(ic - 1)
            def _(sync, ic=ic, slot=slot):
                if ic == 0:
                    sync.wait_ge(sp[0].h, F1_SP[0])
                    sync.wait_ge(sp[1].h, F1_SP[1])
                    sync.wait_ge(sq[0].h, F1_SQ[0])
                    sync.wait_ge(sq[1].h, F1_SQ[1])
                else:
                    sync.wait_ge(ss.h, f23_yu[ic - 1])
                lu.inc(sync.dma_start(out=ubuf[slot][:],
                                      in_=U_hbm[ic * 128:(ic + 1) * 128, :]), 16)
            block.sync(_)
        emit_t2(NI - 2)
        emit_t1(NI - 1)
        emit_ta(NI - 1)
        emit_t2(NI - 1)
        F3_V = vs.n
        F3_SP = (sp[0].n, sp[1].n)

        fc = fold_stat(qstat[:, 0:NI], 0, F3_V)
        bc = barrier("Cc", fc)
        def _(vector):
            vector.wait_ge(vs.h, bc)
            def ch(inst):
                vs.inc(inst); vector.wait_ge(vs.h, vs.n)
            ch(vector.tensor_tensor(srow[0:1, 6:7], srow[0:1, 4:5], srow[0:1, 8:9],
                                    ALU.mult))
            ch(vector.tensor_tensor(srow[0:1, 7:8], srow[0:1, 0:1], srow[0:1, 6:7],
                                    ALU.mult))
            ch(vector.tensor_scalar(srow[0:1, 13:14], srow[0:1, 7:8], INV_QMAX, 1.0e-12,
                                    ALU.mult, ALU.max))
            ch(vector.reciprocal(srow[0:1, 14:15], srow[0:1, 13:14]))
            ch(vector.tensor_tensor(srow[0:1, 15:16], srow[0:1, 6:7], srow[0:1, 14:15],
                                    ALU.mult))
        block.vector(_)
        bcc = bcast_scols([(15, ISCQ), (13, SC)], vs.n)

        stk4a.close()
        oT = [sb(f"oT{i}", [128, NI * TQ], BF16, stk4b) for i in range(2)]
        qqb = [sb(f"qqb{i}", [128, TQ], F32, stk4b) for i in range(4)]
        lr4 = [lr[0], lr[1], mk("lr2"), mk("lr3")]
        oki = [sb(f"oki{i}", [128, TQ], F32, stk4b) for i in range(2)]

        out_v = []
        mmq_ms = {}
        dq_ms = {}
        row_lists = {}
        bank_user = {0: None, 1: None}

        def emit_dequant_quarter(tq):
            qslot = tq % 2
            dq_prev = row_lists.get(tq - 1)
            row_ms = []
            row_lists[tq] = row_ms
            for ic in range(NI):
                slot = ic % 2
                qs = ic % 4
                def _(sync, tq=tq, ic=ic, qs=qs):
                    if tq == 0 and ic < 4:
                        sync.wait_ge(sp[0].h, F3_SP[0])
                        sync.wait_ge(sp[1].h, F3_SP[1])
                    elif ic >= 4:
                        sync.wait_ge(vs.h, row_ms[ic - 4])
                    else:
                        sync.wait_ge(vs.h, dq_prev[ic - 4])
                    lr4[qs].inc(sync.dma_start(
                        out=qqb[qs][:],
                        in_=QQ_hbm[ic * 128:(ic + 1) * 128, tq * TQ:(tq + 1) * TQ]), 16)
                block.sync(_)
                def _(scalar, tq=tq, ic=ic, slot=slot, qs=qs):
                    scalar.wait_ge(lr4[qs].h, lr4[qs].n)
                    scalar.wait_ge(vs.h, bcc)
                    if ic >= 2:
                        scalar.wait_ge(vs.h, row_ms[ic - 2])
                    elif dq_prev is not None:
                        scalar.wait_ge(vs.h, dq_prev[ic - 2])
                    ss.inc(scalar.activation(oki[slot][:], qqb[qs][:], ACTF.Copy,
                                             scale=scols[:, ISCQ:ISCQ + 1]))
                block.scalar(_)
                yq_s = ss.n
                def _(vector, tq=tq, ic=ic, qslot=qslot, slot=slot):
                    vector.wait_ge(ss.h, yq_s)
                    if tq >= 2 and ic == 0:
                        vector.wait_ge(ts.h, mmq_ms[tq - 2])
                    vs.inc(rnd(vector, oki[slot][:], oki[slot][:]))
                    vector.wait_ge(vs.h, vs.n)
                    vs.inc(vector.tensor_scalar(oT[qslot][:, ic * TQ:(ic + 1) * TQ],
                                                oki[slot][:], scols[:, SC:SC + 1], None,
                                                ALU.mult))
                    row_ms.append(vs.n)
                block.vector(_)
            dq_ms[tq] = vs.n

        def emit_mm_quarter(tq):
            qslot = tq % 2
            for tt in range(NTQ):
                g = tq * NTQ + tt
                banks = (pA, pB) if tt % 2 == 0 else (pC, pD)
                slot = g % 2
                def _(gpsimd, g=g, slot=slot):
                    if len(out_v) >= 2:
                        gpsimd.wait_ge(vs.h, out_v[-2])
                        gpsimd.wait_ge(so[slot].h, so[slot].n)
                    else:
                        gpsimd.wait_ge(vs.h, bcc)
                    lg[slot].inc(gpsimd.dma_start(out=x2t[slot][:],
                                                  in_=x2_hbm[g * 128:(g + 1) * 128, :]), 16)
                block.gpsimd(_)
                def _(tensor, tq=tq, tt=tt, banks=banks, qslot=qslot):
                    if tq == 0 and tt == 0:
                        tensor.wait_ge(wl.h, 64)
                    tensor.wait_ge(vs.h, dq_ms[tq])
                    bu = bank_user[tt % 2]
                    if bu is not None:
                        tensor.wait_ge(vs.h, out_v[bu])
                    else:
                        tensor.wait_ge(vs.h, bcc)
                    for h in range(2):
                        for ic in range(NI):
                            inst = tensor.matmul(
                                banks[h][:],
                                oT[qslot][:, ic * TQ + tt * 128: ic * TQ + tt * 128 + 128],
                                WdnT[:, ic * C + h * 512: ic * C + (h + 1) * 512],
                                start=(ic == 0), stop=(ic == NI - 1))
                    ts.inc(inst)
                block.tensor(_)
                mmf = ts.n
                if tt == NTQ - 1:
                    mmq_ms[tq] = ts.n
                def _(vector, g=g, banks=banks, slot=slot):
                    vector.wait_ge(ts.h, mmf)
                    vector.wait_ge(lg[slot].h, lg[slot].n)
                    vector.tensor_tensor(x2t[slot][:, 0:512], banks[0][:],
                                         x2t[slot][:, 0:512], ALU.add)
                    vs.inc(vector.tensor_tensor(x2t[slot][:, 512:1024], banks[1][:],
                                                x2t[slot][:, 512:1024], ALU.add))
                    out_v.append(vs.n)
                    bank_user[tt % 2] = len(out_v) - 1
                block.vector(_)
                def _(gpsimd, g=g, slot=slot):
                    gpsimd.wait_ge(vs.h, out_v[-1])
                    so[slot].inc(gpsimd.dma_start(out=out[g * 128:(g + 1) * 128, :],
                                                  in_=x2t[slot][:]), 16)
                block.gpsimd(_)

        emit_dequant_quarter(0)
        for tq in range(4):
            if tq < 3:
                emit_dequant_quarter(tq + 1)
            emit_mm_quarter(tq)

        def _(gpsimd):
            gpsimd.wait_ge(so[0].h, so[0].n)
            gpsimd.wait_ge(so[1].h, so[1].n)
        block.gpsimd(_)

    with nc.Block() as rb:
        def _replay(eng, ops):
            for attr, a, k, inst in ops:
                real = getattr(eng, attr)(*a, **k)
                for sem, v in inst.incs:
                    real.then_inc(sem, v)
        rb.sync(lambda e: _replay(e, block.eng["sync"].ops))
        rb.scalar(lambda e: _replay(e, block.eng["scalar"].ops))
        rb.vector(lambda e: _replay(e, block.eng["vector"].ops))
        rb.tensor(lambda e: _replay(e, block.eng["tensor"].ops))
        rb.gpsimd(lambda e: _replay(e, block.eng["gpsimd"].ops))
    stk4b.close()
    stk4o.close()

    return nc


_NC_CACHE = {}


def _consts():
    iden = np.eye(128, dtype=np.float32)
    tril = np.tril(np.ones((128, 128), dtype=bool))
    maskneg = np.where(tril, np.float32(0.0), np.float32(NEGBIG)).astype(np.float32)
    ones = np.ones((1, 128), dtype=np.float32)
    return iden, maskneg, ones


def make_in_maps(x, Wb, Wv, bv, Wo, bo, Wg, Wu, Wd):
    iden, maskneg, ones = _consts()
    base = {
        "Wb": np.ascontiguousarray(np.asarray(Wb, np.float32)),
        "Wv": np.ascontiguousarray(np.asarray(Wv, np.float32)),
        "bv": np.asarray(bv, np.float32).reshape(1, C),
        "Wo": np.ascontiguousarray(np.asarray(Wo, np.float32)),
        "bo": np.asarray(bo, np.float32).reshape(1, C),
        "Wg": np.ascontiguousarray(np.asarray(Wg, np.float32)),
        "Wu": np.ascontiguousarray(np.asarray(Wu, np.float32)),
        "Wd": np.ascontiguousarray(np.asarray(Wd, np.float32)),
        "iden_in": iden, "maskneg_in": maskneg, "ones_in": ones,
    }
    return [dict(base, x_in=np.ascontiguousarray(np.asarray(xb, np.float32)))
            for xb in x]


def kernel(L0_attn_hidden_states, L0_attn_W_basis, L0_attn_W_v, L0_attn_b_v,
           L0_attn_W_o, L0_attn_b_o, L0_ffn_w_gate, L0_ffn_w_up, L0_ffn_w_down):
    x = np.ascontiguousarray(np.asarray(L0_attn_hidden_states, dtype=np.float32))
    B, T, C_ = x.shape
    assert C_ == C and B == 8
    if T not in _NC_CACHE:
        _NC_CACHE[T] = build_nc(T)
    nc = _NC_CACHE[T]
    in_maps = make_in_maps(x, L0_attn_W_basis, L0_attn_W_v, L0_attn_b_v,
                           L0_attn_W_o, L0_attn_b_o, L0_ffn_w_gate, L0_ffn_w_up,
                           L0_ffn_w_down)
    res = run_bass_kernel_spmd(nc, in_maps, core_ids=list(range(B)))
    return np.stack([np.asarray(res.results[b]["out"]) for b in range(B)], axis=0)


# version 25
# speedup vs baseline: 1.0192x; 1.0192x over previous
import contextlib
import numpy as np

import concourse.bass as bass
import concourse.mybir as mybir
from concourse.bass_utils import run_bass_kernel_spmd

F32 = mybir.dt.float32
BF16 = mybir.dt.bfloat16
ALU = mybir.AluOpType
ACTF = mybir.ActivationFunctionType
AX = mybir.AxisListType

C = 1024
NCH = 8
R = 64
I = 3072
NI = 24
INV_QMAX = 1.0 / 32767.0
INV_WQMAX = 1.0 / 7.0
U16_CAP = 65535.4921875
NEGBIG = -1.0e30


class Sem:
    def __init__(self, ctx, nc, name):
        self.h = ctx.enter_context(nc.semaphore(name))
        self.n = 0

    def inc(self, inst, k=1):
        inst.then_inc(self.h, k)
        self.n += k
        return self.n


def build_nc(T=4096):
    NT = T // 128
    NQ = T // 512
    TQ = T // 4
    NTQ = TQ // 128
    nc = bass.Bass("TRN2", target_bir_lowering=False, debug=False)
    ctx = contextlib.ExitStack()

    x_in = nc.dram_tensor("x_in", [T, C], F32, kind="ExternalInput").ap()
    Wb = nc.dram_tensor("Wb", [R, C], F32, kind="ExternalInput").ap()
    Wv = nc.dram_tensor("Wv", [C, C], F32, kind="ExternalInput").ap()
    bv = nc.dram_tensor("bv", [1, C], F32, kind="ExternalInput").ap()
    Wo = nc.dram_tensor("Wo", [C, C], F32, kind="ExternalInput").ap()
    bo = nc.dram_tensor("bo", [1, C], F32, kind="ExternalInput").ap()
    Wg = nc.dram_tensor("Wg", [I, C], F32, kind="ExternalInput").ap()
    Wu = nc.dram_tensor("Wu", [I, C], F32, kind="ExternalInput").ap()
    Wd = nc.dram_tensor("Wd", [C, I], F32, kind="ExternalInput").ap()
    iden_in = nc.dram_tensor("iden_in", [128, 128], F32, kind="ExternalInput").ap()
    maskneg_in = nc.dram_tensor("maskneg_in", [128, 128], F32, kind="ExternalInput").ap()
    ones_in = nc.dram_tensor("ones_in", [1, 128], F32, kind="ExternalInput").ap()
    out = nc.dram_tensor("out", [T, C], F32, kind="ExternalOutput").ap()

    x2_hbm = nc.dram_tensor("x2_hbm", [T, C], F32, kind="Internal").ap()
    U_hbm = nc.dram_tensor("U_hbm", [I, T], BF16, kind="Internal").ap()
    G_hbm = nc.dram_tensor("G_hbm", [I, T], BF16, kind="Internal").ap()
    QQ_hbm = nc.dram_tensor("QQ_hbm", [I, T], F32, kind="Internal").ap()
    WdnT_hbm = nc.dram_tensor("WdnT_hbm", [128, (I // 128) * C], BF16, kind="Internal").ap()
    cc_in, cc_out = {}, {}
    for bn in ("X", "A", "B", "Cc"):
        cc_in[bn] = nc.dram_tensor(f"cc_in_{bn}", [1, 8], F32, kind="Internal").ap()
        cc_out[bn] = nc.dram_tensor(f"cc_out_{bn}", [1, 8], F32, kind="Internal",
                                    addr_space="Shared").ap()

    def sb(name, shape, dt, stack=ctx, side=None):
        if side is None:
            return stack.enter_context(nc.sbuf_tensor(name, shape, dt))
        return stack.enter_context(nc.sbuf_tensor(name, shape, dt, side=side))

    iden_f = sb("iden_f", [128, 128], F32)
    iden_b = sb("iden_b", [128, 128], BF16)
    ones_f = sb("ones_f", [1, 128], F32)
    ones_b = sb("ones_b", [1, 128], BF16)
    mcol = sb("mcol", [128, 1], F32)
    nmcol = sb("nmcol", [128, 1], F32)
    tmpc = sb("tmpc", [128, 4], F32)
    scols = sb("scols", [128, 16], F32)
    SX, ISX, SG, ISG, SU, ISU, SGQ, ISB, ISCQ, SC = range(10)
    srow = sb("srow", [1, 16], F32)
    s1t = sb("s1t", [128, NT], F32)
    s2t = sb("s2t", [128, NT], F32)
    rstdt = sb("rstdt", [128, NT], F32)
    nmbt = sb("nmbt", [128, NT], F32)
    tmpstat = sb("tmpstat", [128, NT], F32)
    statx = sb("statx", [128, NT], F32)
    ustat = sb("ustat", [128, 2 * NI], F32)
    gstat = sb("gstat", [128, 2 * NI], F32)
    gstat2 = sb("gstat2", [128, 2 * NI], F32)
    qstat = sb("qstat", [128, NI], F32)
    padt = sb("padt", [128, 128], F32)
    mparts = sb("mparts", [128, 16], F32)
    zparts = sb("zparts", [128, 16], F32)
    nbias = sb("nbias", [128, 8], F32)
    zcol = sb("zcol", [128, 8], F32)
    rzcol = sb("rzcol", [128, 8], F32)

    x2t = [sb(f"x2t{i}", [128, C], F32) for i in range(2)]
    xlt = [sb(f"xlt{i}", [128, C], BF16) for i in range(2)]

    psl = lambda name: ctx.enter_context(nc.psum_tensor(name, [128, 512], F32))
    pA, pB, pC, pD, pE, pF = psl("pA"), psl("pB"), psl("pC"), psl("pD"), psl("pE"), psl("pF")
    pT = ctx.enter_context(nc.psum_tensor("pT", [128, 1024], BF16))
    pT2 = ctx.enter_context(nc.psum_tensor("pT2", [128, 1024], BF16))

    stk1 = contextlib.ExitStack()
    stk2a = contextlib.ExitStack()
    stk2b = contextlib.ExitStack()
    stk3 = contextlib.ExitStack()
    stk4o = contextlib.ExitStack()
    stk4a = contextlib.ExitStack()
    stk4b = contextlib.ExitStack()

    stk0 = contextlib.ExitStack()
    xlnT = sb("xlnT", [128, NCH * T], BF16, stk1, side="right")
    WvT = sb("WvT", [128, NCH * C], BF16, stk1, side="right")
    WbT = sb("WbT", [128, NCH * R], BF16, stk1, side="right")
    wtr = [sb(f"wtr{i}", [128, C], F32, stk0, side="right") for i in range(2)]
    wtb = [sb(f"wtb{i}", [128, C], BF16, stk0, side="right") for i in range(2)]
    wbraw = sb("wbraw", [64, C], F32, stk0, side="right")
    bv_f = sb("bv_f", [1, C], F32, stk0, side="right")
    bo_f = sb("bo_f", [1, C], F32, stk0, side="right")
    WoT = sb("WoT", [128, NCH * C], BF16, stk2a)
    xsq = [sb(f"xsq{i}", [128, C], F32, stk2a) for i in range(2)]
    LT = sb("LT", [64, T], BF16, stk2a)
    bo_bc = sb("bo_bc", [128, C], F32, stk2a)
    bv_b = sb("bv_b", [1, C], BF16, stk2a)

    class _RecInst:
        __slots__ = ("incs",)
        def __init__(self):
            self.incs = []
        def then_inc(self, sem, v):
            self.incs.append((sem, v))
            return self

    class _RecEngine:
        def __init__(self):
            self.ops = []
        def __getattr__(self, attr):
            ops = self.ops
            def call(*a, **k):
                inst = _RecInst()
                ops.append((attr, a, k, inst))
                return inst
            return call

    class _RecBlock:
        def __init__(self):
            self.eng = {n: _RecEngine() for n in
                        ("sync", "scalar", "vector", "tensor", "gpsimd")}
        def __getattr__(self, name):
            e = self.eng[name]
            return lambda fn: fn(e)

    block = _RecBlock()
    if True:
        mk = lambda name: Sem(ctx, nc, name)
        lc = mk("lc")
        lx = [mk("lx0"), mk("lx1")]
        lw = [mk("lw0"), mk("lw1")]
        lr = [mk("lr0"), mk("lr1")]
        lg = [mk("lg0"), mk("lg1")]
        sp = [mk("sp0"), mk("sp1")]
        lu = [mk("lu0"), mk("lu1")]
        wds = mk("wds")
        sq = [mk("sq0"), mk("sq1")]
        wl = mk("wl")
        so = [mk("so0"), mk("so1")]
        ts = mk("ts")
        vs = mk("vs")
        ss = mk("ss")
        gs = mk("gs")
        ptg = [0]
        ptgv = [0]
        pt2g = [0]

        def rnd(vector, dst, src):
            return vector.tensor_scalar(dst, src, mcol[:], nmcol[:], ALU.add, ALU.add)

        def _(sync):
            for dst, src in ((iden_f[:], iden_in),
                             (ones_f[:], ones_in), (bv_f[:], bv), (bo_f[:], bo),
                             (wbraw[:], Wb)):
                lc.inc(sync.dma_start(out=dst, in_=src), 16)
        block.sync(_)

        def _(vector):
            vector.wait_ge(lc.h, lc.n)
            def ch(inst):
                vs.inc(inst); vector.wait_ge(vs.h, vs.n)
            ch(vector.memset(tmpc[:, 0:1], 0.75))
            ch(vector.reciprocal(tmpc[:, 0:1], tmpc[:, 0:1]))
            ch(vector.tensor_scalar(tmpc[:, 0:1], tmpc[:, 0:1], -1.0, 3.0, ALU.add, ALU.mult))
            ch(vector.tensor_scalar(tmpc[:, 0:1], tmpc[:, 0:1], -1.0, None, ALU.add))
            ch(vector.tensor_scalar(tmpc[:, 1:2], tmpc[:, 0:1], -1.0, None, ALU.mult))
            ch(vector.tensor_tensor(tmpc[:, 0:1], tmpc[:, 0:1], tmpc[:, 1:2], ALU.max))
            ch(vector.reciprocal(tmpc[:, 0:1], tmpc[:, 0:1]))
            ch(vector.tensor_scalar(mcol[:], tmpc[:, 0:1], 1.5, None, ALU.mult))
            ch(vector.tensor_scalar(nmcol[:], mcol[:], -1.0, None, ALU.mult))
            vector.memset(zparts[:], 0.0)
            vector.memset(srow[:], 0.0)
            vector.tensor_copy(iden_b[:], iden_f[:])
            vector.tensor_copy(ones_b[:], ones_f[:])
            vector.tensor_copy(wtb[0][0:64, :], wbraw[:])
            vs.inc(vector.tensor_copy(bv_b[:], bv_f[:]))
        block.vector(_)
        P0_V = vs.n

        def _(tensor):
            tensor.wait_ge(vs.h, P0_V)
            tensor.matmul(pA[:], ones_f[:], bo_f[:, 0:512], start=True, stop=True)
            ts.inc(tensor.matmul(pB[:], ones_f[:], bo_f[:, 512:1024], start=True, stop=True))
        block.tensor(_)
        bo_ts = ts.n

        def _(vector):
            vector.wait_ge(ts.h, bo_ts)
            vector.tensor_copy(bo_bc[:, 0:512], pA[:])
            vs.inc(vector.tensor_copy(bo_bc[:, 512:1024], pB[:]))
        block.vector(_)
        bo_v = vs.n

        def _(tensor):
            for ci in range(NCH):
                inst = tensor.transpose(pT2[:, ci * R:(ci + 1) * R],
                                        wtb[0][0:64, ci * 128:(ci + 1) * 128],
                                        iden_b[0:64, 0:64])
            ts.inc(inst)
        block.tensor(_)
        wb_ts = ts.n

        def _(scalar):
            scalar.wait_ge(ts.h, wb_ts)
            ss.inc(scalar.copy(WbT[:], pT2[:, 0:NCH * R]))
            pt2g[0] = ss.n
        block.scalar(_)

        attn_cast = []
        attn_tr = [ts.n, ts.n]
        def prep_attn_weight(Wsrc, dst):
            cast_ms, tr_ms = attn_cast, attn_tr
            for rt in range(NCH):
                slot = (len(cast_ms)) % 2
                def _(sync, rt=rt, slot=slot):
                    if len(cast_ms) >= 2:
                        sync.wait_ge(vs.h, cast_ms[-2])
                    lw[slot].inc(sync.dma_start(out=wtr[slot][:],
                                                in_=Wsrc[rt * 128:(rt + 1) * 128, :]), 16)
                block.sync(_)
                def _(vector, rt=rt, slot=slot):
                    vector.wait_ge(lw[slot].h, lw[slot].n)
                    vector.wait_ge(ts.h, tr_ms[-2])
                    vs.inc(vector.tensor_copy(wtb[slot][:], wtr[slot][:]))
                    cast_ms.append(vs.n)
                block.vector(_)
                def _(tensor, rt=rt, slot=slot):
                    tensor.wait_ge(vs.h, cast_ms[-1])
                    tensor.wait_ge(ss.h, pt2g[0])
                    for ci in range(NCH):
                        inst = tensor.transpose(pT2[:, ci * 128:(ci + 1) * 128],
                                                wtb[slot][:, ci * 128:(ci + 1) * 128], iden_b[:])
                    ts.inc(inst)
                    tr_ms.append(ts.n)
                block.tensor(_)
                def _(scalar, rt=rt):
                    scalar.wait_ge(ts.h, tr_ms[-1])
                    for ci in range(NCH):
                        inst = scalar.copy(dst[:, ci * C + rt * 128: ci * C + rt * 128 + 128],
                                           pT2[:, ci * 128:(ci + 1) * 128])
                    ss.inc(inst)
                    pt2g[0] = ss.n
                block.scalar(_)

        prep_attn_weight(Wv, WvT)
        prep_attn_weight(Wo, WoT)
        P0_SS_END = ss.n
        stk0.close()

        red_ms, sq_ms = [], []
        for t in range(NT):
            slot = t % 2
            def _(sync, t=t, slot=slot):
                if t >= 2:
                    sync.wait_ge(vs.h, red_ms[t - 2])
                    sync.wait_ge(ss.h, sq_ms[t - 2])
                lx[slot].inc(sync.dma_start(out=x2t[slot][:],
                                            in_=x_in[t * 128:(t + 1) * 128, :]), 16)
            block.sync(_)
            def _(scalar, t=t, slot=slot):
                scalar.wait_ge(lx[slot].h, lx[slot].n)
                if t >= 2:
                    scalar.wait_ge(ss.h, sq_ms[t - 2])
                ss.inc(scalar.activation(xsq[slot][:], x2t[slot][:], ACTF.Square,
                                         accum_out=s2t[:, t:t + 1]))
                sq_ms.append(ss.n)
            block.scalar(_)
            def _(vector, t=t, slot=slot):
                vector.wait_ge(lx[slot].h, lx[slot].n)
                vs.inc(vector.reduce_sum(s1t[:, t:t + 1], x2t[slot][:], axis=AX.X))
                red_ms.append(vs.n)
            block.vector(_)

        def emit_ln_chain(pre_v):
            def _(vector):
                vector.wait_ge(vs.h, pre_v)
                def ch(inst):
                    vs.inc(inst); vector.wait_ge(vs.h, vs.n)
                invC = 1.0 / C
                ch(vector.tensor_scalar(s1t[:], s1t[:], invC, None, ALU.mult))
                ch(vector.tensor_tensor(tmpstat[:], s1t[:], s1t[:], ALU.mult))
                ch(vector.scalar_tensor_tensor(s2t[:], s2t[:], invC, tmpstat[:],
                                               ALU.mult, ALU.subtract))
                ch(vector.tensor_scalar(s2t[:], s2t[:], 1.0e-5, None, ALU.add))
            block.vector(_)
            mid_v = vs.n
            def _(scalar):
                scalar.wait_ge(vs.h, mid_v)
                ss.inc(scalar.activation(rstdt[:], s2t[:], ACTF.Sqrt))
            block.scalar(_)
            mid_s = ss.n
            def _(vector):
                vector.wait_ge(ss.h, mid_s)
                def ch(inst):
                    vs.inc(inst); vector.wait_ge(vs.h, vs.n)
                ch(vector.reciprocal(rstdt[:], rstdt[:]))
                ch(vector.tensor_tensor(tmpstat[:], s2t[:], rstdt[:], ALU.mult))
                ch(vector.tensor_tensor(tmpstat[:], tmpstat[:], rstdt[:], ALU.mult))
                ch(vector.tensor_scalar(tmpstat[:], tmpstat[:], -0.5, 1.5, ALU.mult, ALU.add))
                ch(vector.tensor_tensor(rstdt[:], rstdt[:], tmpstat[:], ALU.mult))
                ch(vector.tensor_tensor(nmbt[:], s1t[:], rstdt[:], ALU.mult))
                ch(vector.tensor_scalar(nmbt[:], nmbt[:], -1.0, None, ALU.mult))
            block.vector(_)
            return vs.n

        def _(vector):
            vector.wait_ge(ss.h, sq_ms[-1])
        block.vector(_)
        ln1_v = emit_ln_chain(red_ms[-1])

        norm_ms = []
        xl_tr = []
        for t in range(NT):
            slot = t % 2
            def _(sync, t=t, slot=slot):
                if t >= 2:
                    sync.wait_ge(ss.h, norm_ms[t - 2])
                else:
                    sync.wait_ge(vs.h, red_ms[NT - 2 + t])
                    sync.wait_ge(ss.h, sq_ms[NT - 2 + t])
                lx[slot].inc(sync.dma_start(out=x2t[slot][:],
                                            in_=x_in[t * 128:(t + 1) * 128, :]), 16)
            block.sync(_)
            def _(scalar, t=t, slot=slot):
                scalar.wait_ge(lx[slot].h, lx[slot].n)
                scalar.wait_ge(vs.h, ln1_v)
                if t >= 2:
                    scalar.wait_ge(ts.h, xl_tr[t - 2])
                ss.inc(scalar.activation(xlt[slot][:], x2t[slot][:], ACTF.Identity,
                                         bias=nmbt[:, t:t + 1], scale=rstdt[:, t:t + 1]))
                norm_ms.append(ss.n)
            block.scalar(_)
            def _(tensor, t=t, slot=slot):
                tensor.wait_ge(ss.h, norm_ms[t])
                tensor.wait_ge(ss.h, ptg[0])
                tensor.wait_ge(vs.h, ptgv[0])
                for ci in range(NCH):
                    inst = tensor.transpose(pT[:, ci * 128:(ci + 1) * 128],
                                            xlt[slot][:, ci * 128:(ci + 1) * 128], iden_b[:])
                ts.inc(inst)
                xl_tr.append(ts.n)
            block.tensor(_)
            trm = ts.n
            def _(scalar, t=t):
                scalar.wait_ge(ts.h, trm)
                for ci in range(NCH):
                    inst = scalar.copy(xlnT[:, ci * T + t * 128: ci * T + t * 128 + 128],
                                       pT[:, ci * 128:(ci + 1) * 128])
                ss.inc(inst)
                ptg[0] = ss.n
            block.scalar(_)
        P1_SS = ss.n

        lat_ms = []
        for f in range(NQ):
            bank = pA if f % 2 == 0 else pB
            def _(tensor, f=f, bank=bank):
                if f < 2:
                    tensor.wait_ge(ss.h, P1_SS)
                    tensor.wait_ge(vs.h, bo_v)
                else:
                    tensor.wait_ge(ss.h, lat_ms[f - 2])
                for ci in range(NCH):
                    inst = tensor.matmul(bank[0:R, :],
                                         WbT[:, ci * R:(ci + 1) * R],
                                         xlnT[:, ci * T + f * 512: ci * T + (f + 1) * 512],
                                         start=(ci == 0), stop=(ci == NCH - 1))
                ts.inc(inst)
            block.tensor(_)
            mm = ts.n
            def _(scalar, f=f, bank=bank):
                scalar.wait_ge(ts.h, mm)
                ss.inc(scalar.copy(LT[:, f * 512:(f + 1) * 512], bank[0:R, :]))
                lat_ms.append(ss.n)
            block.scalar(_)

        v_sb = sb("v_sb", [128, NT * C], BF16, stk2a)
        v_ms = []
        for t in range(NT):
            banks = (pC, pD) if t % 2 == 0 else (pE, pF)
            for h in range(2):
                bank = banks[h]
                def _(tensor, t=t, h=h, bank=bank):
                    if t >= 2:
                        tensor.wait_ge(ss.h, v_ms[(t - 2) * 2 + h])
                    for ci in range(NCH):
                        tensor.matmul(bank[:],
                                      xlnT[:, ci * T + t * 128: ci * T + t * 128 + 128],
                                      WvT[:, ci * C + h * 512: ci * C + (h + 1) * 512],
                                      start=(ci == 0), stop=False)
                    inst = tensor.matmul(bank[:], ones_b[:], bv_b[:, h * 512:(h + 1) * 512],
                                         start=False, stop=True)
                    ts.inc(inst)
                block.tensor(_)
                mm = ts.n
                def _(scalar, t=t, h=h, bank=bank):
                    scalar.wait_ge(ts.h, mm)
                    ss.inc(scalar.copy(v_sb[:, t * C + h * 512: t * C + (h + 1) * 512], bank[:]))
                    v_ms.append(ss.n)
                block.scalar(_)
        P2_SS = ss.n
        stk1.close()

        pTs = sb("pTs", [128, NT * 512], BF16, stk2b)
        p_row = sb("p_row", [128, T], BF16, stk2b)
        yT = sb("yT", [128, NCH * 512], BF16, stk2b)
        maskneg = sb("maskneg", [128, 128], F32, stk2b)
        smsk = sb("smsk", [128, 128], F32, stk2b)
        ymm_last = [0]
        ptz = [0]
        def _(sync):
            sync.wait_ge(ss.h, P2_SS)
            lc.inc(sync.dma_start(out=maskneg[:], in_=maskneg_in), 16)
        block.sync(_)
        MSK_LC = lc.n
        last_exp_ss = [P2_SS]
        last_yev = [P2_SS]
        epi_v = []
        epi_sq = []
        x2st_ms = [0, 0]

        def emit_pass1(qc, qt):
            qp = (qc % 2) * 4 + qt
            g = qc * 4 + qt
            k_hi = (g + 1) * 128
            nkc = (k_hi + 511) // 512
            sbank = (pA, pB)
            red1 = []
            for kc in range(nkc):
                N = min(512, k_hi - kc * 512)
                def _(tensor, kc=kc, N=N, g=g):
                    if kc >= 2:
                        tensor.wait_ge(vs.h, red1[kc - 2])
                    else:
                        tensor.wait_ge(vs.h, p1_mv[-1])
                    ts.inc(tensor.matmul(sbank[kc % 2][:, 0:N],
                                         LT[:, g * 128:(g + 1) * 128],
                                         LT[:, kc * 512: kc * 512 + N],
                                         start=True, stop=True))
                block.tensor(_)
                mm = ts.n
                def _(vector, kc=kc, N=N, nkc=nkc):
                    vector.wait_ge(ts.h, mm)
                    if kc == nkc - 1:
                        vector.wait_ge(lc.h, MSK_LC)
                        off = N - 128
                        vs.inc(vector.tensor_tensor(smsk[:], sbank[kc % 2][:, off:off + 128],
                                                    maskneg[:], ALU.add))
                        vector.wait_ge(vs.h, vs.n)
                        if off > 0:
                            vector.reduce_max(mparts[:, kc:kc + 1],
                                              sbank[kc % 2][:, 0:off], axis=AX.X)
                        else:
                            vector.memset(mparts[:, kc:kc + 1], NEGBIG)
                        vs.inc(vector.reduce_max(mparts[:, kc + 1:kc + 2], smsk[:], axis=AX.X))
                    else:
                        vs.inc(vector.reduce_max(mparts[:, kc:kc + 1],
                                                 sbank[kc % 2][:, 0:N], axis=AX.X))
                    red1.append(vs.n)
                block.vector(_)
            def _(vector, qt=qt, nkc=nkc, qp=qp):
                vector.wait_ge(vs.h, red1[-1])
                vs.inc(vector.reduce_max(tmpc[:, 2:3], mparts[:, 0:nkc + 1], axis=AX.X))
                vector.wait_ge(vs.h, vs.n)
                vs.inc(vector.tensor_scalar(nbias[:, qp:qp + 1], tmpc[:, 2:3],
                                            -0.125, None, ALU.mult))
            block.vector(_)
            p1_mv.append(vs.n)
            return vs.n

        def emit_pass2(qc, qt, m_v):
            qp = (qc % 2) * 4 + qt
            g = qc * 4 + qt
            k_hi = (g + 1) * 128
            nkc = (k_hi + 511) // 512
            sbank = (pD, pE)
            exp_ms = []
            for kc in range(nkc):
                N = min(512, k_hi - kc * 512)
                def _(tensor, kc=kc, N=N, g=g, m_v=m_v):
                    if kc >= 2:
                        tensor.wait_ge(ss.h, exp_ms[kc - 2])
                    else:
                        tensor.wait_ge(vs.h, m_v)
                        tensor.wait_ge(ss.h, last_exp_ss[0])
                        tensor.wait_ge(ss.h, last_yev[0])
                    ts.inc(tensor.matmul(sbank[kc % 2][:, 0:N],
                                         LT[:, g * 128:(g + 1) * 128],
                                         LT[:, kc * 512: kc * 512 + N],
                                         start=True, stop=True))
                block.tensor(_)
                mm2 = ts.n
                last = (kc == nkc - 1)
                if last:
                    off = N - 128
                    def _(vector, kc=kc, N=N, off=off, m_v=m_v):
                        vector.wait_ge(ts.h, mm2)
                        vector.wait_ge(vs.h, m_v)
                        if off == 0:
                            vector.memset(zparts[:, kc:kc + 1], 0.0)
                        vs.inc(vector.tensor_tensor(smsk2[:], sbank[kc % 2][:, off:off + 128],
                                                    maskneg[:], ALU.add))
                    block.vector(_)
                    mk_v = vs.n
                    def _(scalar, kc=kc, N=N, off=off, qp=qp):
                        scalar.wait_ge(vs.h, mk_v)
                        if off > 0:
                            scalar.activation(p_row[:, kc * 512: kc * 512 + off],
                                              sbank[kc % 2][:, 0:off], ACTF.Exp,
                                              bias=nbias[:, qp:qp + 1], scale=0.125,
                                              accum_out=zparts[:, kc:kc + 1])
                        ss.inc(scalar.activation(p_row[:, kc * 512 + off: kc * 512 + N],
                                                 smsk2[:], ACTF.Exp,
                                                 bias=nbias[:, qp:qp + 1], scale=0.125,
                                                 accum_out=zparts[:, kc + 1:kc + 2]))
                        exp_ms.append(ss.n)
                    block.scalar(_)
                else:
                    def _(scalar, kc=kc, N=N, qp=qp, m_v=m_v):
                        scalar.wait_ge(ts.h, mm2)
                        scalar.wait_ge(vs.h, m_v)
                        ss.inc(scalar.activation(p_row[:, kc * 512: kc * 512 + N],
                                                 sbank[kc % 2][:, 0:N], ACTF.Exp,
                                                 bias=nbias[:, qp:qp + 1], scale=0.125,
                                                 accum_out=zparts[:, kc:kc + 1]))
                        exp_ms.append(ss.n)
                    block.scalar(_)
            last_exp_ss[0] = exp_ms[-1]
            def _(vector, qp=qp, nkc=nkc):
                vector.wait_ge(ss.h, exp_ms[-1])
                vs.inc(vector.reduce_sum(zcol[:, qp:qp + 1], zparts[:, 0:nkc + 1], axis=AX.X))
                vector.wait_ge(vs.h, vs.n)
                vs.inc(vector.reciprocal(rzcol[:, qp:qp + 1], zcol[:, qp:qp + 1]))
            block.vector(_)
            return exp_ms[-1]

        def emit_transposes(qc, qt, exp_last):
            g = qc * 4 + qt
            k_hi = (g + 1) * 128
            nblk = k_hi // 128
            for b0 in range(0, nblk, 8):
                bn = min(8, nblk - b0)
                def _(tensor, b0=b0, bn=bn, exp_last=exp_last):
                    tensor.wait_ge(ss.h, exp_last)
                    tensor.wait_ge(ss.h, ptg[0])
                    tensor.wait_ge(vs.h, ptgv[0])
                    for j in range(bn):
                        inst = tensor.transpose(pT[:, j * 128:(j + 1) * 128],
                                                p_row[:, (b0 + j) * 128:(b0 + j + 1) * 128],
                                                iden_b[:])
                    ts.inc(inst)
                block.tensor(_)
                trm = ts.n
                def _(scalar, b0=b0, bn=bn, qt=qt):
                    scalar.wait_ge(ts.h, trm)
                    for j in range(bn):
                        kb = b0 + j
                        inst = scalar.copy(
                            pTs[:, kb * 512 + qt * 128: kb * 512 + qt * 128 + 128],
                            pT[:, j * 128:(j + 1) * 128])
                    ss.inc(inst)
                    ptg[0] = ss.n
                block.scalar(_)

        p1_mv = [0]
        smsk2 = sb("smsk2", [128, 128], F32, stk2b)

        def emit_sm_memset(qc):
            def _(vector, qc=qc):
                vector.wait_ge(ts.h, ymm_last[0])
                if qc == 0:
                    vector.wait_ge(ss.h, P2_SS)
                inst = None
                for j in range(1, 4):
                    kb = qc * 4 + j
                    inst = vector.memset(pTs[:, kb * 512: kb * 512 + j * 128], 0.0)
                if inst is not None:
                    vs.inc(inst)
                    ptz[0] = vs.n
            block.vector(_)

        def emit_softmax_chunk(qc):
            emit_sm_memset(qc)
            mv_next = emit_pass1(qc, 0)
            for qt in range(4):
                e_last = emit_pass2(qc, qt, mv_next)
                if qt < 3:
                    mv_next = emit_pass1(qc, qt + 1)
                emit_transposes(qc, qt, e_last)

        def emit_yacc_group(qc, d, yev_ms, ptr_ss):
            nkb = (qc + 1) * 4
            bank = pC if d % 2 == 0 else pF
            def _(tensor, d=d, bank=bank, nkb=nkb, ptr_ss=ptr_ss):
                tensor.wait_ge(ss.h, ptr_ss)
                tensor.wait_ge(vs.h, ptz[0])
                if d >= 2:
                    tensor.wait_ge(ss.h, yev_ms[d - 2])
                elif epi_v:
                    tensor.wait_ge(vs.h, epi_v[-1])
                for kb in range(nkb):
                    inst = tensor.matmul(bank[:],
                                         v_sb[:, kb * C + d * 128: kb * C + d * 128 + 128],
                                         pTs[:, kb * 512:(kb + 1) * 512],
                                         start=(kb == 0), stop=(kb == nkb - 1))
                ts.inc(inst)
                if d == NCH - 1:
                    ymm_last[0] = ts.n
            block.tensor(_)
            ym = ts.n
            def _(scalar, d=d, bank=bank):
                scalar.wait_ge(ts.h, ym)
                ss.inc(scalar.copy(yT[:, d * 512:(d + 1) * 512], bank[:]))
                yev_ms.append(ss.n)
            block.scalar(_)
            if d == NCH - 1:
                last_yev[0] = yev_ms[-1]

        def emit_oproj_qt(qc, qt, yev_ms):
            if True:
                g = qc * 4 + qt
                slot = g % 2
                def _(sync, g=g, slot=slot):
                    if len(epi_sq) >= 2:
                        sync.wait_ge(ss.h, epi_sq[-2])
                        sync.wait_ge(sp[slot].h, x2st_ms[slot])
                    else:
                        sync.wait_ge(ss.h, norm_ms[NT - 2 + g])
                    lx[slot].inc(sync.dma_start(out=x2t[slot][:],
                                                in_=x_in[g * 128:(g + 1) * 128, :]), 16)
                block.sync(_)
                obanks = (pC, pF)
                def _(tensor, qt=qt):
                    tensor.wait_ge(ss.h, yev_ms[-1])
                    if epi_v:
                        tensor.wait_ge(vs.h, epi_v[-1])
                    for h in range(2):
                        for d in range(NCH):
                            inst = tensor.matmul(
                                obanks[h][:],
                                yT[:, d * 512 + qt * 128: d * 512 + qt * 128 + 128],
                                WoT[:, d * C + h * 512: d * C + (h + 1) * 512],
                                start=(d == 0), stop=(d == NCH - 1))
                    ts.inc(inst)
                block.tensor(_)
                om = ts.n
                def _(vector, qt=qt, g=g, slot=slot, qc=qc):
                    qp = (qc % 2) * 4 + qt
                    vector.wait_ge(ts.h, om)
                    vector.wait_ge(lx[slot].h, lx[slot].n)
                    def ch(inst):
                        vs.inc(inst); vector.wait_ge(vs.h, vs.n)
                    ch(vector.tensor_tensor(x2t[slot][:], x2t[slot][:], bo_bc[:], ALU.add))
                    ch(vector.scalar_tensor_tensor(x2t[slot][:, 0:512], obanks[0][:],
                                                   rzcol[:, qp:qp + 1], x2t[slot][:, 0:512],
                                                   ALU.mult, ALU.add))
                    ch(vector.scalar_tensor_tensor(x2t[slot][:, 512:1024], obanks[1][:],
                                                   rzcol[:, qp:qp + 1], x2t[slot][:, 512:1024],
                                                   ALU.mult, ALU.add))
                    vs.inc(vector.reduce_sum(s1t[:, g:g + 1], x2t[slot][:], axis=AX.X))
                    epi_v.append(vs.n)
                block.vector(_)
                def _(scalar, g=g, slot=slot):
                    scalar.wait_ge(vs.h, epi_v[-1])
                    if len(epi_sq) >= 2:
                        scalar.wait_ge(ss.h, epi_sq[-2])
                    elif sq_ms:
                        scalar.wait_ge(ss.h, sq_ms[-2 + len(epi_sq)])
                    ss.inc(scalar.activation(xsq[slot][:], x2t[slot][:], ACTF.Square,
                                             accum_out=s2t[:, g:g + 1]))
                    epi_sq.append(ss.n)
                block.scalar(_)
                def _(vector, g=g, slot=slot):
                    def ch(inst):
                        vs.inc(inst); vector.wait_ge(vs.h, vs.n)
                    vector.wait_ge(vs.h, vs.n)
                    ch(vector.tensor_scalar(tmpc[:, 3:4], s1t[:, g:g + 1], 1.0 / C,
                                            None, ALU.mult))
                    ch(vector.tensor_scalar(tmpc[:, 3:4], tmpc[:, 3:4], -1.0, None, ALU.mult))
                    vector.wait_ge(ss.h, epi_sq[-1])
                    ch(vector.tensor_scalar(xsq[slot][:], x2t[slot][:], tmpc[:, 3:4],
                                            None, ALU.add))
                    vs.inc(vector.reduce_max(statx[:, g:g + 1], xsq[slot][:], axis=AX.X,
                                             apply_absolute_value=True))
                    epi_v.append(vs.n)
                block.vector(_)
                def _(gpsimd, g=g, slot=slot):
                    gpsimd.wait_ge(vs.h, epi_v[-1])
                    gpsimd.wait_ge(ss.h, epi_sq[-1])
                    sp[slot].inc(gpsimd.dma_start(out=x2_hbm[g * 128:(g + 1) * 128, :],
                                                  in_=x2t[slot][:]), 16)
                    x2st_ms[slot] = sp[slot].n
                block.gpsimd(_)

        stk4w = contextlib.ExitStack()
        wrd = [sb("wrd0", [128, I], F32, stk4w)] * 2
        wqd = sb("wqd", [128, I], BF16, stk4w)
        blkd = sb("blkd", [128, 96], F32, stk4w)
        bind = sb("bind", [128, 96], F32, stk4w)
        wstg = sb("wstg", [128, I], BF16, stk4w)
        P3_TS0 = ts.n
        wd_cast, wd_tr = [], []
        def emit_wd_head(rt):
            slot = rt % 2
            def _(sync, rt=rt, slot=slot):
                if wd_cast:
                    sync.wait_ge(vs.h, wd_cast[-1])
                else:
                    sync.wait_ge(ts.h, P3_TS0)
                lw[slot].inc(sync.dma_start(out=wrd[slot][:],
                                            in_=Wd[rt * 128:(rt + 1) * 128, :]), 16)
            block.sync(_)
            def _(vector, rt=rt, slot=slot):
                vector.wait_ge(lw[slot].h, lw[slot].n)
                if wd_tr:
                    vector.wait_ge(ts.h, wd_tr[-1])
                def ch(inst):
                    vs.inc(inst); vector.wait_ge(vs.h, vs.n)
                w3 = wrd[slot][:].rearrange("p (b k) -> p b k", k=32)
                ch(vector.reduce_max(blkd[:], w3, axis=AX.X, apply_absolute_value=True))
                ch(vector.tensor_scalar(blkd[:], blkd[:], INV_WQMAX, 1.0e-12,
                                        ALU.mult, ALU.max))
                ch(vector.reciprocal(bind[:], blkd[:]))
                bmr = bind[:].unsqueeze(2).broadcast_to([128, 96, 32])
                ch(vector.tensor_tensor(w3, w3, bmr, ALU.mult))
                ch(rnd(vector, wrd[slot][:], wrd[slot][:]))
                bms = blkd[:].unsqueeze(2).broadcast_to([128, 96, 32])
                ch(vector.tensor_tensor(wqd[:].rearrange("p (b k) -> p b k", k=32),
                                        w3, bms, ALU.mult))
                wd_cast.append(vs.n)
            block.vector(_)
        def emit_wd_tr(rt):
            for hb in range(3):
                def _(tensor, rt=rt, hb=hb):
                    tensor.wait_ge(vs.h, wd_cast[rt])
                    tensor.wait_ge(ss.h, pt2g[0])
                    for j in range(8):
                        ii = hb * 8 + j
                        inst = tensor.transpose(pT2[:, j * 128:(j + 1) * 128],
                                                wqd[:, ii * 128:(ii + 1) * 128], iden_b[:])
                    ts.inc(inst)
                    wd_tr.append(ts.n)
                block.tensor(_)
                def _(scalar, rt=rt, hb=hb):
                    scalar.wait_ge(ts.h, wd_tr[-1])
                    if hb == 0:
                        scalar.wait_ge(wds.h, 16 * rt)
                    for j in range(8):
                        ii = hb * 8 + j
                        inst = scalar.copy(wstg[:, ii * 128:(ii + 1) * 128],
                                           pT2[:, j * 128:(j + 1) * 128])
                    ss.inc(inst)
                    pt2g[0] = ss.n
                block.scalar(_)
            def _(sync, rt=rt):
                sync.wait_ge(ss.h, pt2g[0])
                wds.inc(sync.dma_start(
                    out=WdnT_hbm.rearrange("p (ii c) -> p ii c", c=C)[:, :,
                                                                     rt * 128:rt * 128 + 128],
                    in_=wstg[:].rearrange("p (ii c) -> p ii c", c=128)), 16)
            block.sync(_)

        emit_softmax_chunk(0)
        wd_emitted = 0
        for qc in range(NQ):
            nxt = qc + 1 < NQ
            ptr_ss = ss.n
            yev = []
            if nxt:
                emit_sm_memset(qc + 1)
                mv = emit_pass1(qc + 1, 0)
            for d in range(4):
                emit_yacc_group(qc, d, yev, ptr_ss)
            if nxt:
                e_last = emit_pass2(qc + 1, 0, mv)
            for d in range(4, 8):
                emit_yacc_group(qc, d, yev, ptr_ss)
            if nxt:
                mv = emit_pass1(qc + 1, 1)
                emit_transposes(qc + 1, 0, e_last)
            for qt in range(4):
                emit_oproj_qt(qc, qt, yev)
                if nxt and qt < 3:
                    e_last = emit_pass2(qc + 1, qt + 1, mv)
                    if qt < 2:
                        mv = emit_pass1(qc + 1, qt + 2)
                    emit_transposes(qc + 1, qt + 1, e_last)
            if wd_emitted < 8:
                if wd_emitted >= 1:
                    emit_wd_tr(wd_emitted - 1)
                emit_wd_head(wd_emitted)
                wd_emitted += 1
        while wd_emitted < 8:
            if wd_emitted >= 1:
                emit_wd_tr(wd_emitted - 1)
            emit_wd_head(wd_emitted)
            wd_emitted += 1
        emit_wd_tr(7)
        stk4w.close()
        X2_SP = (sp[0].n, sp[1].n)
        stk2b.close()
        stk2a.close()

        def _(vector):
            vector.wait_ge(ss.h, epi_sq[-1])
        block.vector(_)
        ln2_v = emit_ln_chain(epi_v[-1])

        def _(vector):
            vector.wait_ge(vs.h, ln2_v)
            vs.inc(vector.tensor_tensor(statx[:], statx[:], rstdt[:], ALU.mult))
        block.vector(_)
        amax_v = vs.n
        pa_done = ss.n

        def fold_stat(stat_ap, col, pre_v, use_abs=True):
            def _(vector):
                if pre_v is not None:
                    vector.wait_ge(vs.h, pre_v)
                vs.inc(vector.memset(padt[:], NEGBIG if not use_abs else 0.0))
                vector.wait_ge(vs.h, vs.n)
                vs.inc(vector.reduce_max(padt[:, 0:1], stat_ap, axis=AX.X,
                                         apply_absolute_value=use_abs))
            block.vector(_)
            fv = vs.n
            def _(tensor):
                tensor.wait_ge(vs.h, fv)
                ts.inc(tensor.transpose(pA[:, 0:128], padt[:], iden_f[:]))
            block.tensor(_)
            ft = ts.n
            def _(vector):
                vector.wait_ge(ts.h, ft)
                vs.inc(vector.reduce_max(srow[0:1, col:col + 1], pA[0:1, 0:128], axis=AX.X))
            block.vector(_)
            return vs.n

        def barrier(tag, pre_v):
            def _(sync):
                sync.wait_ge(vs.h, pre_v)
                lc.inc(sync.dma_start(out=cc_in[tag], in_=srow[0:1, 0:8]), 16)
            block.sync(_)
            ldn = lc.n
            def _(gpsimd):
                gpsimd.wait_ge(lc.h, ldn)
                gs.inc(gpsimd.collective_compute(
                    "AllReduce", ALU.max, replica_groups=[list(range(8))],
                    ins=[cc_in[tag]], outs=[cc_out[tag]]))
            block.gpsimd(_)
            gn = gs.n
            def _(sync):
                sync.wait_ge(gs.h, gn)
                lc.inc(sync.dma_start(out=srow[0:1, 0:8], in_=cc_out[tag]), 16)
            block.sync(_)
            ldn2 = lc.n
            def _(vector):
                vector.wait_ge(lc.h, ldn2)
                vs.inc(vector.nop())
            block.vector(_)
            return vs.n

        def bcast_scols(pairs, pre_v, pre_extra=None):
            def _(tensor):
                tensor.wait_ge(vs.h, pre_v)
                if pre_extra is not None:
                    tensor.wait_ge(vs.h, pre_extra)
                ts.inc(tensor.matmul(pB[:, 0:16], ones_f[:], srow[:],
                                     start=True, stop=True))
            block.tensor(_)
            bt = ts.n
            def _(vector):
                vector.wait_ge(ts.h, bt)
                for (rc, cc_) in pairs:
                    inst = vector.tensor_copy(scols[:, cc_:cc_ + 1], pB[:, rc:rc + 1])
                vs.inc(inst)
            block.vector(_)
            return vs.n

        fx = fold_stat(statx[:, 0:NT], 0, amax_v)
        bx = barrier("X", fx)
        def _(vector):
            vector.wait_ge(vs.h, bx)
            def ch(inst):
                vs.inc(inst); vector.wait_ge(vs.h, vs.n)
            ch(vector.tensor_scalar(srow[0:1, 1:2], srow[0:1, 0:1], INV_QMAX, 1.0e-12,
                                    ALU.mult, ALU.max))
            ch(vector.reciprocal(srow[0:1, 2:3], srow[0:1, 1:2]))
        block.vector(_)
        bcx = bcast_scols([(1, SX), (2, ISX)], vs.n)
        rsdx = sb("rsdx", [128, NT], F32)
        nmbx = sb("nmbx", [128, NT], F32)
        def _(vector):
            vector.wait_ge(vs.h, bcx)
            def ch(inst):
                vs.inc(inst); vector.wait_ge(vs.h, vs.n)
            ch(vector.tensor_scalar(rsdx[:], rstdt[:], scols[:, ISX:ISX + 1], None, ALU.mult))
            ch(vector.tensor_scalar(nmbx[:], nmbt[:], scols[:, ISX:ISX + 1], None, ALU.mult))
        block.vector(_)
        bcx2 = vs.n

        xqT = sb("xqT", [128, NCH * T], BF16, stk3)
        wkt = [sb(f"wkt{i}", [128, C], F32, stk3) for i in range(2)]
        WupT = sb("WupT", [128, NCH * I], BF16, stk3)
        WgT = sb("WgT", [128, NCH * I], BF16, stk3)
        wrs = [sb(f"wrs{i}", [128, C], F32, stk3) for i in range(2)]
        wqb = [sb(f"wqb{i}", [128, C], BF16, stk3) for i in range(2)]
        blkmx = [sb(f"blkmx{i}", [128, 32], F32, stk3) for i in range(2)]
        binx = sb("binx", [128, 32], F32, stk3)

        wp_cast, wp_tr = [], []
        wp_i = [0]
        def emit_wprep_tile(Wsrc, dstT, rt):
            slot = wp_i[0] % 2
            wp_i[0] += 1
            def _(sync, rt=rt, slot=slot):
                if len(wp_cast) >= 2:
                    sync.wait_ge(vs.h, wp_cast[-2])
                elif not wp_cast:
                    sync.wait_ge(vs.h, bcx)
                    sync.wait_ge(wds.h, 128)
                lw[slot].inc(sync.dma_start(out=wrs[slot][:],
                                            in_=Wsrc[rt * 128:(rt + 1) * 128, :]), 16)
            block.sync(_)
            def _(vector, rt=rt, slot=slot):
                vector.wait_ge(lw[slot].h, lw[slot].n)
                if len(wp_tr) >= 2:
                    vector.wait_ge(ts.h, wp_tr[-2])
                def ch(inst):
                    vs.inc(inst); vector.wait_ge(vs.h, vs.n)
                w3 = wrs[slot][:].rearrange("p (b k) -> p b k", k=32)
                ch(vector.reduce_max(blkmx[slot][:], w3, axis=AX.X, apply_absolute_value=True))
                ch(vector.tensor_scalar(blkmx[slot][:], blkmx[slot][:], INV_WQMAX, 1.0e-12,
                                        ALU.mult, ALU.max))
                ch(vector.reciprocal(binx[:], blkmx[slot][:]))
                bmr = binx[:].unsqueeze(2).broadcast_to([128, 32, 32])
                ch(vector.tensor_tensor(w3, w3, bmr, ALU.mult))
                ch(rnd(vector, wrs[slot][:], wrs[slot][:]))
                bms = blkmx[slot][:].unsqueeze(2).broadcast_to([128, 32, 32])
                ch(vector.tensor_tensor(wqb[slot][:].rearrange("p (b k) -> p b k", k=32),
                                        w3, bms, ALU.mult))
                wp_cast.append(vs.n)
            block.vector(_)
            def _(tensor, rt=rt, slot=slot):
                tensor.wait_ge(vs.h, wp_cast[-1])
                tensor.wait_ge(ss.h, pt2g[0])
                for ci in range(NCH):
                    inst = tensor.transpose(pT2[:, ci * 128:(ci + 1) * 128],
                                            wqb[slot][:, ci * 128:(ci + 1) * 128], iden_b[:])
                ts.inc(inst)
                wp_tr.append(ts.n)
            block.tensor(_)
            def _(scalar, rt=rt, first=(wp_i[0] == 1)):
                scalar.wait_ge(ts.h, wp_tr[-1])
                if first:
                    scalar.wait_ge(wds.h, 128)
                for ci in range(NCH):
                    inst = scalar.copy(dstT[:, ci * I + rt * 128: ci * I + rt * 128 + 128],
                                       pT2[:, ci * 128:(ci + 1) * 128])
                ss.inc(inst)
                pt2g[0] = ss.n
            block.scalar(_)

        qdq_ev = []
        wtiles = [(Wu, WupT, rt) for rt in range(NI)] + [(Wg, WgT, rt) for rt in range(NI)]
        wi = 0
        for t in range(NT):
            slot = t % 2
            def _(sync, t=t, slot=slot):
                if t >= 2:
                    sync.wait_ge(ss.h, qdq_ev[-1])
                    sync.wait_ge(ss.h, ptg[0])
                else:
                    sync.wait_ge(sp[slot].h, X2_SP[slot])
                lx[slot].inc(sync.dma_start(out=x2t[slot][:],
                                            in_=x2_hbm[t * 128:(t + 1) * 128, :]), 16)
            block.sync(_)
            def _(scalar, t=t, slot=slot):
                scalar.wait_ge(lx[slot].h, lx[slot].n)
                scalar.wait_ge(vs.h, bcx2)
                if len(qdq_ev) >= 2:
                    scalar.wait_ge(ss.h, qdq_ev[-2])
                ss.inc(scalar.activation(wkt[slot][:], x2t[slot][:], ACTF.Identity,
                                         bias=nmbx[:, t:t + 1], scale=rsdx[:, t:t + 1]))
            block.scalar(_)
            y_s = ss.n
            def _(vector, t=t, slot=slot):
                vector.wait_ge(ss.h, y_s)
                vs.inc(rnd(vector, wkt[slot][:], wkt[slot][:]))
            block.vector(_)
            k_v = vs.n
            def _(scalar, t=t, slot=slot):
                scalar.wait_ge(vs.h, k_v)
                ss.inc(scalar.activation(xlt[slot][:], wkt[slot][:], ACTF.Copy,
                                         scale=scols[:, SX:SX + 1]))
            block.scalar(_)
            xq_s = ss.n
            def _(tensor, t=t, slot=slot):
                tensor.wait_ge(ss.h, xq_s)
                tensor.wait_ge(ss.h, ptg[0])
                tensor.wait_ge(vs.h, ptgv[0])
                for ci in range(NCH):
                    inst = tensor.transpose(pT[:, ci * 128:(ci + 1) * 128],
                                            xlt[slot][:, ci * 128:(ci + 1) * 128], iden_b[:])
                ts.inc(inst)
            block.tensor(_)
            trm = ts.n
            def _(scalar, t=t):
                scalar.wait_ge(ts.h, trm)
                if t == 0:
                    scalar.wait_ge(wds.h, 128)
                for ci in range(NCH):
                    inst = scalar.copy(xqT[:, ci * T + t * 128: ci * T + t * 128 + 128],
                                       pT[:, ci * 128:(ci + 1) * 128])
                ss.inc(inst)
                ptg[0] = ss.n
            block.scalar(_)
            qdq_ev.append(ss.n)
            nw = 2 if (t % 2 == 0) else 1
            for _k in range(nw):
                if wi < len(wtiles):
                    Ws, dT, rt = wtiles[wi]
                    emit_wprep_tile(Ws, dT, rt)
                    wi += 1
        while wi < len(wtiles):
            Ws, dT, rt = wtiles[wi]
            emit_wprep_tile(Ws, dT, rt)
            wi += 1
        P4_done = ss.n
        P4_wv = vs.n

        ustg = [sb("ustg0", [128, T], BF16, stk3)] * 2
        f1_ev = []
        f1_st = [0, 0]
        f1_stat_v = []

        f1_lo = []
        f1_hi = []
        def emit_big_mm(dstT_hbm, WT, stat, first, stat2=None):
            H = T // 2
            for ic in range(NI):
                slot = ic % 2
                banks = (pA, pB) if ic % 2 == 0 else (pC, pD)
                ev_here = []
                for f in range(NQ):
                    bank = banks[f % 2]
                    def _(tensor, ic=ic, f=f, bank=bank):
                        if first and ic == 0 and f == 0:
                            tensor.wait_ge(ss.h, P4_done)
                            tensor.wait_ge(vs.h, bcx)
                        if f >= 2:
                            tensor.wait_ge(ss.h, ev_here[f - 2])
                        elif f1_ev:
                            tensor.wait_ge(ss.h, f1_ev[-1])
                        for ci in range(NCH):
                            inst = tensor.matmul(
                                bank[:],
                                WT[:, ci * I + ic * 128: ci * I + ic * 128 + 128],
                                xqT[:, ci * T + f * 512: ci * T + (f + 1) * 512],
                                start=(ci == 0), stop=(ci == NCH - 1))
                        ts.inc(inst)
                    block.tensor(_)
                    mm = ts.n
                    def _(scalar, ic=ic, f=f, bank=bank, slot=slot):
                        scalar.wait_ge(ts.h, mm)
                        if f == 0 and NQ >= 2 and f1_lo:
                            psl, pv, pvs = f1_lo[-1]
                            scalar.wait_ge(sq[psl].h, pv)
                            scalar.wait_ge(vs.h, pvs)
                        if f == 0 and NQ < 2 and f1_hi:
                            psl, pv, pvs = f1_hi[-1]
                            scalar.wait_ge(sp[psl].h, pv)
                            scalar.wait_ge(vs.h, pvs)
                        if NQ >= 2 and f == NQ // 2 and f1_hi:
                            psl, pv, pvs = f1_hi[-1]
                            scalar.wait_ge(sp[psl].h, pv)
                            scalar.wait_ge(vs.h, pvs)
                        ss.inc(scalar.copy(ustg[slot][:, f * 512:(f + 1) * 512], bank[:]))
                        ev_here.append(ss.n)
                    block.scalar(_)
                    if NQ >= 2 and f == NQ // 2 - 1:
                        def _(vector, ic=ic, slot=slot):
                            vector.wait_ge(ss.h, ev_here[-1])
                            if stat2 is not None:
                                vector.reduce_max(stat2[:, 2 * ic:2 * ic + 1],
                                                  ustg[slot][:, 0:H], axis=AX.X)
                            vs.inc(vector.reduce_max(stat[:, 2 * ic:2 * ic + 1],
                                                     ustg[slot][:, 0:H], axis=AX.X,
                                                     apply_absolute_value=True))
                        block.vector(_)
                        vlo = vs.n
                        def _(gpsimd, ic=ic, slot=slot, evv=ev_here[-1]):
                            gpsimd.wait_ge(ss.h, evv)
                            sq[slot].inc(gpsimd.dma_start(
                                out=dstT_hbm[ic * 128:(ic + 1) * 128, 0:H],
                                in_=ustg[slot][:, 0:H]), 16)
                        block.gpsimd(_)
                        f1_lo.append((slot, sq[slot].n, vlo))
                f1_ev.append(ev_here[-1])
                lo = 0 if NQ >= 2 else None
                def _(vector, ic=ic, slot=slot, lo=lo):
                    vector.wait_ge(ss.h, ev_here[-1])
                    if lo is not None:
                        src = ustg[slot][:, H:T]
                        c0 = 2 * ic + 1
                    else:
                        src = ustg[slot][:]
                        c0 = 2 * ic
                    if stat2 is not None:
                        vector.reduce_max(stat2[:, c0:c0 + 1], src, axis=AX.X)
                        if lo is None:
                            vector.reduce_max(stat2[:, c0 + 1:c0 + 2], src, axis=AX.X)
                    if lo is None:
                        vector.reduce_max(stat[:, c0 + 1:c0 + 2], src, axis=AX.X,
                                          apply_absolute_value=True)
                    vs.inc(vector.reduce_max(stat[:, c0:c0 + 1], src, axis=AX.X,
                                             apply_absolute_value=True))
                    f1_stat_v.append(vs.n)
                block.vector(_)
                def _(gpsimd, ic=ic, slot=slot, lo=lo):
                    gpsimd.wait_ge(ss.h, ev_here[-1])
                    if lo is not None:
                        sp[slot].inc(gpsimd.dma_start(
                            out=dstT_hbm[ic * 128:(ic + 1) * 128, H:T],
                            in_=ustg[slot][:, H:T]), 16)
                    else:
                        sp[slot].inc(gpsimd.dma_start(
                            out=dstT_hbm[ic * 128:(ic + 1) * 128, :],
                            in_=ustg[slot][:]), 16)
                    f1_st[slot] = sp[slot].n
                block.gpsimd(_)
                f1_hi.append((slot, sp[slot].n, f1_stat_v[-1]))

        emit_big_mm(U_hbm, WupT, ustat, True)
        emit_big_mm(G_hbm, WgT, gstat, False, stat2=gstat2)
        F1_V = vs.n
        F1_SP = (sp[0].n, sp[1].n)
        F1_SQ = (sq[0].n, sq[1].n)

        fa1 = fold_stat(ustat[:, 0:2 * NI], 0, F1_V)
        fa2 = fold_stat(gstat[:, 0:2 * NI], 1, None)
        fa3 = fold_stat(gstat2[:, 0:2 * NI], 2, None, use_abs=False)
        ba = barrier("A", fa3)
        def _(vector):
            vector.wait_ge(vs.h, ba)
            def ch(inst):
                vs.inc(inst); vector.wait_ge(vs.h, vs.n)
            ch(vector.tensor_scalar(srow[0:1, 8:9], srow[0:1, 0:1], INV_QMAX, 1.0e-12,
                                    ALU.mult, ALU.max))
            ch(vector.reciprocal(srow[0:1, 9:10], srow[0:1, 8:9]))
            ch(vector.tensor_scalar(srow[0:1, 10:11], srow[0:1, 1:2], INV_QMAX, 1.0e-12,
                                    ALU.mult, ALU.max))
            ch(vector.reciprocal(srow[0:1, 11:12], srow[0:1, 10:11]))
            ch(vector.tensor_scalar(srow[0:1, 12:13], srow[0:1, 10:11], 1.0 / 65536.0,
                                    None, ALU.mult))
            ch(vector.tensor_tensor(srow[0:1, 3:4], srow[0:1, 2:3], srow[0:1, 11:12],
                                    ALU.mult))
            ch(vector.tensor_scalar(srow[0:1, 3:4], srow[0:1, 3:4], mcol[0:1, :],
                                    nmcol[0:1, :], ALU.add, ALU.add))
            ch(vector.tensor_tensor(srow[0:1, 3:4], srow[0:1, 3:4], srow[0:1, 10:11],
                                    ALU.mult))
        block.vector(_)
        smA = vs.n
        def _(scalar):
            scalar.wait_ge(vs.h, smA)
            ss.inc(scalar.activation(srow[0:1, 6:7], srow[0:1, 3:4], ACTF.Sigmoid))
        block.scalar(_)
        smAs = ss.n
        def _(vector):
            vector.wait_ge(ss.h, smAs)
            def ch(inst):
                vs.inc(inst); vector.wait_ge(vs.h, vs.n)
            ch(vector.tensor_scalar(srow[0:1, 6:7], srow[0:1, 6:7], 65536.0, U16_CAP,
                                    ALU.mult, ALU.min))
            ch(vector.tensor_scalar(srow[0:1, 6:7], srow[0:1, 6:7], mcol[0:1, :],
                                    nmcol[0:1, :], ALU.add, ALU.add))
            ch(vector.tensor_tensor(srow[0:1, 6:7], srow[0:1, 6:7], srow[0:1, 3:4],
                                    ALU.mult))
            ch(vector.tensor_scalar(srow[0:1, 6:7], srow[0:1, 6:7], 1.0 / 65536.0,
                                    None, ALU.mult))
            ch(vector.tensor_scalar(srow[0:1, 4:5], srow[0:1, 6:7], INV_QMAX, 1.0e-12,
                                    ALU.mult, ALU.max))
            ch(vector.reciprocal(srow[0:1, 5:6], srow[0:1, 4:5]))
            ch(vector.tensor_tensor(srow[0:1, 7:8], srow[0:1, 12:13], srow[0:1, 5:6],
                                    ALU.mult))
        block.vector(_)
        bca = bcast_scols([(8, SU), (9, ISU), (10, SG), (11, ISG), (12, SGQ),
                           (7, ISB)], vs.n)

        stk3.close()

        WdnT = sb("WdnT", [128, NI * C], BF16, stk4o)
        def _(sync):
            sync.wait_ge(sp[0].h, F1_SP[0])
            sync.wait_ge(sp[1].h, F1_SP[1])
            sync.wait_ge(sq[0].h, F1_SQ[0])
            sync.wait_ge(sq[1].h, F1_SQ[1])
            sync.wait_ge(wds.h, 16 * 8)
            for q in range(4):
                wl.inc(sync.dma_start(
                    out=WdnT[:, q * (NI * C // 4):(q + 1) * (NI * C // 4)],
                    in_=WdnT_hbm[:, q * (NI * C // 4):(q + 1) * (NI * C // 4)]), 16)
        block.sync(_)
        gbuf = [sb("gbuf0", [128, T], BF16, stk4a)] * 2
        ubuf = [sb(f"ubuf{i}", [128, T], BF16, stk4a) for i in range(2)]
        A1 = [sb(f"A1{i}", [128, T], F32, stk4a) for i in range(3)]
        A2 = [sb(f"A2{i}", [128, T], F32, stk4a) for i in range(2)]
        A3 = [sb(f"A3{i}", [128, T], F32, stk4a) for i in range(2)]
        f23_v, f23_pp, f23_yg, f23_sg, f23_yu = [], [], [], [], []
        qq_v = {}
        spq = {}
        def emit_t1(ic):
            slot = ic % 2
            s3 = ic % 3
            def _(vector, ic=ic, slot=slot, s3=s3):
                vector.wait_ge(ss.h, f23_sg[ic])
                def ch(inst):
                    vs.inc(inst); vector.wait_ge(vs.h, vs.n)
                ch(vector.tensor_scalar(A1[s3][:], A1[s3][:], 65536.0, U16_CAP,
                                        ALU.mult, ALU.min))
                ch(rnd(vector, A1[s3][:], A1[s3][:]))
                if ic >= 2:
                    vector.wait_ge(sp[slot].h, spq[ic - 2])
                ch(vector.tensor_tensor(A3[slot][:], A2[slot][:], A1[s3][:], ALU.mult))
                ch(vector.tensor_scalar(A1[s3][:], A3[slot][:], scols[:, ISB:ISB + 1],
                                        mcol[:], ALU.mult, ALU.add))
                ch(vector.tensor_scalar(A1[s3][:], A1[s3][:], nmcol[:], None, ALU.add))
                f23_pp.append(vs.n)
            block.vector(_)
        def emit_t2(ic):
            slot = ic % 2
            s3 = ic % 3
            def _(vector, ic=ic, slot=slot, s3=s3):
                vector.wait_ge(vs.h, f23_pp[ic])
                vector.wait_ge(lu[slot].h, 16 * (ic // 2 + 1))
                def ch(inst):
                    vs.inc(inst); vector.wait_ge(vs.h, vs.n)
                ch(vector.tensor_scalar(A2[slot][:], ubuf[slot][:], scols[:, ISU:ISU + 1],
                                        mcol[:], ALU.mult, ALU.add))
                ch(vector.tensor_scalar(A2[slot][:], A2[slot][:], nmcol[:], None, ALU.add))
                ch(vector.tensor_tensor(A3[slot][:], A1[s3][:], A2[slot][:], ALU.mult))
                qq_v[ic] = vs.n
                vs.inc(vector.reduce_max(qstat[:, ic:ic + 1], A3[slot][:], axis=AX.X,
                                         apply_absolute_value=True))
                f23_v.append(vs.n)
            block.vector(_)
            def _(gpsimd, ic=ic, slot=slot):
                gpsimd.wait_ge(vs.h, f23_v[-1])
                sp[slot].inc(gpsimd.dma_start(out=QQ_hbm[ic * 128:(ic + 1) * 128, :],
                                              in_=A3[slot][:]), 16)
                spq[ic] = sp[slot].n
            block.gpsimd(_)

        for ic in range(NI):
            slot = ic % 2
            if ic >= 2:
                emit_t2(ic - 2)
            def _(sync, ic=ic, slot=slot):
                if ic < 2:
                    sync.wait_ge(sp[0].h, F1_SP[0])
                    sync.wait_ge(sp[1].h, F1_SP[1])
                    sync.wait_ge(sq[0].h, F1_SQ[0])
                    sync.wait_ge(sq[1].h, F1_SQ[1])
                if f23_yg:
                    sync.wait_ge(ss.h, f23_yg[-1])
                lr[slot].inc(sync.dma_start(out=gbuf[slot][:],
                                            in_=G_hbm[ic * 128:(ic + 1) * 128, :]), 16)
            block.sync(_)
            s3 = ic % 3
            def _(scalar, ic=ic, slot=slot, s3=s3):
                scalar.wait_ge(lr[slot].h, lr[slot].n)
                scalar.wait_ge(vs.h, bca)
                if ic >= 3:
                    scalar.wait_ge(vs.h, qq_v[ic - 3])
                ss.inc(scalar.activation(A1[s3][:], gbuf[slot][:], ACTF.Copy,
                                         scale=scols[:, ISG:ISG + 1]))
                f23_yg.append(ss.n)
            block.scalar(_)
            yg_s = ss.n
            def _(vector, ic=ic, slot=slot, s3=s3):
                vector.wait_ge(ss.h, yg_s)
                if ic >= 2:
                    vector.wait_ge(vs.h, qq_v[ic - 2])
                vs.inc(rnd(vector, A2[slot][:], A1[s3][:]))
            block.vector(_)
            kg_v = vs.n
            def _(scalar, ic=ic, slot=slot, s3=s3):
                scalar.wait_ge(vs.h, kg_v)
                ss.inc(scalar.activation(A1[s3][:], A2[slot][:], ACTF.Sigmoid,
                                         scale=scols[:, SG:SG + 1]))
                f23_sg.append(ss.n)
            block.scalar(_)
            if ic >= 1:
                emit_t1(ic - 1)
            def _(sync, ic=ic, slot=slot):
                if ic < 2:
                    sync.wait_ge(sp[0].h, F1_SP[0])
                    sync.wait_ge(sp[1].h, F1_SP[1])
                    sync.wait_ge(sq[0].h, F1_SQ[0])
                    sync.wait_ge(sq[1].h, F1_SQ[1])
                else:
                    sync.wait_ge(vs.h, f23_v[ic - 2])
                lu[slot].inc(sync.dma_start(out=ubuf[slot][:],
                                              in_=U_hbm[ic * 128:(ic + 1) * 128, :]), 16)
            block.sync(_)
        emit_t2(NI - 2)
        emit_t1(NI - 1)
        emit_t2(NI - 1)
        F3_V = vs.n
        F3_SP = (sp[0].n, sp[1].n)

        fc = fold_stat(qstat[:, 0:NI], 0, F3_V)
        bc = barrier("Cc", fc)
        def _(vector):
            vector.wait_ge(vs.h, bc)
            def ch(inst):
                vs.inc(inst); vector.wait_ge(vs.h, vs.n)
            ch(vector.tensor_tensor(srow[0:1, 6:7], srow[0:1, 4:5], srow[0:1, 8:9],
                                    ALU.mult))
            ch(vector.tensor_tensor(srow[0:1, 7:8], srow[0:1, 0:1], srow[0:1, 6:7],
                                    ALU.mult))
            ch(vector.tensor_scalar(srow[0:1, 13:14], srow[0:1, 7:8], INV_QMAX, 1.0e-12,
                                    ALU.mult, ALU.max))
            ch(vector.reciprocal(srow[0:1, 14:15], srow[0:1, 13:14]))
            ch(vector.tensor_tensor(srow[0:1, 15:16], srow[0:1, 6:7], srow[0:1, 14:15],
                                    ALU.mult))
        block.vector(_)
        bcc = bcast_scols([(15, ISCQ), (13, SC)], vs.n)

        stk4a.close()
        oT = [sb(f"oT{i}", [128, NI * TQ], BF16, stk4b) for i in range(2)]
        qqb = [sb(f"qqb{i}", [128, TQ], F32, stk4b) for i in range(4)]
        lr4 = [lr[0], lr[1], mk("lr2"), mk("lr3")]
        oki = [sb(f"oki{i}", [128, TQ], F32, stk4b) for i in range(2)]

        out_v = []
        mmq_ms = {}
        dq_ms = {}
        row_lists = {}
        bank_user = {0: None, 1: None}

        def emit_dequant_quarter(tq):
            qslot = tq % 2
            dq_prev = row_lists.get(tq - 1)
            row_ms = []
            row_lists[tq] = row_ms
            for ic in range(NI):
                slot = ic % 2
                qs = ic % 4
                def _(sync, tq=tq, ic=ic, qs=qs):
                    if tq == 0 and ic < 4:
                        sync.wait_ge(sp[0].h, F3_SP[0])
                        sync.wait_ge(sp[1].h, F3_SP[1])
                    elif ic >= 4:
                        sync.wait_ge(vs.h, row_ms[ic - 4])
                    else:
                        sync.wait_ge(vs.h, dq_prev[ic - 4])
                    lr4[qs].inc(sync.dma_start(
                        out=qqb[qs][:],
                        in_=QQ_hbm[ic * 128:(ic + 1) * 128, tq * TQ:(tq + 1) * TQ]), 16)
                block.sync(_)
                def _(scalar, tq=tq, ic=ic, slot=slot, qs=qs):
                    scalar.wait_ge(lr4[qs].h, lr4[qs].n)
                    scalar.wait_ge(vs.h, bcc)
                    if ic >= 2:
                        scalar.wait_ge(vs.h, row_ms[ic - 2])
                    elif dq_prev is not None:
                        scalar.wait_ge(vs.h, dq_prev[ic - 2])
                    ss.inc(scalar.activation(oki[slot][:], qqb[qs][:], ACTF.Copy,
                                             scale=scols[:, ISCQ:ISCQ + 1]))
                block.scalar(_)
                yq_s = ss.n
                def _(vector, tq=tq, ic=ic, qslot=qslot, slot=slot):
                    vector.wait_ge(ss.h, yq_s)
                    if tq >= 2 and ic == 0:
                        vector.wait_ge(ts.h, mmq_ms[tq - 2])
                    vs.inc(rnd(vector, oki[slot][:], oki[slot][:]))
                    vector.wait_ge(vs.h, vs.n)
                    vs.inc(vector.tensor_scalar(oT[qslot][:, ic * TQ:(ic + 1) * TQ],
                                                oki[slot][:], scols[:, SC:SC + 1], None,
                                                ALU.mult))
                    row_ms.append(vs.n)
                block.vector(_)
            dq_ms[tq] = vs.n

        def emit_mm_quarter(tq):
            qslot = tq % 2
            for tt in range(NTQ):
                g = tq * NTQ + tt
                banks = (pA, pB) if tt % 2 == 0 else (pC, pD)
                slot = g % 2
                def _(gpsimd, g=g, slot=slot):
                    if len(out_v) >= 2:
                        gpsimd.wait_ge(vs.h, out_v[-2])
                        gpsimd.wait_ge(so[slot].h, so[slot].n)
                    else:
                        gpsimd.wait_ge(vs.h, bcc)
                    lg[slot].inc(gpsimd.dma_start(out=x2t[slot][:],
                                                  in_=x2_hbm[g * 128:(g + 1) * 128, :]), 16)
                block.gpsimd(_)
                def _(tensor, tq=tq, tt=tt, banks=banks, qslot=qslot):
                    if tq == 0 and tt == 0:
                        tensor.wait_ge(wl.h, 64)
                    tensor.wait_ge(vs.h, dq_ms[tq])
                    bu = bank_user[tt % 2]
                    if bu is not None:
                        tensor.wait_ge(vs.h, out_v[bu])
                    else:
                        tensor.wait_ge(vs.h, bcc)
                    for h in range(2):
                        for ic in range(NI):
                            inst = tensor.matmul(
                                banks[h][:],
                                oT[qslot][:, ic * TQ + tt * 128: ic * TQ + tt * 128 + 128],
                                WdnT[:, ic * C + h * 512: ic * C + (h + 1) * 512],
                                start=(ic == 0), stop=(ic == NI - 1))
                    ts.inc(inst)
                block.tensor(_)
                mmf = ts.n
                if tt == NTQ - 1:
                    mmq_ms[tq] = ts.n
                def _(vector, g=g, banks=banks, slot=slot):
                    vector.wait_ge(ts.h, mmf)
                    vector.wait_ge(lg[slot].h, lg[slot].n)
                    vector.tensor_tensor(x2t[slot][:, 0:512], banks[0][:],
                                         x2t[slot][:, 0:512], ALU.add)
                    vs.inc(vector.tensor_tensor(x2t[slot][:, 512:1024], banks[1][:],
                                                x2t[slot][:, 512:1024], ALU.add))
                    out_v.append(vs.n)
                    bank_user[tt % 2] = len(out_v) - 1
                block.vector(_)
                def _(gpsimd, g=g, slot=slot):
                    gpsimd.wait_ge(vs.h, out_v[-1])
                    so[slot].inc(gpsimd.dma_start(out=out[g * 128:(g + 1) * 128, :],
                                                  in_=x2t[slot][:]), 16)
                block.gpsimd(_)

        emit_dequant_quarter(0)
        for tq in range(4):
            if tq < 3:
                emit_dequant_quarter(tq + 1)
            emit_mm_quarter(tq)

        def _(gpsimd):
            gpsimd.wait_ge(so[0].h, so[0].n)
            gpsimd.wait_ge(so[1].h, so[1].n)
        block.gpsimd(_)

    with nc.Block() as rb:
        def _replay(eng, ops):
            for attr, a, k, inst in ops:
                real = getattr(eng, attr)(*a, **k)
                for sem, v in inst.incs:
                    real.then_inc(sem, v)
        rb.sync(lambda e: _replay(e, block.eng["sync"].ops))
        rb.scalar(lambda e: _replay(e, block.eng["scalar"].ops))
        rb.vector(lambda e: _replay(e, block.eng["vector"].ops))
        rb.tensor(lambda e: _replay(e, block.eng["tensor"].ops))
        rb.gpsimd(lambda e: _replay(e, block.eng["gpsimd"].ops))
    stk4b.close()
    stk4o.close()

    return nc


_NC_CACHE = {}


def _consts():
    iden = np.eye(128, dtype=np.float32)
    tril = np.tril(np.ones((128, 128), dtype=bool))
    maskneg = np.where(tril, np.float32(0.0), np.float32(NEGBIG)).astype(np.float32)
    ones = np.ones((1, 128), dtype=np.float32)
    return iden, maskneg, ones


def make_in_maps(x, Wb, Wv, bv, Wo, bo, Wg, Wu, Wd):
    iden, maskneg, ones = _consts()
    base = {
        "Wb": np.ascontiguousarray(np.asarray(Wb, np.float32)),
        "Wv": np.ascontiguousarray(np.asarray(Wv, np.float32)),
        "bv": np.asarray(bv, np.float32).reshape(1, C),
        "Wo": np.ascontiguousarray(np.asarray(Wo, np.float32)),
        "bo": np.asarray(bo, np.float32).reshape(1, C),
        "Wg": np.ascontiguousarray(np.asarray(Wg, np.float32)),
        "Wu": np.ascontiguousarray(np.asarray(Wu, np.float32)),
        "Wd": np.ascontiguousarray(np.asarray(Wd, np.float32)),
        "iden_in": iden, "maskneg_in": maskneg, "ones_in": ones,
    }
    return [dict(base, x_in=np.ascontiguousarray(np.asarray(xb, np.float32)))
            for xb in x]


def kernel(L0_attn_hidden_states, L0_attn_W_basis, L0_attn_W_v, L0_attn_b_v,
           L0_attn_W_o, L0_attn_b_o, L0_ffn_w_gate, L0_ffn_w_up, L0_ffn_w_down):
    x = np.ascontiguousarray(np.asarray(L0_attn_hidden_states, dtype=np.float32))
    B, T, C_ = x.shape
    assert C_ == C and B == 8
    if T not in _NC_CACHE:
        _NC_CACHE[T] = build_nc(T)
    nc = _NC_CACHE[T]
    in_maps = make_in_maps(x, L0_attn_W_basis, L0_attn_W_v, L0_attn_b_v,
                           L0_attn_W_o, L0_attn_b_o, L0_ffn_w_gate, L0_ffn_w_up,
                           L0_ffn_w_down)
    res = run_bass_kernel_spmd(nc, in_maps, core_ids=list(range(B)))
    return np.stack([np.asarray(res.results[b]["out"]) for b in range(B)], axis=0)


# version 26 (final)
# speedup vs baseline: 1.0483x; 1.0285x over previous
import contextlib
import numpy as np

import concourse.bass as bass
import concourse.mybir as mybir
from concourse.bass_utils import run_bass_kernel_spmd

F32 = mybir.dt.float32
BF16 = mybir.dt.bfloat16
ALU = mybir.AluOpType
ACTF = mybir.ActivationFunctionType
AX = mybir.AxisListType

C = 1024
NCH = 8
R = 64
I = 3072
NI = 24
INV_QMAX = 1.0 / 32767.0
INV_WQMAX = 1.0 / 7.0
U16_CAP = 65535.4921875
NEGBIG = -1.0e30


class Sem:
    def __init__(self, ctx, nc, name):
        self.h = ctx.enter_context(nc.semaphore(name))
        self.n = 0

    def inc(self, inst, k=1):
        inst.then_inc(self.h, k)
        self.n += k
        return self.n


def build_nc(T=4096):
    NT = T // 128
    NQ = T // 512
    TQ = T // 4
    NTQ = TQ // 128
    nc = bass.Bass("TRN2", target_bir_lowering=False, debug=False)
    ctx = contextlib.ExitStack()

    x_in = nc.dram_tensor("x_in", [T, C], F32, kind="ExternalInput").ap()
    Wb = nc.dram_tensor("Wb", [R, C], F32, kind="ExternalInput").ap()
    Wv = nc.dram_tensor("Wv", [C, C], F32, kind="ExternalInput").ap()
    bv = nc.dram_tensor("bv", [1, C], F32, kind="ExternalInput").ap()
    Wo = nc.dram_tensor("Wo", [C, C], F32, kind="ExternalInput").ap()
    bo = nc.dram_tensor("bo", [1, C], F32, kind="ExternalInput").ap()
    Wg = nc.dram_tensor("Wg", [I, C], F32, kind="ExternalInput").ap()
    Wu = nc.dram_tensor("Wu", [I, C], F32, kind="ExternalInput").ap()
    Wd = nc.dram_tensor("Wd", [C, I], F32, kind="ExternalInput").ap()
    iden_in = nc.dram_tensor("iden_in", [128, 128], F32, kind="ExternalInput").ap()
    maskneg_in = nc.dram_tensor("maskneg_in", [128, 128], F32, kind="ExternalInput").ap()
    ones_in = nc.dram_tensor("ones_in", [1, 128], F32, kind="ExternalInput").ap()
    out = nc.dram_tensor("out", [T, C], F32, kind="ExternalOutput").ap()

    x2_hbm = nc.dram_tensor("x2_hbm", [T, C], F32, kind="Internal").ap()
    U_hbm = nc.dram_tensor("U_hbm", [I, T], BF16, kind="Internal").ap()
    G_hbm = nc.dram_tensor("G_hbm", [I, T], BF16, kind="Internal").ap()
    QQ_hbm = nc.dram_tensor("QQ_hbm", [I, T], F32, kind="Internal").ap()
    WdnT_hbm = nc.dram_tensor("WdnT_hbm", [128, (I // 128) * C], BF16, kind="Internal").ap()
    cc_in, cc_out = {}, {}
    for bn in ("X", "A", "B", "Cc"):
        cc_in[bn] = nc.dram_tensor(f"cc_in_{bn}", [1, 8], F32, kind="Internal").ap()
        cc_out[bn] = nc.dram_tensor(f"cc_out_{bn}", [1, 8], F32, kind="Internal",
                                    addr_space="Shared").ap()

    def sb(name, shape, dt, stack=ctx, side=None):
        if side is None:
            return stack.enter_context(nc.sbuf_tensor(name, shape, dt))
        return stack.enter_context(nc.sbuf_tensor(name, shape, dt, side=side))

    iden_f = sb("iden_f", [128, 128], F32)
    iden_b = sb("iden_b", [128, 128], BF16)
    ones_f = sb("ones_f", [1, 128], F32)
    ones_b = sb("ones_b", [1, 128], BF16)
    mcol = sb("mcol", [128, 1], F32)
    nmcol = sb("nmcol", [128, 1], F32)
    tmpc = sb("tmpc", [128, 4], F32)
    scols = sb("scols", [128, 16], F32)
    SX, ISX, SG, ISG, SU, ISU, SGQ, ISB, ISCQ, SC = range(10)
    srow = sb("srow", [1, 16], F32)
    s1t = sb("s1t", [128, NT], F32)
    s2t = sb("s2t", [128, NT], F32)
    rstdt = sb("rstdt", [128, NT], F32)
    nmbt = sb("nmbt", [128, NT], F32)
    tmpstat = sb("tmpstat", [128, NT], F32)
    statx = sb("statx", [128, NT], F32)
    ustat = sb("ustat", [128, 2 * NI], F32)
    gstat = sb("gstat", [128, 2 * NI], F32)
    gstat2 = sb("gstat2", [128, 2 * NI], F32)
    qstat = sb("qstat", [128, NI], F32)
    padt = sb("padt", [128, 128], F32)
    mparts = sb("mparts", [128, 16], F32)
    zparts = sb("zparts", [128, 16], F32)
    nbias = sb("nbias", [128, 8], F32)
    zcol = sb("zcol", [128, 8], F32)
    rzcol = sb("rzcol", [128, 8], F32)

    x2t = [sb(f"x2t{i}", [128, C], F32) for i in range(2)]
    xlt = [sb(f"xlt{i}", [128, C], BF16) for i in range(2)]

    psl = lambda name: ctx.enter_context(nc.psum_tensor(name, [128, 512], F32))
    pA, pB, pC, pD, pE, pF = psl("pA"), psl("pB"), psl("pC"), psl("pD"), psl("pE"), psl("pF")
    pT = ctx.enter_context(nc.psum_tensor("pT", [128, 1024], BF16))
    pT2 = ctx.enter_context(nc.psum_tensor("pT2", [128, 1024], BF16))

    stk1 = contextlib.ExitStack()
    stk2a = contextlib.ExitStack()
    stk2b = contextlib.ExitStack()
    stk3 = contextlib.ExitStack()
    stk4o = contextlib.ExitStack()
    stk4a = contextlib.ExitStack()
    stk4b = contextlib.ExitStack()

    stk0 = contextlib.ExitStack()
    xlnT = sb("xlnT", [128, NCH * T], BF16, stk1, side="right")
    WvT = sb("WvT", [128, NCH * C], BF16, stk1, side="right")
    WbT = sb("WbT", [128, NCH * R], BF16, stk1, side="right")
    wtr = [sb(f"wtr{i}", [128, C], F32, stk0, side="right") for i in range(2)]
    wtb = [sb(f"wtb{i}", [128, C], BF16, stk0, side="right") for i in range(2)]
    wbraw = sb("wbraw", [64, C], F32, stk0, side="right")
    bv_f = sb("bv_f", [1, C], F32, stk0, side="right")
    bo_f = sb("bo_f", [1, C], F32, stk0, side="right")
    WoT = sb("WoT", [128, NCH * C], BF16, stk2a)
    xsq = [sb(f"xsq{i}", [128, C], F32, stk2a) for i in range(2)]
    LT = sb("LT", [64, T], BF16, stk2a)
    bo_bc = sb("bo_bc", [128, C], F32, stk2a)
    bv_b = sb("bv_b", [1, C], BF16, stk2a)

    class _RecInst:
        __slots__ = ("incs",)
        def __init__(self):
            self.incs = []
        def then_inc(self, sem, v):
            self.incs.append((sem, v))
            return self

    class _RecEngine:
        def __init__(self):
            self.ops = []
        def __getattr__(self, attr):
            ops = self.ops
            def call(*a, **k):
                inst = _RecInst()
                ops.append((attr, a, k, inst))
                return inst
            return call

    class _RecBlock:
        def __init__(self):
            self.eng = {n: _RecEngine() for n in
                        ("sync", "scalar", "vector", "tensor", "gpsimd")}
        def __getattr__(self, name):
            e = self.eng[name]
            return lambda fn: fn(e)

    block = _RecBlock()
    if True:
        mk = lambda name: Sem(ctx, nc, name)
        lc = mk("lc")
        lx = [mk("lx0"), mk("lx1")]
        lw = [mk("lw0"), mk("lw1")]
        lr = [mk("lr0"), mk("lr1")]
        lg = [mk("lg0"), mk("lg1")]
        sp = [mk("sp0"), mk("sp1")]
        lu = [mk("lu0"), mk("lu1")]
        wds = mk("wds")
        sq = [mk("sq0"), mk("sq1")]
        wl = mk("wl")
        so = [mk("so0"), mk("so1")]
        ts = mk("ts")
        vs = mk("vs")
        ss = mk("ss")
        gs = mk("gs")
        ptg = [0]
        ptgv = [0]
        pt2g = [0]

        def rnd(vector, dst, src):
            return vector.tensor_scalar(dst, src, mcol[:], nmcol[:], ALU.add, ALU.add)

        def _(sync):
            for dst, src in ((iden_f[:], iden_in),
                             (ones_f[:], ones_in), (bv_f[:], bv), (bo_f[:], bo),
                             (wbraw[:], Wb)):
                lc.inc(sync.dma_start(out=dst, in_=src), 16)
        block.sync(_)

        def _(vector):
            vector.wait_ge(lc.h, lc.n)
            def ch(inst):
                vs.inc(inst); vector.wait_ge(vs.h, vs.n)
            ch(vector.memset(tmpc[:, 0:1], 0.75))
            ch(vector.reciprocal(tmpc[:, 0:1], tmpc[:, 0:1]))
            ch(vector.tensor_scalar(tmpc[:, 0:1], tmpc[:, 0:1], -1.0, 3.0, ALU.add, ALU.mult))
            ch(vector.tensor_scalar(tmpc[:, 0:1], tmpc[:, 0:1], -1.0, None, ALU.add))
            ch(vector.tensor_scalar(tmpc[:, 1:2], tmpc[:, 0:1], -1.0, None, ALU.mult))
            ch(vector.tensor_tensor(tmpc[:, 0:1], tmpc[:, 0:1], tmpc[:, 1:2], ALU.max))
            ch(vector.reciprocal(tmpc[:, 0:1], tmpc[:, 0:1]))
            ch(vector.tensor_scalar(mcol[:], tmpc[:, 0:1], 1.5, None, ALU.mult))
            ch(vector.tensor_scalar(nmcol[:], mcol[:], -1.0, None, ALU.mult))
            vector.memset(zparts[:], 0.0)
            vector.memset(srow[:], 0.0)
            vector.tensor_copy(iden_b[:], iden_f[:])
            vector.tensor_copy(ones_b[:], ones_f[:])
            vector.tensor_copy(wtb[0][0:64, :], wbraw[:])
            vs.inc(vector.tensor_copy(bv_b[:], bv_f[:]))
        block.vector(_)
        P0_V = vs.n

        def _(tensor):
            tensor.wait_ge(vs.h, P0_V)
            tensor.matmul(pA[:], ones_f[:], bo_f[:, 0:512], start=True, stop=True)
            ts.inc(tensor.matmul(pB[:], ones_f[:], bo_f[:, 512:1024], start=True, stop=True))
        block.tensor(_)
        bo_ts = ts.n

        def _(vector):
            vector.wait_ge(ts.h, bo_ts)
            vector.tensor_copy(bo_bc[:, 0:512], pA[:])
            vs.inc(vector.tensor_copy(bo_bc[:, 512:1024], pB[:]))
        block.vector(_)
        bo_v = vs.n

        def _(tensor):
            for ci in range(NCH):
                inst = tensor.transpose(pT2[:, ci * R:(ci + 1) * R],
                                        wtb[0][0:64, ci * 128:(ci + 1) * 128],
                                        iden_b[0:64, 0:64])
            ts.inc(inst)
        block.tensor(_)
        wb_ts = ts.n

        def _(scalar):
            scalar.wait_ge(ts.h, wb_ts)
            ss.inc(scalar.copy(WbT[:], pT2[:, 0:NCH * R]))
            pt2g[0] = ss.n
        block.scalar(_)

        attn_cast = []
        attn_tr = [ts.n, ts.n]
        def prep_attn_weight(Wsrc, dst):
            cast_ms, tr_ms = attn_cast, attn_tr
            for rt in range(NCH):
                slot = (len(cast_ms)) % 2
                def _(sync, rt=rt, slot=slot):
                    if len(cast_ms) >= 2:
                        sync.wait_ge(vs.h, cast_ms[-2])
                    lw[slot].inc(sync.dma_start(out=wtr[slot][:],
                                                in_=Wsrc[rt * 128:(rt + 1) * 128, :]), 16)
                block.sync(_)
                def _(vector, rt=rt, slot=slot):
                    vector.wait_ge(lw[slot].h, lw[slot].n)
                    vector.wait_ge(ts.h, tr_ms[-2])
                    vs.inc(vector.tensor_copy(wtb[slot][:], wtr[slot][:]))
                    cast_ms.append(vs.n)
                block.vector(_)
                def _(tensor, rt=rt, slot=slot):
                    tensor.wait_ge(vs.h, cast_ms[-1])
                    tensor.wait_ge(ss.h, pt2g[0])
                    for ci in range(NCH):
                        inst = tensor.transpose(pT2[:, ci * 128:(ci + 1) * 128],
                                                wtb[slot][:, ci * 128:(ci + 1) * 128], iden_b[:])
                    ts.inc(inst)
                    tr_ms.append(ts.n)
                block.tensor(_)
                def _(scalar, rt=rt):
                    scalar.wait_ge(ts.h, tr_ms[-1])
                    for ci in range(NCH):
                        inst = scalar.copy(dst[:, ci * C + rt * 128: ci * C + rt * 128 + 128],
                                           pT2[:, ci * 128:(ci + 1) * 128])
                    ss.inc(inst)
                    pt2g[0] = ss.n
                block.scalar(_)

        prep_attn_weight(Wv, WvT)
        prep_attn_weight(Wo, WoT)
        P0_SS_END = ss.n
        stk0.close()

        red_ms, sq_ms = [], []
        for t in range(NT):
            slot = t % 2
            def _(sync, t=t, slot=slot):
                if t >= 2:
                    sync.wait_ge(vs.h, red_ms[t - 2])
                    sync.wait_ge(ss.h, sq_ms[t - 2])
                lx[slot].inc(sync.dma_start(out=x2t[slot][:],
                                            in_=x_in[t * 128:(t + 1) * 128, :]), 16)
            block.sync(_)
            def _(scalar, t=t, slot=slot):
                scalar.wait_ge(lx[slot].h, lx[slot].n)
                if t >= 2:
                    scalar.wait_ge(ss.h, sq_ms[t - 2])
                ss.inc(scalar.activation(xsq[slot][:], x2t[slot][:], ACTF.Square,
                                         accum_out=s2t[:, t:t + 1]))
                sq_ms.append(ss.n)
            block.scalar(_)
            def _(vector, t=t, slot=slot):
                vector.wait_ge(lx[slot].h, lx[slot].n)
                vs.inc(vector.reduce_sum(s1t[:, t:t + 1], x2t[slot][:], axis=AX.X))
                red_ms.append(vs.n)
            block.vector(_)

        def emit_ln_chain(pre_v):
            def _(vector):
                vector.wait_ge(vs.h, pre_v)
                def ch(inst):
                    vs.inc(inst); vector.wait_ge(vs.h, vs.n)
                invC = 1.0 / C
                ch(vector.tensor_scalar(s1t[:], s1t[:], invC, None, ALU.mult))
                ch(vector.tensor_tensor(tmpstat[:], s1t[:], s1t[:], ALU.mult))
                ch(vector.scalar_tensor_tensor(s2t[:], s2t[:], invC, tmpstat[:],
                                               ALU.mult, ALU.subtract))
                ch(vector.tensor_scalar(s2t[:], s2t[:], 1.0e-5, None, ALU.add))
            block.vector(_)
            mid_v = vs.n
            def _(scalar):
                scalar.wait_ge(vs.h, mid_v)
                ss.inc(scalar.activation(rstdt[:], s2t[:], ACTF.Sqrt))
            block.scalar(_)
            mid_s = ss.n
            def _(vector):
                vector.wait_ge(ss.h, mid_s)
                def ch(inst):
                    vs.inc(inst); vector.wait_ge(vs.h, vs.n)
                ch(vector.reciprocal(rstdt[:], rstdt[:]))
                ch(vector.tensor_tensor(tmpstat[:], s2t[:], rstdt[:], ALU.mult))
                ch(vector.tensor_tensor(tmpstat[:], tmpstat[:], rstdt[:], ALU.mult))
                ch(vector.tensor_scalar(tmpstat[:], tmpstat[:], -0.5, 1.5, ALU.mult, ALU.add))
                ch(vector.tensor_tensor(rstdt[:], rstdt[:], tmpstat[:], ALU.mult))
                ch(vector.tensor_tensor(nmbt[:], s1t[:], rstdt[:], ALU.mult))
                ch(vector.tensor_scalar(nmbt[:], nmbt[:], -1.0, None, ALU.mult))
            block.vector(_)
            return vs.n

        def _(vector):
            vector.wait_ge(ss.h, sq_ms[-1])
        block.vector(_)
        ln1_v = emit_ln_chain(red_ms[-1])

        norm_ms = []
        xl_tr = []
        for t in range(NT):
            slot = t % 2
            def _(sync, t=t, slot=slot):
                if t >= 2:
                    sync.wait_ge(ss.h, norm_ms[t - 2])
                else:
                    sync.wait_ge(vs.h, red_ms[NT - 2 + t])
                    sync.wait_ge(ss.h, sq_ms[NT - 2 + t])
                lx[slot].inc(sync.dma_start(out=x2t[slot][:],
                                            in_=x_in[t * 128:(t + 1) * 128, :]), 16)
            block.sync(_)
            def _(scalar, t=t, slot=slot):
                scalar.wait_ge(lx[slot].h, lx[slot].n)
                scalar.wait_ge(vs.h, ln1_v)
                if t >= 2:
                    scalar.wait_ge(ts.h, xl_tr[t - 2])
                ss.inc(scalar.activation(xlt[slot][:], x2t[slot][:], ACTF.Identity,
                                         bias=nmbt[:, t:t + 1], scale=rstdt[:, t:t + 1]))
                norm_ms.append(ss.n)
            block.scalar(_)
            def _(tensor, t=t, slot=slot):
                tensor.wait_ge(ss.h, norm_ms[t])
                tensor.wait_ge(ss.h, ptg[0])
                tensor.wait_ge(vs.h, ptgv[0])
                for ci in range(NCH):
                    inst = tensor.transpose(pT[:, ci * 128:(ci + 1) * 128],
                                            xlt[slot][:, ci * 128:(ci + 1) * 128], iden_b[:])
                ts.inc(inst)
                xl_tr.append(ts.n)
            block.tensor(_)
            trm = ts.n
            def _(scalar, t=t):
                scalar.wait_ge(ts.h, trm)
                for ci in range(NCH):
                    inst = scalar.copy(xlnT[:, ci * T + t * 128: ci * T + t * 128 + 128],
                                       pT[:, ci * 128:(ci + 1) * 128])
                ss.inc(inst)
                ptg[0] = ss.n
            block.scalar(_)
        P1_SS = ss.n

        lat_ms = []
        for f in range(NQ):
            bank = pA if f % 2 == 0 else pB
            def _(tensor, f=f, bank=bank):
                if f < 2:
                    tensor.wait_ge(ss.h, P1_SS)
                    tensor.wait_ge(vs.h, bo_v)
                else:
                    tensor.wait_ge(ss.h, lat_ms[f - 2])
                for ci in range(NCH):
                    inst = tensor.matmul(bank[0:R, :],
                                         WbT[:, ci * R:(ci + 1) * R],
                                         xlnT[:, ci * T + f * 512: ci * T + (f + 1) * 512],
                                         start=(ci == 0), stop=(ci == NCH - 1))
                ts.inc(inst)
            block.tensor(_)
            mm = ts.n
            def _(scalar, f=f, bank=bank):
                scalar.wait_ge(ts.h, mm)
                ss.inc(scalar.copy(LT[:, f * 512:(f + 1) * 512], bank[0:R, :]))
                lat_ms.append(ss.n)
            block.scalar(_)

        v_sb = sb("v_sb", [128, NT * C], BF16, stk2a)
        v_ms = []
        for t in range(NT):
            banks = (pC, pD) if t % 2 == 0 else (pE, pF)
            for h in range(2):
                bank = banks[h]
                def _(tensor, t=t, h=h, bank=bank):
                    if t >= 2:
                        tensor.wait_ge(ss.h, v_ms[(t - 2) * 2 + h])
                    for ci in range(NCH):
                        tensor.matmul(bank[:],
                                      xlnT[:, ci * T + t * 128: ci * T + t * 128 + 128],
                                      WvT[:, ci * C + h * 512: ci * C + (h + 1) * 512],
                                      start=(ci == 0), stop=False)
                    inst = tensor.matmul(bank[:], ones_b[:], bv_b[:, h * 512:(h + 1) * 512],
                                         start=False, stop=True)
                    ts.inc(inst)
                block.tensor(_)
                mm = ts.n
                def _(scalar, t=t, h=h, bank=bank):
                    scalar.wait_ge(ts.h, mm)
                    ss.inc(scalar.copy(v_sb[:, t * C + h * 512: t * C + (h + 1) * 512], bank[:]))
                    v_ms.append(ss.n)
                block.scalar(_)
        P2_SS = ss.n
        stk1.close()

        pTs = sb("pTs", [128, NT * 512], BF16, stk2b)
        p_row = sb("p_row", [128, T], BF16, stk2b)
        yT = sb("yT", [128, NCH * 512], BF16, stk2b)
        maskneg = sb("maskneg", [128, 128], F32, stk2b)
        smsk = sb("smsk", [128, 128], F32, stk2b)
        ymm_last = [0]
        ptz = [0]
        def _(sync):
            sync.wait_ge(ss.h, P2_SS)
            lc.inc(sync.dma_start(out=maskneg[:], in_=maskneg_in), 16)
        block.sync(_)
        MSK_LC = lc.n
        last_exp_ss = [P2_SS]
        last_yev = [P2_SS]
        epi_v = []
        epi_sq = []
        x2st_ms = [0, 0]

        def emit_pass1(qc, qt):
            qp = (qc % 2) * 4 + qt
            g = qc * 4 + qt
            k_hi = (g + 1) * 128
            nkc = (k_hi + 511) // 512
            sbank = (pA, pB)
            red1 = []
            for kc in range(nkc):
                N = min(512, k_hi - kc * 512)
                def _(tensor, kc=kc, N=N, g=g):
                    if kc >= 2:
                        tensor.wait_ge(vs.h, red1[kc - 2])
                    else:
                        tensor.wait_ge(vs.h, p1_mv[-1])
                    ts.inc(tensor.matmul(sbank[kc % 2][:, 0:N],
                                         LT[:, g * 128:(g + 1) * 128],
                                         LT[:, kc * 512: kc * 512 + N],
                                         start=True, stop=True))
                block.tensor(_)
                mm = ts.n
                def _(vector, kc=kc, N=N, nkc=nkc):
                    vector.wait_ge(ts.h, mm)
                    if kc == nkc - 1:
                        vector.wait_ge(lc.h, MSK_LC)
                        off = N - 128
                        vs.inc(vector.tensor_tensor(smsk[:], sbank[kc % 2][:, off:off + 128],
                                                    maskneg[:], ALU.add))
                        vector.wait_ge(vs.h, vs.n)
                        if off > 0:
                            vector.reduce_max(mparts[:, kc:kc + 1],
                                              sbank[kc % 2][:, 0:off], axis=AX.X)
                        else:
                            vector.memset(mparts[:, kc:kc + 1], NEGBIG)
                        vs.inc(vector.reduce_max(mparts[:, kc + 1:kc + 2], smsk[:], axis=AX.X))
                    else:
                        vs.inc(vector.reduce_max(mparts[:, kc:kc + 1],
                                                 sbank[kc % 2][:, 0:N], axis=AX.X))
                    red1.append(vs.n)
                block.vector(_)
            def _(vector, qt=qt, nkc=nkc, qp=qp):
                vector.wait_ge(vs.h, red1[-1])
                vs.inc(vector.reduce_max(tmpc[:, 2:3], mparts[:, 0:nkc + 1], axis=AX.X))
                vector.wait_ge(vs.h, vs.n)
                vs.inc(vector.tensor_scalar(nbias[:, qp:qp + 1], tmpc[:, 2:3],
                                            -0.125, None, ALU.mult))
            block.vector(_)
            p1_mv.append(vs.n)
            return vs.n

        def emit_pass2(qc, qt, m_v):
            qp = (qc % 2) * 4 + qt
            g = qc * 4 + qt
            k_hi = (g + 1) * 128
            nkc = (k_hi + 511) // 512
            sbank = (pD, pE)
            exp_ms = []
            for kc in range(nkc):
                N = min(512, k_hi - kc * 512)
                def _(tensor, kc=kc, N=N, g=g, m_v=m_v):
                    if kc >= 2:
                        tensor.wait_ge(ss.h, exp_ms[kc - 2])
                    else:
                        tensor.wait_ge(vs.h, m_v)
                        tensor.wait_ge(ss.h, last_exp_ss[0])
                        tensor.wait_ge(ss.h, last_yev[0])
                    ts.inc(tensor.matmul(sbank[kc % 2][:, 0:N],
                                         LT[:, g * 128:(g + 1) * 128],
                                         LT[:, kc * 512: kc * 512 + N],
                                         start=True, stop=True))
                block.tensor(_)
                mm2 = ts.n
                last = (kc == nkc - 1)
                if last:
                    off = N - 128
                    def _(vector, kc=kc, N=N, off=off, m_v=m_v):
                        vector.wait_ge(ts.h, mm2)
                        vector.wait_ge(vs.h, m_v)
                        if off == 0:
                            vector.memset(zparts[:, kc:kc + 1], 0.0)
                        vs.inc(vector.tensor_tensor(smsk2[:], sbank[kc % 2][:, off:off + 128],
                                                    maskneg[:], ALU.add))
                    block.vector(_)
                    mk_v = vs.n
                    def _(scalar, kc=kc, N=N, off=off, qp=qp):
                        scalar.wait_ge(vs.h, mk_v)
                        if off > 0:
                            scalar.activation(p_row[:, kc * 512: kc * 512 + off],
                                              sbank[kc % 2][:, 0:off], ACTF.Exp,
                                              bias=nbias[:, qp:qp + 1], scale=0.125,
                                              accum_out=zparts[:, kc:kc + 1])
                        ss.inc(scalar.activation(p_row[:, kc * 512 + off: kc * 512 + N],
                                                 smsk2[:], ACTF.Exp,
                                                 bias=nbias[:, qp:qp + 1], scale=0.125,
                                                 accum_out=zparts[:, kc + 1:kc + 2]))
                        exp_ms.append(ss.n)
                    block.scalar(_)
                else:
                    def _(scalar, kc=kc, N=N, qp=qp, m_v=m_v):
                        scalar.wait_ge(ts.h, mm2)
                        scalar.wait_ge(vs.h, m_v)
                        ss.inc(scalar.activation(p_row[:, kc * 512: kc * 512 + N],
                                                 sbank[kc % 2][:, 0:N], ACTF.Exp,
                                                 bias=nbias[:, qp:qp + 1], scale=0.125,
                                                 accum_out=zparts[:, kc:kc + 1]))
                        exp_ms.append(ss.n)
                    block.scalar(_)
            last_exp_ss[0] = exp_ms[-1]
            def _(vector, qp=qp, nkc=nkc):
                vector.wait_ge(ss.h, exp_ms[-1])
                vs.inc(vector.reduce_sum(zcol[:, qp:qp + 1], zparts[:, 0:nkc + 1], axis=AX.X))
                vector.wait_ge(vs.h, vs.n)
                vs.inc(vector.reciprocal(rzcol[:, qp:qp + 1], zcol[:, qp:qp + 1]))
            block.vector(_)
            return exp_ms[-1]

        def emit_transposes(qc, qt, exp_last):
            g = qc * 4 + qt
            k_hi = (g + 1) * 128
            nblk = k_hi // 128
            for b0 in range(0, nblk, 8):
                bn = min(8, nblk - b0)
                def _(tensor, b0=b0, bn=bn, exp_last=exp_last):
                    tensor.wait_ge(ss.h, exp_last)
                    tensor.wait_ge(ss.h, ptg[0])
                    tensor.wait_ge(vs.h, ptgv[0])
                    for j in range(bn):
                        inst = tensor.transpose(pT[:, j * 128:(j + 1) * 128],
                                                p_row[:, (b0 + j) * 128:(b0 + j + 1) * 128],
                                                iden_b[:])
                    ts.inc(inst)
                block.tensor(_)
                trm = ts.n
                def _(scalar, b0=b0, bn=bn, qt=qt):
                    scalar.wait_ge(ts.h, trm)
                    for j in range(bn):
                        kb = b0 + j
                        inst = scalar.copy(
                            pTs[:, kb * 512 + qt * 128: kb * 512 + qt * 128 + 128],
                            pT[:, j * 128:(j + 1) * 128])
                    ss.inc(inst)
                    ptg[0] = ss.n
                block.scalar(_)

        p1_mv = [0]
        smsk2 = sb("smsk2", [128, 128], F32, stk2b)

        def emit_sm_memset(qc):
            def _(vector, qc=qc):
                vector.wait_ge(ts.h, ymm_last[0])
                if qc == 0:
                    vector.wait_ge(ss.h, P2_SS)
                inst = None
                for j in range(1, 4):
                    kb = qc * 4 + j
                    inst = vector.memset(pTs[:, kb * 512: kb * 512 + j * 128], 0.0)
                if inst is not None:
                    vs.inc(inst)
                    ptz[0] = vs.n
            block.vector(_)

        def emit_softmax_chunk(qc):
            emit_sm_memset(qc)
            mv_next = emit_pass1(qc, 0)
            for qt in range(4):
                e_last = emit_pass2(qc, qt, mv_next)
                if qt < 3:
                    mv_next = emit_pass1(qc, qt + 1)
                emit_transposes(qc, qt, e_last)

        def emit_yacc_group(qc, d, yev_ms, ptr_ss):
            nkb = (qc + 1) * 4
            bank = pC if d % 2 == 0 else pF
            def _(tensor, d=d, bank=bank, nkb=nkb, ptr_ss=ptr_ss):
                tensor.wait_ge(ss.h, ptr_ss)
                tensor.wait_ge(vs.h, ptz[0])
                if d >= 2:
                    tensor.wait_ge(ss.h, yev_ms[d - 2])
                elif epi_v:
                    tensor.wait_ge(vs.h, epi_v[-1])
                for kb in range(nkb):
                    inst = tensor.matmul(bank[:],
                                         v_sb[:, kb * C + d * 128: kb * C + d * 128 + 128],
                                         pTs[:, kb * 512:(kb + 1) * 512],
                                         start=(kb == 0), stop=(kb == nkb - 1))
                ts.inc(inst)
                if d == NCH - 1:
                    ymm_last[0] = ts.n
            block.tensor(_)
            ym = ts.n
            def _(scalar, d=d, bank=bank):
                scalar.wait_ge(ts.h, ym)
                ss.inc(scalar.copy(yT[:, d * 512:(d + 1) * 512], bank[:]))
                yev_ms.append(ss.n)
            block.scalar(_)
            if d == NCH - 1:
                last_yev[0] = yev_ms[-1]

        def emit_oproj_qt(qc, qt, yev_ms):
            if True:
                g = qc * 4 + qt
                slot = g % 2
                def _(sync, g=g, slot=slot):
                    if len(epi_sq) >= 2:
                        sync.wait_ge(ss.h, epi_sq[-2])
                        sync.wait_ge(sp[slot].h, x2st_ms[slot])
                    else:
                        sync.wait_ge(ss.h, norm_ms[NT - 2 + g])
                    lx[slot].inc(sync.dma_start(out=x2t[slot][:],
                                                in_=x_in[g * 128:(g + 1) * 128, :]), 16)
                block.sync(_)
                obanks = (pC, pF)
                def _(tensor, qt=qt):
                    tensor.wait_ge(ss.h, yev_ms[-1])
                    if epi_v:
                        tensor.wait_ge(vs.h, epi_v[-1])
                    for h in range(2):
                        for d in range(NCH):
                            inst = tensor.matmul(
                                obanks[h][:],
                                yT[:, d * 512 + qt * 128: d * 512 + qt * 128 + 128],
                                WoT[:, d * C + h * 512: d * C + (h + 1) * 512],
                                start=(d == 0), stop=(d == NCH - 1))
                    ts.inc(inst)
                block.tensor(_)
                om = ts.n
                def _(vector, qt=qt, g=g, slot=slot, qc=qc):
                    qp = (qc % 2) * 4 + qt
                    vector.wait_ge(ts.h, om)
                    vector.wait_ge(lx[slot].h, lx[slot].n)
                    def ch(inst):
                        vs.inc(inst); vector.wait_ge(vs.h, vs.n)
                    ch(vector.tensor_tensor(x2t[slot][:], x2t[slot][:], bo_bc[:], ALU.add))
                    ch(vector.scalar_tensor_tensor(x2t[slot][:, 0:512], obanks[0][:],
                                                   rzcol[:, qp:qp + 1], x2t[slot][:, 0:512],
                                                   ALU.mult, ALU.add))
                    ch(vector.scalar_tensor_tensor(x2t[slot][:, 512:1024], obanks[1][:],
                                                   rzcol[:, qp:qp + 1], x2t[slot][:, 512:1024],
                                                   ALU.mult, ALU.add))
                    vs.inc(vector.reduce_sum(s1t[:, g:g + 1], x2t[slot][:], axis=AX.X))
                    epi_v.append(vs.n)
                block.vector(_)
                def _(scalar, g=g, slot=slot):
                    scalar.wait_ge(vs.h, epi_v[-1])
                    if len(epi_sq) >= 2:
                        scalar.wait_ge(ss.h, epi_sq[-2])
                    elif sq_ms:
                        scalar.wait_ge(ss.h, sq_ms[-2 + len(epi_sq)])
                    ss.inc(scalar.activation(xsq[slot][:], x2t[slot][:], ACTF.Square,
                                             accum_out=s2t[:, g:g + 1]))
                    epi_sq.append(ss.n)
                block.scalar(_)
                def _(vector, g=g, slot=slot):
                    def ch(inst):
                        vs.inc(inst); vector.wait_ge(vs.h, vs.n)
                    vector.wait_ge(vs.h, vs.n)
                    ch(vector.tensor_scalar(tmpc[:, 3:4], s1t[:, g:g + 1], 1.0 / C,
                                            None, ALU.mult))
                    ch(vector.tensor_scalar(tmpc[:, 3:4], tmpc[:, 3:4], -1.0, None, ALU.mult))
                    vector.wait_ge(ss.h, epi_sq[-1])
                    ch(vector.tensor_scalar(xsq[slot][:], x2t[slot][:], tmpc[:, 3:4],
                                            None, ALU.add))
                    vs.inc(vector.reduce_max(statx[:, g:g + 1], xsq[slot][:], axis=AX.X,
                                             apply_absolute_value=True))
                    epi_v.append(vs.n)
                block.vector(_)
                def _(gpsimd, g=g, slot=slot):
                    gpsimd.wait_ge(vs.h, epi_v[-1])
                    gpsimd.wait_ge(ss.h, epi_sq[-1])
                    sp[slot].inc(gpsimd.dma_start(out=x2_hbm[g * 128:(g + 1) * 128, :],
                                                  in_=x2t[slot][:]), 16)
                    x2st_ms[slot] = sp[slot].n
                block.gpsimd(_)

        stk4w = contextlib.ExitStack()
        wrd = [sb("wrd0", [128, I], F32, stk4w)] * 2
        wqd = sb("wqd", [128, I], BF16, stk4w)
        blkd = sb("blkd", [128, 96], F32, stk4w)
        bind = sb("bind", [128, 96], F32, stk4w)
        wstg = sb("wstg", [128, I], BF16, stk4w)
        P3_TS0 = ts.n
        wd_cast, wd_tr = [], []
        def emit_wd_head(rt):
            slot = rt % 2
            def _(sync, rt=rt, slot=slot):
                if wd_cast:
                    sync.wait_ge(vs.h, wd_cast[-1])
                else:
                    sync.wait_ge(ts.h, P3_TS0)
                lw[slot].inc(sync.dma_start(out=wrd[slot][:],
                                            in_=Wd[rt * 128:(rt + 1) * 128, :]), 16)
            block.sync(_)
            def _(vector, rt=rt, slot=slot):
                vector.wait_ge(lw[slot].h, lw[slot].n)
                if wd_tr:
                    vector.wait_ge(ts.h, wd_tr[-1])
                def ch(inst):
                    vs.inc(inst); vector.wait_ge(vs.h, vs.n)
                w3 = wrd[slot][:].rearrange("p (b k) -> p b k", k=32)
                ch(vector.reduce_max(blkd[:], w3, axis=AX.X, apply_absolute_value=True))
                ch(vector.tensor_scalar(blkd[:], blkd[:], INV_WQMAX, 1.0e-12,
                                        ALU.mult, ALU.max))
                ch(vector.reciprocal(bind[:], blkd[:]))
                bmr = bind[:].unsqueeze(2).broadcast_to([128, 96, 32])
                ch(vector.tensor_tensor(w3, w3, bmr, ALU.mult))
                ch(rnd(vector, wrd[slot][:], wrd[slot][:]))
                bms = blkd[:].unsqueeze(2).broadcast_to([128, 96, 32])
                ch(vector.tensor_tensor(wqd[:].rearrange("p (b k) -> p b k", k=32),
                                        w3, bms, ALU.mult))
                wd_cast.append(vs.n)
            block.vector(_)
        def emit_wd_tr(rt):
            for hb in range(3):
                def _(tensor, rt=rt, hb=hb):
                    tensor.wait_ge(vs.h, wd_cast[rt])
                    tensor.wait_ge(ss.h, pt2g[0])
                    for j in range(8):
                        ii = hb * 8 + j
                        inst = tensor.transpose(pT2[:, j * 128:(j + 1) * 128],
                                                wqd[:, ii * 128:(ii + 1) * 128], iden_b[:])
                    ts.inc(inst)
                    wd_tr.append(ts.n)
                block.tensor(_)
                def _(scalar, rt=rt, hb=hb):
                    scalar.wait_ge(ts.h, wd_tr[-1])
                    if hb == 0:
                        scalar.wait_ge(wds.h, 16 * rt)
                    for j in range(8):
                        ii = hb * 8 + j
                        inst = scalar.copy(wstg[:, ii * 128:(ii + 1) * 128],
                                           pT2[:, j * 128:(j + 1) * 128])
                    ss.inc(inst)
                    pt2g[0] = ss.n
                block.scalar(_)
            def _(sync, rt=rt):
                sync.wait_ge(ss.h, pt2g[0])
                wds.inc(sync.dma_start(
                    out=WdnT_hbm.rearrange("p (ii c) -> p ii c", c=C)[:, :,
                                                                     rt * 128:rt * 128 + 128],
                    in_=wstg[:].rearrange("p (ii c) -> p ii c", c=128)), 16)
            block.sync(_)

        emit_softmax_chunk(0)
        wd_emitted = 0
        for qc in range(NQ):
            nxt = qc + 1 < NQ
            ptr_ss = ss.n
            yev = []
            if nxt:
                emit_sm_memset(qc + 1)
                mv = emit_pass1(qc + 1, 0)
            for d in range(4):
                emit_yacc_group(qc, d, yev, ptr_ss)
            if nxt:
                e_last = emit_pass2(qc + 1, 0, mv)
            for d in range(4, 8):
                emit_yacc_group(qc, d, yev, ptr_ss)
            if nxt:
                mv = emit_pass1(qc + 1, 1)
                emit_transposes(qc + 1, 0, e_last)
            for qt in range(4):
                emit_oproj_qt(qc, qt, yev)
                if nxt and qt < 3:
                    e_last = emit_pass2(qc + 1, qt + 1, mv)
                    if qt < 2:
                        mv = emit_pass1(qc + 1, qt + 2)
                    emit_transposes(qc + 1, qt + 1, e_last)
            if wd_emitted < 8:
                if wd_emitted >= 1:
                    emit_wd_tr(wd_emitted - 1)
                emit_wd_head(wd_emitted)
                wd_emitted += 1
        while wd_emitted < 8:
            if wd_emitted >= 1:
                emit_wd_tr(wd_emitted - 1)
            emit_wd_head(wd_emitted)
            wd_emitted += 1
        emit_wd_tr(7)
        stk4w.close()
        X2_SP = (sp[0].n, sp[1].n)
        stk2b.close()
        stk2a.close()

        def _(vector):
            vector.wait_ge(ss.h, epi_sq[-1])
        block.vector(_)
        ln2_v = emit_ln_chain(epi_v[-1])

        def _(vector):
            vector.wait_ge(vs.h, ln2_v)
            vs.inc(vector.tensor_tensor(statx[:], statx[:], rstdt[:], ALU.mult))
        block.vector(_)
        amax_v = vs.n
        pa_done = ss.n

        def fold_stat(stat_ap, col, pre_v, use_abs=True):
            def _(vector):
                if pre_v is not None:
                    vector.wait_ge(vs.h, pre_v)
                vs.inc(vector.memset(padt[:], NEGBIG if not use_abs else 0.0))
                vector.wait_ge(vs.h, vs.n)
                vs.inc(vector.reduce_max(padt[:, 0:1], stat_ap, axis=AX.X,
                                         apply_absolute_value=use_abs))
            block.vector(_)
            fv = vs.n
            def _(tensor):
                tensor.wait_ge(vs.h, fv)
                ts.inc(tensor.transpose(pA[:, 0:128], padt[:], iden_f[:]))
            block.tensor(_)
            ft = ts.n
            def _(vector):
                vector.wait_ge(ts.h, ft)
                vs.inc(vector.reduce_max(srow[0:1, col:col + 1], pA[0:1, 0:128], axis=AX.X))
            block.vector(_)
            return vs.n

        def barrier(tag, pre_v):
            def _(sync):
                sync.wait_ge(vs.h, pre_v)
                lc.inc(sync.dma_start(out=cc_in[tag], in_=srow[0:1, 0:8]), 16)
            block.sync(_)
            ldn = lc.n
            def _(gpsimd):
                gpsimd.wait_ge(lc.h, ldn)
                gs.inc(gpsimd.collective_compute(
                    "AllReduce", ALU.max, replica_groups=[list(range(8))],
                    ins=[cc_in[tag]], outs=[cc_out[tag]]))
            block.gpsimd(_)
            gn = gs.n
            def _(sync):
                sync.wait_ge(gs.h, gn)
                lc.inc(sync.dma_start(out=srow[0:1, 0:8], in_=cc_out[tag]), 16)
            block.sync(_)
            ldn2 = lc.n
            def _(vector):
                vector.wait_ge(lc.h, ldn2)
                vs.inc(vector.nop())
            block.vector(_)
            return vs.n

        def bcast_scols(pairs, pre_v, pre_extra=None):
            def _(tensor):
                tensor.wait_ge(vs.h, pre_v)
                if pre_extra is not None:
                    tensor.wait_ge(vs.h, pre_extra)
                ts.inc(tensor.matmul(pB[:, 0:16], ones_f[:], srow[:],
                                     start=True, stop=True))
            block.tensor(_)
            bt = ts.n
            def _(vector):
                vector.wait_ge(ts.h, bt)
                for (rc, cc_) in pairs:
                    inst = vector.tensor_copy(scols[:, cc_:cc_ + 1], pB[:, rc:rc + 1])
                vs.inc(inst)
            block.vector(_)
            return vs.n

        fx = fold_stat(statx[:, 0:NT], 0, amax_v)
        bx = barrier("X", fx)
        def _(vector):
            vector.wait_ge(vs.h, bx)
            def ch(inst):
                vs.inc(inst); vector.wait_ge(vs.h, vs.n)
            ch(vector.tensor_scalar(srow[0:1, 1:2], srow[0:1, 0:1], INV_QMAX, 1.0e-12,
                                    ALU.mult, ALU.max))
            ch(vector.reciprocal(srow[0:1, 2:3], srow[0:1, 1:2]))
        block.vector(_)
        bcx = bcast_scols([(1, SX), (2, ISX)], vs.n)
        rsdx = sb("rsdx", [128, NT], F32)
        nmbx = sb("nmbx", [128, NT], F32)
        def _(vector):
            vector.wait_ge(vs.h, bcx)
            def ch(inst):
                vs.inc(inst); vector.wait_ge(vs.h, vs.n)
            ch(vector.tensor_scalar(rsdx[:], rstdt[:], scols[:, ISX:ISX + 1], None, ALU.mult))
            ch(vector.tensor_scalar(nmbx[:], nmbt[:], scols[:, ISX:ISX + 1], None, ALU.mult))
        block.vector(_)
        bcx2 = vs.n

        xqT = sb("xqT", [128, NCH * T], BF16, stk3)
        wkt = [sb(f"wkt{i}", [128, C], F32, stk3) for i in range(2)]
        WupT = sb("WupT", [128, NCH * I], BF16, stk3)
        WgT = sb("WgT", [128, NCH * I], BF16, stk3)
        wrs = [sb(f"wrs{i}", [128, C], F32, stk3) for i in range(2)]
        wqb = [sb(f"wqb{i}", [128, C], BF16, stk3) for i in range(2)]
        blkmx = [sb(f"blkmx{i}", [128, 32], F32, stk3) for i in range(2)]
        binx = sb("binx", [128, 32], F32, stk3)

        wp_cast, wp_tr = [], []
        wp_i = [0]
        def emit_wprep_tile(Wsrc, dstT, rt):
            slot = wp_i[0] % 2
            wp_i[0] += 1
            def _(sync, rt=rt, slot=slot):
                if len(wp_cast) >= 2:
                    sync.wait_ge(vs.h, wp_cast[-2])
                elif not wp_cast:
                    sync.wait_ge(vs.h, bcx)
                    sync.wait_ge(wds.h, 128)
                lw[slot].inc(sync.dma_start(out=wrs[slot][:],
                                            in_=Wsrc[rt * 128:(rt + 1) * 128, :]), 16)
            block.sync(_)
            def _(vector, rt=rt, slot=slot):
                vector.wait_ge(lw[slot].h, lw[slot].n)
                if len(wp_tr) >= 2:
                    vector.wait_ge(ts.h, wp_tr[-2])
                def ch(inst):
                    vs.inc(inst); vector.wait_ge(vs.h, vs.n)
                w3 = wrs[slot][:].rearrange("p (b k) -> p b k", k=32)
                ch(vector.reduce_max(blkmx[slot][:], w3, axis=AX.X, apply_absolute_value=True))
                ch(vector.tensor_scalar(blkmx[slot][:], blkmx[slot][:], INV_WQMAX, 1.0e-12,
                                        ALU.mult, ALU.max))
                ch(vector.reciprocal(binx[:], blkmx[slot][:]))
                bmr = binx[:].unsqueeze(2).broadcast_to([128, 32, 32])
                ch(vector.tensor_tensor(w3, w3, bmr, ALU.mult))
                ch(rnd(vector, wrs[slot][:], wrs[slot][:]))
                bms = blkmx[slot][:].unsqueeze(2).broadcast_to([128, 32, 32])
                ch(vector.tensor_tensor(wqb[slot][:].rearrange("p (b k) -> p b k", k=32),
                                        w3, bms, ALU.mult))
                wp_cast.append(vs.n)
            block.vector(_)
            def _(tensor, rt=rt, slot=slot):
                tensor.wait_ge(vs.h, wp_cast[-1])
                tensor.wait_ge(ss.h, pt2g[0])
                for ci in range(NCH):
                    inst = tensor.transpose(pT2[:, ci * 128:(ci + 1) * 128],
                                            wqb[slot][:, ci * 128:(ci + 1) * 128], iden_b[:])
                ts.inc(inst)
                wp_tr.append(ts.n)
            block.tensor(_)
            def _(scalar, rt=rt, first=(wp_i[0] == 1)):
                scalar.wait_ge(ts.h, wp_tr[-1])
                if first:
                    scalar.wait_ge(wds.h, 128)
                for ci in range(NCH):
                    inst = scalar.copy(dstT[:, ci * I + rt * 128: ci * I + rt * 128 + 128],
                                       pT2[:, ci * 128:(ci + 1) * 128])
                ss.inc(inst)
                pt2g[0] = ss.n
            block.scalar(_)

        qdq_ev = []
        wtiles = [(Wu, WupT, rt) for rt in range(NI)] + [(Wg, WgT, rt) for rt in range(NI)]
        wi = 0
        for t in range(NT):
            slot = t % 2
            def _(sync, t=t, slot=slot):
                if t >= 2:
                    sync.wait_ge(ss.h, qdq_ev[-1])
                    sync.wait_ge(ss.h, ptg[0])
                else:
                    sync.wait_ge(sp[slot].h, X2_SP[slot])
                lx[slot].inc(sync.dma_start(out=x2t[slot][:],
                                            in_=x2_hbm[t * 128:(t + 1) * 128, :]), 16)
            block.sync(_)
            def _(scalar, t=t, slot=slot):
                scalar.wait_ge(lx[slot].h, lx[slot].n)
                scalar.wait_ge(vs.h, bcx2)
                if len(qdq_ev) >= 2:
                    scalar.wait_ge(ss.h, qdq_ev[-2])
                ss.inc(scalar.activation(wkt[slot][:], x2t[slot][:], ACTF.Identity,
                                         bias=nmbx[:, t:t + 1], scale=rsdx[:, t:t + 1]))
            block.scalar(_)
            y_s = ss.n
            def _(vector, t=t, slot=slot):
                vector.wait_ge(ss.h, y_s)
                vs.inc(rnd(vector, wkt[slot][:], wkt[slot][:]))
            block.vector(_)
            k_v = vs.n
            def _(scalar, t=t, slot=slot):
                scalar.wait_ge(vs.h, k_v)
                ss.inc(scalar.activation(xlt[slot][:], wkt[slot][:], ACTF.Copy,
                                         scale=scols[:, SX:SX + 1]))
            block.scalar(_)
            xq_s = ss.n
            def _(tensor, t=t, slot=slot):
                tensor.wait_ge(ss.h, xq_s)
                tensor.wait_ge(ss.h, ptg[0])
                tensor.wait_ge(vs.h, ptgv[0])
                for ci in range(NCH):
                    inst = tensor.transpose(pT[:, ci * 128:(ci + 1) * 128],
                                            xlt[slot][:, ci * 128:(ci + 1) * 128], iden_b[:])
                ts.inc(inst)
            block.tensor(_)
            trm = ts.n
            def _(scalar, t=t):
                scalar.wait_ge(ts.h, trm)
                if t == 0:
                    scalar.wait_ge(wds.h, 128)
                for ci in range(NCH):
                    inst = scalar.copy(xqT[:, ci * T + t * 128: ci * T + t * 128 + 128],
                                       pT[:, ci * 128:(ci + 1) * 128])
                ss.inc(inst)
                ptg[0] = ss.n
            block.scalar(_)
            qdq_ev.append(ss.n)
            nw = 2 if (t % 2 == 0) else 1
            for _k in range(nw):
                if wi < len(wtiles):
                    Ws, dT, rt = wtiles[wi]
                    emit_wprep_tile(Ws, dT, rt)
                    wi += 1
        while wi < len(wtiles):
            Ws, dT, rt = wtiles[wi]
            emit_wprep_tile(Ws, dT, rt)
            wi += 1
        P4_done = ss.n
        P4_wv = vs.n

        ustg = [sb("ustg0", [128, T], BF16, stk3)] * 2
        f1_ev = []
        f1_st = [0, 0]
        f1_stat_v = []

        f1_lo = []
        f1_hi = []
        def emit_big_mm(dstT_hbm, WT, stat, first, stat2=None):
            H = T // 2
            for ic in range(NI):
                slot = ic % 2
                banks = (pA, pB) if ic % 2 == 0 else (pC, pD)
                ev_here = []
                for f in range(NQ):
                    bank = banks[f % 2]
                    def _(tensor, ic=ic, f=f, bank=bank):
                        if first and ic == 0 and f == 0:
                            tensor.wait_ge(ss.h, P4_done)
                            tensor.wait_ge(vs.h, bcx)
                        if f >= 2:
                            tensor.wait_ge(ss.h, ev_here[f - 2])
                        elif f1_ev:
                            tensor.wait_ge(ss.h, f1_ev[-1])
                        for ci in range(NCH):
                            inst = tensor.matmul(
                                bank[:],
                                WT[:, ci * I + ic * 128: ci * I + ic * 128 + 128],
                                xqT[:, ci * T + f * 512: ci * T + (f + 1) * 512],
                                start=(ci == 0), stop=(ci == NCH - 1))
                        ts.inc(inst)
                    block.tensor(_)
                    mm = ts.n
                    def _(scalar, ic=ic, f=f, bank=bank, slot=slot):
                        scalar.wait_ge(ts.h, mm)
                        if f == 0 and NQ >= 2 and f1_lo:
                            psl, pv, pvs = f1_lo[-1]
                            scalar.wait_ge(sq[psl].h, pv)
                            scalar.wait_ge(vs.h, pvs)
                        if f == 0 and NQ < 2 and f1_hi:
                            psl, pv, pvs = f1_hi[-1]
                            scalar.wait_ge(sp[psl].h, pv)
                            scalar.wait_ge(vs.h, pvs)
                        if NQ >= 2 and f == NQ // 2 and f1_hi:
                            psl, pv, pvs = f1_hi[-1]
                            scalar.wait_ge(sp[psl].h, pv)
                            scalar.wait_ge(vs.h, pvs)
                        ss.inc(scalar.copy(ustg[slot][:, f * 512:(f + 1) * 512], bank[:]))
                        ev_here.append(ss.n)
                    block.scalar(_)
                    if NQ >= 2 and f == NQ // 2 - 1:
                        def _(vector, ic=ic, slot=slot):
                            vector.wait_ge(ss.h, ev_here[-1])
                            if stat2 is not None:
                                vector.reduce_max(stat2[:, 2 * ic:2 * ic + 1],
                                                  ustg[slot][:, 0:H], axis=AX.X)
                            vs.inc(vector.reduce_max(stat[:, 2 * ic:2 * ic + 1],
                                                     ustg[slot][:, 0:H], axis=AX.X,
                                                     apply_absolute_value=True))
                        block.vector(_)
                        vlo = vs.n
                        def _(gpsimd, ic=ic, slot=slot, evv=ev_here[-1]):
                            gpsimd.wait_ge(ss.h, evv)
                            sq[slot].inc(gpsimd.dma_start(
                                out=dstT_hbm[ic * 128:(ic + 1) * 128, 0:H],
                                in_=ustg[slot][:, 0:H]), 16)
                        block.gpsimd(_)
                        f1_lo.append((slot, sq[slot].n, vlo))
                f1_ev.append(ev_here[-1])
                lo = 0 if NQ >= 2 else None
                def _(vector, ic=ic, slot=slot, lo=lo):
                    vector.wait_ge(ss.h, ev_here[-1])
                    if lo is not None:
                        src = ustg[slot][:, H:T]
                        c0 = 2 * ic + 1
                    else:
                        src = ustg[slot][:]
                        c0 = 2 * ic
                    if stat2 is not None:
                        vector.reduce_max(stat2[:, c0:c0 + 1], src, axis=AX.X)
                        if lo is None:
                            vector.reduce_max(stat2[:, c0 + 1:c0 + 2], src, axis=AX.X)
                    if lo is None:
                        vector.reduce_max(stat[:, c0 + 1:c0 + 2], src, axis=AX.X,
                                          apply_absolute_value=True)
                    vs.inc(vector.reduce_max(stat[:, c0:c0 + 1], src, axis=AX.X,
                                             apply_absolute_value=True))
                    f1_stat_v.append(vs.n)
                block.vector(_)
                def _(gpsimd, ic=ic, slot=slot, lo=lo):
                    gpsimd.wait_ge(ss.h, ev_here[-1])
                    if lo is not None:
                        sp[slot].inc(gpsimd.dma_start(
                            out=dstT_hbm[ic * 128:(ic + 1) * 128, H:T],
                            in_=ustg[slot][:, H:T]), 16)
                    else:
                        sp[slot].inc(gpsimd.dma_start(
                            out=dstT_hbm[ic * 128:(ic + 1) * 128, :],
                            in_=ustg[slot][:]), 16)
                    f1_st[slot] = sp[slot].n
                block.gpsimd(_)
                f1_hi.append((slot, sp[slot].n, f1_stat_v[-1]))

        emit_big_mm(U_hbm, WupT, ustat, True)
        emit_big_mm(G_hbm, WgT, gstat, False, stat2=gstat2)
        F1_V = vs.n
        F1_SP = (sp[0].n, sp[1].n)
        F1_SQ = (sq[0].n, sq[1].n)

        fa1 = fold_stat(ustat[:, 0:2 * NI], 0, F1_V)
        fa2 = fold_stat(gstat[:, 0:2 * NI], 1, None)
        fa3 = fold_stat(gstat2[:, 0:2 * NI], 2, None, use_abs=False)
        ba = barrier("A", fa3)
        def _(vector):
            vector.wait_ge(vs.h, ba)
            def ch(inst):
                vs.inc(inst); vector.wait_ge(vs.h, vs.n)
            ch(vector.tensor_scalar(srow[0:1, 8:9], srow[0:1, 0:1], INV_QMAX, 1.0e-12,
                                    ALU.mult, ALU.max))
            ch(vector.reciprocal(srow[0:1, 9:10], srow[0:1, 8:9]))
            ch(vector.tensor_scalar(srow[0:1, 10:11], srow[0:1, 1:2], INV_QMAX, 1.0e-12,
                                    ALU.mult, ALU.max))
            ch(vector.reciprocal(srow[0:1, 11:12], srow[0:1, 10:11]))
            ch(vector.tensor_scalar(srow[0:1, 12:13], srow[0:1, 10:11], 1.0 / 65536.0,
                                    None, ALU.mult))
            ch(vector.tensor_tensor(srow[0:1, 3:4], srow[0:1, 2:3], srow[0:1, 11:12],
                                    ALU.mult))
            ch(vector.tensor_scalar(srow[0:1, 3:4], srow[0:1, 3:4], mcol[0:1, :],
                                    nmcol[0:1, :], ALU.add, ALU.add))
            ch(vector.tensor_tensor(srow[0:1, 3:4], srow[0:1, 3:4], srow[0:1, 10:11],
                                    ALU.mult))
        block.vector(_)
        smA = vs.n
        def _(scalar):
            scalar.wait_ge(vs.h, smA)
            ss.inc(scalar.activation(srow[0:1, 6:7], srow[0:1, 3:4], ACTF.Sigmoid))
        block.scalar(_)
        smAs = ss.n
        def _(vector):
            vector.wait_ge(ss.h, smAs)
            def ch(inst):
                vs.inc(inst); vector.wait_ge(vs.h, vs.n)
            ch(vector.tensor_scalar(srow[0:1, 6:7], srow[0:1, 6:7], 65536.0, U16_CAP,
                                    ALU.mult, ALU.min))
            ch(vector.tensor_scalar(srow[0:1, 6:7], srow[0:1, 6:7], mcol[0:1, :],
                                    nmcol[0:1, :], ALU.add, ALU.add))
            ch(vector.tensor_tensor(srow[0:1, 6:7], srow[0:1, 6:7], srow[0:1, 3:4],
                                    ALU.mult))
            ch(vector.tensor_scalar(srow[0:1, 6:7], srow[0:1, 6:7], 1.0 / 65536.0,
                                    None, ALU.mult))
            ch(vector.tensor_scalar(srow[0:1, 4:5], srow[0:1, 6:7], INV_QMAX, 1.0e-12,
                                    ALU.mult, ALU.max))
            ch(vector.reciprocal(srow[0:1, 5:6], srow[0:1, 4:5]))
            ch(vector.tensor_tensor(srow[0:1, 7:8], srow[0:1, 12:13], srow[0:1, 5:6],
                                    ALU.mult))
        block.vector(_)
        bca = bcast_scols([(8, SU), (9, ISU), (10, SG), (11, ISG), (12, SGQ),
                           (7, ISB)], vs.n)

        stk3.close()

        WdnT = sb("WdnT", [128, NI * C], BF16, stk4o)
        def _(sync):
            sync.wait_ge(sp[0].h, F1_SP[0])
            sync.wait_ge(sp[1].h, F1_SP[1])
            sync.wait_ge(sq[0].h, F1_SQ[0])
            sync.wait_ge(sq[1].h, F1_SQ[1])
            sync.wait_ge(wds.h, 16 * 8)
            for q in range(4):
                wl.inc(sync.dma_start(
                    out=WdnT[:, q * (NI * C // 4):(q + 1) * (NI * C // 4)],
                    in_=WdnT_hbm[:, q * (NI * C // 4):(q + 1) * (NI * C // 4)]), 16)
        block.sync(_)
        gbuf = [sb("gbuf0", [128, T], BF16, stk4a)] * 2
        ubuf = [sb(f"ubuf{i}", [128, T], BF16, stk4a) for i in range(2)]
        A1 = [sb(f"A1{i}", [128, T], F32, stk4a) for i in range(3)]
        A2 = [sb(f"A2{i}", [128, T], F32, stk4a) for i in range(2)]
        A3 = [sb(f"A3{i}", [128, T], F32, stk4a) for i in range(2)]
        f23_v, f23_pp, f23_yg, f23_sg, f23_yu = [], [], [], [], []
        qq_v = {}
        spq = {}
        def emit_t1(ic):
            slot = ic % 2
            s3 = ic % 3
            def _(vector, ic=ic, slot=slot, s3=s3):
                vector.wait_ge(ss.h, f23_sg[ic])
                def ch(inst):
                    vs.inc(inst); vector.wait_ge(vs.h, vs.n)
                ch(vector.tensor_scalar(A1[s3][:], A1[s3][:], 65536.0, U16_CAP,
                                        ALU.mult, ALU.min))
                ch(rnd(vector, A1[s3][:], A1[s3][:]))
                if ic >= 2:
                    vector.wait_ge(sp[slot].h, spq[ic - 2])
                ch(vector.tensor_tensor(A3[slot][:], A2[slot][:], A1[s3][:], ALU.mult))
                ch(vector.tensor_scalar(A1[s3][:], A3[slot][:], scols[:, ISB:ISB + 1],
                                        mcol[:], ALU.mult, ALU.add))
                ch(vector.tensor_scalar(A1[s3][:], A1[s3][:], nmcol[:], None, ALU.add))
                f23_pp.append(vs.n)
            block.vector(_)
        def emit_t2(ic):
            slot = ic % 2
            s3 = ic % 3
            def _(vector, ic=ic, slot=slot, s3=s3):
                vector.wait_ge(vs.h, f23_pp[ic])
                vector.wait_ge(lu[slot].h, 16 * (ic // 2 + 1))
                def ch(inst):
                    vs.inc(inst); vector.wait_ge(vs.h, vs.n)
                ch(vector.tensor_scalar(A2[slot][:], ubuf[slot][:], scols[:, ISU:ISU + 1],
                                        mcol[:], ALU.mult, ALU.add))
                ch(vector.tensor_scalar(A2[slot][:], A2[slot][:], nmcol[:], None, ALU.add))
                ch(vector.tensor_tensor(A3[slot][:], A1[s3][:], A2[slot][:], ALU.mult))
                qq_v[ic] = vs.n
                vs.inc(vector.reduce_max(qstat[:, ic:ic + 1], A3[slot][:], axis=AX.X,
                                         apply_absolute_value=True))
                f23_v.append(vs.n)
            block.vector(_)
            def _(gpsimd, ic=ic, slot=slot):
                gpsimd.wait_ge(vs.h, f23_v[-1])
                sp[slot].inc(gpsimd.dma_start(out=QQ_hbm[ic * 128:(ic + 1) * 128, :],
                                              in_=A3[slot][:]), 16)
                spq[ic] = sp[slot].n
            block.gpsimd(_)

        for ic in range(NI):
            slot = ic % 2
            if ic >= 2:
                emit_t2(ic - 2)
            def _(sync, ic=ic, slot=slot):
                if ic < 2:
                    sync.wait_ge(sp[0].h, F1_SP[0])
                    sync.wait_ge(sp[1].h, F1_SP[1])
                    sync.wait_ge(sq[0].h, F1_SQ[0])
                    sync.wait_ge(sq[1].h, F1_SQ[1])
                if f23_yg:
                    sync.wait_ge(ss.h, f23_yg[-1])
                lr[slot].inc(sync.dma_start(out=gbuf[slot][:],
                                            in_=G_hbm[ic * 128:(ic + 1) * 128, :]), 16)
            block.sync(_)
            s3 = ic % 3
            def _(scalar, ic=ic, slot=slot, s3=s3):
                scalar.wait_ge(lr[slot].h, lr[slot].n)
                scalar.wait_ge(vs.h, bca)
                if ic >= 3:
                    scalar.wait_ge(vs.h, qq_v[ic - 3])
                ss.inc(scalar.activation(A1[s3][:], gbuf[slot][:], ACTF.Copy,
                                         scale=scols[:, ISG:ISG + 1]))
                f23_yg.append(ss.n)
            block.scalar(_)
            yg_s = ss.n
            def _(vector, ic=ic, slot=slot, s3=s3):
                vector.wait_ge(ss.h, yg_s)
                if ic >= 2:
                    vector.wait_ge(vs.h, qq_v[ic - 2])
                vs.inc(rnd(vector, A2[slot][:], A1[s3][:]))
            block.vector(_)
            kg_v = vs.n
            def _(scalar, ic=ic, slot=slot, s3=s3):
                scalar.wait_ge(vs.h, kg_v)
                ss.inc(scalar.activation(A1[s3][:], A2[slot][:], ACTF.Sigmoid,
                                         scale=scols[:, SG:SG + 1]))
                f23_sg.append(ss.n)
            block.scalar(_)
            if ic >= 1:
                emit_t1(ic - 1)
            def _(sync, ic=ic, slot=slot):
                if ic < 2:
                    sync.wait_ge(sp[0].h, F1_SP[0])
                    sync.wait_ge(sp[1].h, F1_SP[1])
                    sync.wait_ge(sq[0].h, F1_SQ[0])
                    sync.wait_ge(sq[1].h, F1_SQ[1])
                else:
                    sync.wait_ge(vs.h, f23_v[ic - 2])
                lu[slot].inc(sync.dma_start(out=ubuf[slot][:],
                                              in_=U_hbm[ic * 128:(ic + 1) * 128, :]), 16)
            block.sync(_)
        emit_t2(NI - 2)
        emit_t1(NI - 1)
        emit_t2(NI - 1)
        F3_V = vs.n
        F3_SP = (sp[0].n, sp[1].n)

        fc = fold_stat(qstat[:, 0:NI], 0, F3_V)
        bc = barrier("Cc", fc)
        def _(vector):
            vector.wait_ge(vs.h, bc)
            def ch(inst):
                vs.inc(inst); vector.wait_ge(vs.h, vs.n)
            ch(vector.tensor_tensor(srow[0:1, 6:7], srow[0:1, 4:5], srow[0:1, 8:9],
                                    ALU.mult))
            ch(vector.tensor_tensor(srow[0:1, 7:8], srow[0:1, 0:1], srow[0:1, 6:7],
                                    ALU.mult))
            ch(vector.tensor_scalar(srow[0:1, 13:14], srow[0:1, 7:8], INV_QMAX, 1.0e-12,
                                    ALU.mult, ALU.max))
            ch(vector.reciprocal(srow[0:1, 14:15], srow[0:1, 13:14]))
            ch(vector.tensor_tensor(srow[0:1, 15:16], srow[0:1, 6:7], srow[0:1, 14:15],
                                    ALU.mult))
        block.vector(_)
        bcc = bcast_scols([(15, ISCQ), (13, SC)], vs.n)

        stk4a.close()
        oT = [sb(f"oT{i}", [128, NI * TQ], BF16, stk4b) for i in range(2)]
        qqb = [sb(f"qqb{i}", [128, TQ], F32, stk4b) for i in range(4)]
        lr4 = [lr[0], lr[1], mk("lr2"), mk("lr3")]
        oki = [sb(f"oki{i}", [128, TQ], F32, stk4b) for i in range(2)]

        out_v = []
        mmq_ms = {}
        dq_ms = {}
        row_lists = {}
        bank_user = {0: None, 1: None}

        def emit_dq_ic(tq, ic):
            qslot = tq % 2
            dq_prev = row_lists.get(tq - 1)
            row_ms = row_lists[tq]
            slot = ic % 2
            qs = ic % 4
            def _(sync, tq=tq, ic=ic, qs=qs):
                if tq == 0 and ic < 4:
                    sync.wait_ge(sp[0].h, F3_SP[0])
                    sync.wait_ge(sp[1].h, F3_SP[1])
                elif ic >= 4:
                    sync.wait_ge(vs.h, row_ms[ic - 4])
                else:
                    sync.wait_ge(vs.h, dq_prev[ic - 4])
                lr4[qs].inc(sync.dma_start(
                    out=qqb[qs][:],
                    in_=QQ_hbm[ic * 128:(ic + 1) * 128, tq * TQ:(tq + 1) * TQ]), 16)
            block.sync(_)
            def _(scalar, tq=tq, ic=ic, slot=slot, qs=qs):
                scalar.wait_ge(lr4[qs].h, lr4[qs].n)
                scalar.wait_ge(vs.h, bcc)
                if ic >= 2:
                    scalar.wait_ge(vs.h, row_ms[ic - 2])
                elif dq_prev is not None:
                    scalar.wait_ge(vs.h, dq_prev[ic - 2])
                ss.inc(scalar.activation(oki[slot][:], qqb[qs][:], ACTF.Copy,
                                         scale=scols[:, ISCQ:ISCQ + 1]))
            block.scalar(_)
            yq_s = ss.n
            def _(vector, tq=tq, ic=ic, qslot=qslot, slot=slot):
                vector.wait_ge(ss.h, yq_s)
                if tq >= 2 and ic == 0:
                    vector.wait_ge(ts.h, mmq_ms[tq - 2])
                vs.inc(rnd(vector, oki[slot][:], oki[slot][:]))
                vector.wait_ge(vs.h, vs.n)
                vs.inc(vector.tensor_scalar(oT[qslot][:, ic * TQ:(ic + 1) * TQ],
                                            oki[slot][:], scols[:, SC:SC + 1], None,
                                            ALU.mult))
                row_ms.append(vs.n)
            block.vector(_)
            if ic == NI - 1:
                dq_ms[tq] = vs.n

        def emit_mm_tile(tq, tt):
            qslot = tq % 2
            if True:
                g = tq * NTQ + tt
                banks = (pA, pB) if tt % 2 == 0 else (pC, pD)
                slot = g % 2
                def _(gpsimd, g=g, slot=slot):
                    if len(out_v) >= 2:
                        gpsimd.wait_ge(vs.h, out_v[-2])
                        gpsimd.wait_ge(so[slot].h, so[slot].n)
                    else:
                        gpsimd.wait_ge(vs.h, bcc)
                    lg[slot].inc(gpsimd.dma_start(out=x2t[slot][:],
                                                  in_=x2_hbm[g * 128:(g + 1) * 128, :]), 16)
                block.gpsimd(_)
                def _(tensor, tq=tq, tt=tt, banks=banks, qslot=qslot):
                    if tq == 0 and tt == 0:
                        tensor.wait_ge(wl.h, 64)
                    tensor.wait_ge(vs.h, dq_ms[tq])
                    bu = bank_user[tt % 2]
                    if bu is not None:
                        tensor.wait_ge(vs.h, out_v[bu])
                    else:
                        tensor.wait_ge(vs.h, bcc)
                    for h in range(2):
                        for ic in range(NI):
                            inst = tensor.matmul(
                                banks[h][:],
                                oT[qslot][:, ic * TQ + tt * 128: ic * TQ + tt * 128 + 128],
                                WdnT[:, ic * C + h * 512: ic * C + (h + 1) * 512],
                                start=(ic == 0), stop=(ic == NI - 1))
                    ts.inc(inst)
                block.tensor(_)
                mmf = ts.n
                if tt == NTQ - 1:
                    mmq_ms[tq] = ts.n
                def _(vector, g=g, banks=banks, slot=slot):
                    vector.wait_ge(ts.h, mmf)
                    vector.wait_ge(lg[slot].h, lg[slot].n)
                    vector.tensor_tensor(x2t[slot][:, 0:512], banks[0][:],
                                         x2t[slot][:, 0:512], ALU.add)
                    vs.inc(vector.tensor_tensor(x2t[slot][:, 512:1024], banks[1][:],
                                                x2t[slot][:, 512:1024], ALU.add))
                    out_v.append(vs.n)
                    bank_user[tt % 2] = len(out_v) - 1
                block.vector(_)
                def _(gpsimd, g=g, slot=slot):
                    gpsimd.wait_ge(vs.h, out_v[-1])
                    so[slot].inc(gpsimd.dma_start(out=out[g * 128:(g + 1) * 128, :],
                                                  in_=x2t[slot][:]), 16)
                block.gpsimd(_)

        kpt = -(-NI // NTQ)
        row_lists[0] = []
        for ic in range(NI):
            emit_dq_ic(0, ic)
        for tq in range(4):
            if tq < 3:
                row_lists[tq + 1] = []
            for tt in range(NTQ):
                if tq < 3:
                    for ic in range(tt * kpt, min((tt + 1) * kpt, NI)):
                        emit_dq_ic(tq + 1, ic)
                emit_mm_tile(tq, tt)

        def _(gpsimd):
            gpsimd.wait_ge(so[0].h, so[0].n)
            gpsimd.wait_ge(so[1].h, so[1].n)
        block.gpsimd(_)

    with nc.Block() as rb:
        def _replay(eng, ops):
            for attr, a, k, inst in ops:
                real = getattr(eng, attr)(*a, **k)
                for sem, v in inst.incs:
                    real.then_inc(sem, v)
        rb.sync(lambda e: _replay(e, block.eng["sync"].ops))
        rb.scalar(lambda e: _replay(e, block.eng["scalar"].ops))
        rb.vector(lambda e: _replay(e, block.eng["vector"].ops))
        rb.tensor(lambda e: _replay(e, block.eng["tensor"].ops))
        rb.gpsimd(lambda e: _replay(e, block.eng["gpsimd"].ops))
    stk4b.close()
    stk4o.close()

    return nc


_NC_CACHE = {}


def _consts():
    iden = np.eye(128, dtype=np.float32)
    tril = np.tril(np.ones((128, 128), dtype=bool))
    maskneg = np.where(tril, np.float32(0.0), np.float32(NEGBIG)).astype(np.float32)
    ones = np.ones((1, 128), dtype=np.float32)
    return iden, maskneg, ones


def make_in_maps(x, Wb, Wv, bv, Wo, bo, Wg, Wu, Wd):
    iden, maskneg, ones = _consts()
    base = {
        "Wb": np.ascontiguousarray(np.asarray(Wb, np.float32)),
        "Wv": np.ascontiguousarray(np.asarray(Wv, np.float32)),
        "bv": np.asarray(bv, np.float32).reshape(1, C),
        "Wo": np.ascontiguousarray(np.asarray(Wo, np.float32)),
        "bo": np.asarray(bo, np.float32).reshape(1, C),
        "Wg": np.ascontiguousarray(np.asarray(Wg, np.float32)),
        "Wu": np.ascontiguousarray(np.asarray(Wu, np.float32)),
        "Wd": np.ascontiguousarray(np.asarray(Wd, np.float32)),
        "iden_in": iden, "maskneg_in": maskneg, "ones_in": ones,
    }
    return [dict(base, x_in=np.ascontiguousarray(np.asarray(xb, np.float32)))
            for xb in x]


def kernel(L0_attn_hidden_states, L0_attn_W_basis, L0_attn_W_v, L0_attn_b_v,
           L0_attn_W_o, L0_attn_b_o, L0_ffn_w_gate, L0_ffn_w_up, L0_ffn_w_down):
    x = np.ascontiguousarray(np.asarray(L0_attn_hidden_states, dtype=np.float32))
    B, T, C_ = x.shape
    assert C_ == C and B == 8
    if T not in _NC_CACHE:
        _NC_CACHE[T] = build_nc(T)
    nc = _NC_CACHE[T]
    in_maps = make_in_maps(x, L0_attn_W_basis, L0_attn_W_v, L0_attn_b_v,
                           L0_attn_W_o, L0_attn_b_o, L0_ffn_w_gate, L0_ffn_w_up,
                           L0_ffn_w_down)
    res = run_bass_kernel_spmd(nc, in_maps, core_ids=list(range(B)))
    return np.stack([np.asarray(res.results[b]["out"]) for b in range(B)], axis=0)


# version 27
# speedup vs baseline: 1.0573x; 1.0086x over previous
import contextlib
import numpy as np

import concourse.bass as bass
import concourse.mybir as mybir
from concourse.bass_utils import run_bass_kernel_spmd

F32 = mybir.dt.float32
BF16 = mybir.dt.bfloat16
ALU = mybir.AluOpType
ACTF = mybir.ActivationFunctionType
AX = mybir.AxisListType

C = 1024
NCH = 8
R = 64
I = 3072
NI = 24
INV_QMAX = 1.0 / 32767.0
INV_WQMAX = 1.0 / 7.0
U16_CAP = 65535.4921875
NEGBIG = -1.0e30


class Sem:
    def __init__(self, ctx, nc, name):
        self.h = ctx.enter_context(nc.semaphore(name))
        self.n = 0

    def inc(self, inst, k=1):
        inst.then_inc(self.h, k)
        self.n += k
        return self.n


def build_nc(T=4096):
    NT = T // 128
    NQ = T // 512
    TQ = T // 4
    NTQ = TQ // 128
    nc = bass.Bass("TRN2", target_bir_lowering=False, debug=False)
    ctx = contextlib.ExitStack()

    x_in = nc.dram_tensor("x_in", [T, C], F32, kind="ExternalInput").ap()
    Wb = nc.dram_tensor("Wb", [R, C], F32, kind="ExternalInput").ap()
    Wv = nc.dram_tensor("Wv", [C, C], F32, kind="ExternalInput").ap()
    bv = nc.dram_tensor("bv", [1, C], F32, kind="ExternalInput").ap()
    Wo = nc.dram_tensor("Wo", [C, C], F32, kind="ExternalInput").ap()
    bo = nc.dram_tensor("bo", [1, C], F32, kind="ExternalInput").ap()
    Wg = nc.dram_tensor("Wg", [I, C], F32, kind="ExternalInput").ap()
    Wu = nc.dram_tensor("Wu", [I, C], F32, kind="ExternalInput").ap()
    Wd = nc.dram_tensor("Wd", [C, I], F32, kind="ExternalInput").ap()
    iden_in = nc.dram_tensor("iden_in", [128, 128], F32, kind="ExternalInput").ap()
    maskneg_in = nc.dram_tensor("maskneg_in", [128, 128], F32, kind="ExternalInput").ap()
    ones_in = nc.dram_tensor("ones_in", [1, 128], F32, kind="ExternalInput").ap()
    out = nc.dram_tensor("out", [T, C], F32, kind="ExternalOutput").ap()

    x2_hbm = nc.dram_tensor("x2_hbm", [T, C], F32, kind="Internal").ap()
    U_hbm = nc.dram_tensor("U_hbm", [I, T], BF16, kind="Internal").ap()
    G_hbm = nc.dram_tensor("G_hbm", [I, T], BF16, kind="Internal").ap()
    QQ_hbm = nc.dram_tensor("QQ_hbm", [I, T], F32, kind="Internal").ap()
    WdnT_hbm = nc.dram_tensor("WdnT_hbm", [128, (I // 128) * C], BF16, kind="Internal").ap()
    cc_in, cc_out = {}, {}
    for bn in ("X", "A", "B", "Cc"):
        cc_in[bn] = nc.dram_tensor(f"cc_in_{bn}", [1, 8], F32, kind="Internal").ap()
        cc_out[bn] = nc.dram_tensor(f"cc_out_{bn}", [1, 8], F32, kind="Internal",
                                    addr_space="Shared").ap()

    def sb(name, shape, dt, stack=ctx, side=None):
        if side is None:
            return stack.enter_context(nc.sbuf_tensor(name, shape, dt))
        return stack.enter_context(nc.sbuf_tensor(name, shape, dt, side=side))

    iden_f = sb("iden_f", [128, 128], F32)
    iden_b = sb("iden_b", [128, 128], BF16)
    ones_f = sb("ones_f", [1, 128], F32)
    ones_b = sb("ones_b", [1, 128], BF16)
    mcol = sb("mcol", [128, 1], F32)
    nmcol = sb("nmcol", [128, 1], F32)
    tmpc = sb("tmpc", [128, 4], F32)
    scols = sb("scols", [128, 16], F32)
    SX, ISX, SG, ISG, SU, ISU, SGQ, ISB, ISCQ, SC = range(10)
    srow = sb("srow", [1, 16], F32)
    s1t = sb("s1t", [128, NT], F32)
    s2t = sb("s2t", [128, NT], F32)
    rstdt = sb("rstdt", [128, NT], F32)
    nmbt = sb("nmbt", [128, NT], F32)
    tmpstat = sb("tmpstat", [128, NT], F32)
    statx = sb("statx", [128, NT], F32)
    ustat = sb("ustat", [128, 2 * NI], F32)
    gstat = sb("gstat", [128, 2 * NI], F32)
    gstat2 = sb("gstat2", [128, 2 * NI], F32)
    qstat = sb("qstat", [128, NI], F32)
    padt = sb("padt", [128, 128], F32)
    mparts = sb("mparts", [128, 16], F32)
    zparts = sb("zparts", [128, 16], F32)
    nbias = sb("nbias", [128, 8], F32)
    zcol = sb("zcol", [128, 8], F32)
    rzcol = sb("rzcol", [128, 8], F32)

    x2t = [sb(f"x2t{i}", [128, C], F32) for i in range(2)]
    xlt = [sb(f"xlt{i}", [128, C], BF16) for i in range(2)]

    psl = lambda name: ctx.enter_context(nc.psum_tensor(name, [128, 512], F32))
    pA, pB, pC, pD, pE, pF = psl("pA"), psl("pB"), psl("pC"), psl("pD"), psl("pE"), psl("pF")
    pT = ctx.enter_context(nc.psum_tensor("pT", [128, 1024], BF16))
    pT2 = ctx.enter_context(nc.psum_tensor("pT2", [128, 1024], BF16))

    stk1 = contextlib.ExitStack()
    stk2a = contextlib.ExitStack()
    stk2b = contextlib.ExitStack()
    stk3 = contextlib.ExitStack()
    stk4o = contextlib.ExitStack()
    stk4a = contextlib.ExitStack()
    stk4b = contextlib.ExitStack()

    stk0 = contextlib.ExitStack()
    xlnT = sb("xlnT", [128, NCH * T], BF16, stk1, side="right")
    WvT = sb("WvT", [128, NCH * C], BF16, stk1, side="right")
    WbT = sb("WbT", [128, NCH * R], BF16, stk1, side="right")
    wtr = [sb(f"wtr{i}", [128, C], F32, stk0, side="right") for i in range(2)]
    wtb = [sb(f"wtb{i}", [128, C], BF16, stk0, side="right") for i in range(2)]
    wbraw = sb("wbraw", [64, C], F32, stk0, side="right")
    bv_f = sb("bv_f", [1, C], F32, stk0, side="right")
    bo_f = sb("bo_f", [1, C], F32, stk0, side="right")
    WoT = sb("WoT", [128, NCH * C], BF16, stk2a)
    xsq = [sb(f"xsq{i}", [128, C], F32, stk2a) for i in range(2)]
    LT = sb("LT", [64, T], BF16, stk2a)
    bo_bc = sb("bo_bc", [128, C], F32, stk2a)
    bv_b = sb("bv_b", [1, C], BF16, stk2a)

    class _RecInst:
        __slots__ = ("incs",)
        def __init__(self):
            self.incs = []
        def then_inc(self, sem, v):
            self.incs.append((sem, v))
            return self

    class _RecEngine:
        def __init__(self):
            self.ops = []
        def __getattr__(self, attr):
            ops = self.ops
            def call(*a, **k):
                inst = _RecInst()
                ops.append((attr, a, k, inst))
                return inst
            return call

    class _RecBlock:
        def __init__(self):
            self.eng = {n: _RecEngine() for n in
                        ("sync", "scalar", "vector", "tensor", "gpsimd")}
        def __getattr__(self, name):
            e = self.eng[name]
            return lambda fn: fn(e)

    block = _RecBlock()
    if True:
        mk = lambda name: Sem(ctx, nc, name)
        lc = mk("lc")
        lx = [mk("lx0"), mk("lx1")]
        lw = [mk("lw0"), mk("lw1")]
        lr = [mk("lr0"), mk("lr1")]
        lg = [mk("lg0"), mk("lg1")]
        sp = [mk("sp0"), mk("sp1")]
        lu = [mk("lu0"), mk("lu1")]
        wds = mk("wds")
        sq = [mk("sq0"), mk("sq1")]
        wl = mk("wl")
        so = [mk("so0"), mk("so1")]
        ts = mk("ts")
        vs = mk("vs")
        ss = mk("ss")
        gs = mk("gs")
        ptg = [0]
        ptgv = [0]
        pt2g = [0]

        def rnd(vector, dst, src):
            return vector.tensor_scalar(dst, src, mcol[:], nmcol[:], ALU.add, ALU.add)

        def _(sync):
            for dst, src in ((iden_f[:], iden_in),
                             (ones_f[:], ones_in), (bv_f[:], bv), (bo_f[:], bo),
                             (wbraw[:], Wb)):
                lc.inc(sync.dma_start(out=dst, in_=src), 16)
        block.sync(_)

        def _(vector):
            vector.wait_ge(lc.h, lc.n)
            def ch(inst):
                vs.inc(inst); vector.wait_ge(vs.h, vs.n)
            ch(vector.memset(tmpc[:, 0:1], 0.75))
            ch(vector.reciprocal(tmpc[:, 0:1], tmpc[:, 0:1]))
            ch(vector.tensor_scalar(tmpc[:, 0:1], tmpc[:, 0:1], -1.0, 3.0, ALU.add, ALU.mult))
            ch(vector.tensor_scalar(tmpc[:, 0:1], tmpc[:, 0:1], -1.0, None, ALU.add))
            ch(vector.tensor_scalar(tmpc[:, 1:2], tmpc[:, 0:1], -1.0, None, ALU.mult))
            ch(vector.tensor_tensor(tmpc[:, 0:1], tmpc[:, 0:1], tmpc[:, 1:2], ALU.max))
            ch(vector.reciprocal(tmpc[:, 0:1], tmpc[:, 0:1]))
            ch(vector.tensor_scalar(mcol[:], tmpc[:, 0:1], 1.5, None, ALU.mult))
            ch(vector.tensor_scalar(nmcol[:], mcol[:], -1.0, None, ALU.mult))
            vector.memset(zparts[:], 0.0)
            vector.memset(srow[:], 0.0)
            vector.tensor_copy(iden_b[:], iden_f[:])
            vector.tensor_copy(ones_b[:], ones_f[:])
            vector.tensor_copy(wtb[0][0:64, :], wbraw[:])
            vs.inc(vector.tensor_copy(bv_b[:], bv_f[:]))
        block.vector(_)
        P0_V = vs.n

        def _(tensor):
            tensor.wait_ge(vs.h, P0_V)
            tensor.matmul(pA[:], ones_f[:], bo_f[:, 0:512], start=True, stop=True)
            ts.inc(tensor.matmul(pB[:], ones_f[:], bo_f[:, 512:1024], start=True, stop=True))
        block.tensor(_)
        bo_ts = ts.n

        def _(vector):
            vector.wait_ge(ts.h, bo_ts)
            vector.tensor_copy(bo_bc[:, 0:512], pA[:])
            vs.inc(vector.tensor_copy(bo_bc[:, 512:1024], pB[:]))
        block.vector(_)
        bo_v = vs.n

        def _(tensor):
            for ci in range(NCH):
                inst = tensor.transpose(pT2[:, ci * R:(ci + 1) * R],
                                        wtb[0][0:64, ci * 128:(ci + 1) * 128],
                                        iden_b[0:64, 0:64])
            ts.inc(inst)
        block.tensor(_)
        wb_ts = ts.n

        def _(scalar):
            scalar.wait_ge(ts.h, wb_ts)
            ss.inc(scalar.copy(WbT[:], pT2[:, 0:NCH * R]))
            pt2g[0] = ss.n
        block.scalar(_)

        attn_cast = []
        attn_tr = [ts.n, ts.n]
        def prep_attn_weight(Wsrc, dst):
            cast_ms, tr_ms = attn_cast, attn_tr
            for rt in range(NCH):
                slot = (len(cast_ms)) % 2
                def _(sync, rt=rt, slot=slot):
                    if len(cast_ms) >= 2:
                        sync.wait_ge(vs.h, cast_ms[-2])
                    lw[slot].inc(sync.dma_start(out=wtr[slot][:],
                                                in_=Wsrc[rt * 128:(rt + 1) * 128, :]), 16)
                block.sync(_)
                def _(vector, rt=rt, slot=slot):
                    vector.wait_ge(lw[slot].h, lw[slot].n)
                    vector.wait_ge(ts.h, tr_ms[-2])
                    vs.inc(vector.tensor_copy(wtb[slot][:], wtr[slot][:]))
                    cast_ms.append(vs.n)
                block.vector(_)
                def _(tensor, rt=rt, slot=slot):
                    tensor.wait_ge(vs.h, cast_ms[-1])
                    tensor.wait_ge(ss.h, pt2g[0])
                    for ci in range(NCH):
                        inst = tensor.transpose(pT2[:, ci * 128:(ci + 1) * 128],
                                                wtb[slot][:, ci * 128:(ci + 1) * 128], iden_b[:])
                    ts.inc(inst)
                    tr_ms.append(ts.n)
                block.tensor(_)
                def _(scalar, rt=rt):
                    scalar.wait_ge(ts.h, tr_ms[-1])
                    for ci in range(NCH):
                        inst = scalar.copy(dst[:, ci * C + rt * 128: ci * C + rt * 128 + 128],
                                           pT2[:, ci * 128:(ci + 1) * 128])
                    ss.inc(inst)
                    pt2g[0] = ss.n
                block.scalar(_)

        prep_attn_weight(Wv, WvT)
        prep_attn_weight(Wo, WoT)
        P0_SS_END = ss.n
        stk0.close()

        red_ms, sq_ms = [], []
        for t in range(NT):
            slot = t % 2
            def _(sync, t=t, slot=slot):
                if t >= 2:
                    sync.wait_ge(vs.h, red_ms[t - 2])
                    sync.wait_ge(ss.h, sq_ms[t - 2])
                lx[slot].inc(sync.dma_start(out=x2t[slot][:],
                                            in_=x_in[t * 128:(t + 1) * 128, :]), 16)
            block.sync(_)
            def _(scalar, t=t, slot=slot):
                scalar.wait_ge(lx[slot].h, lx[slot].n)
                if t >= 2:
                    scalar.wait_ge(ss.h, sq_ms[t - 2])
                ss.inc(scalar.activation(xsq[slot][:], x2t[slot][:], ACTF.Square,
                                         accum_out=s2t[:, t:t + 1]))
                sq_ms.append(ss.n)
            block.scalar(_)
            def _(vector, t=t, slot=slot):
                vector.wait_ge(lx[slot].h, lx[slot].n)
                vs.inc(vector.reduce_sum(s1t[:, t:t + 1], x2t[slot][:], axis=AX.X))
                red_ms.append(vs.n)
            block.vector(_)

        def emit_ln_chain(pre_v):
            def _(vector):
                vector.wait_ge(vs.h, pre_v)
                def ch(inst):
                    vs.inc(inst); vector.wait_ge(vs.h, vs.n)
                invC = 1.0 / C
                ch(vector.tensor_scalar(s1t[:], s1t[:], invC, None, ALU.mult))
                ch(vector.tensor_tensor(tmpstat[:], s1t[:], s1t[:], ALU.mult))
                ch(vector.scalar_tensor_tensor(s2t[:], s2t[:], invC, tmpstat[:],
                                               ALU.mult, ALU.subtract))
                ch(vector.tensor_scalar(s2t[:], s2t[:], 1.0e-5, None, ALU.add))
            block.vector(_)
            mid_v = vs.n
            def _(scalar):
                scalar.wait_ge(vs.h, mid_v)
                ss.inc(scalar.activation(rstdt[:], s2t[:], ACTF.Sqrt))
            block.scalar(_)
            mid_s = ss.n
            def _(vector):
                vector.wait_ge(ss.h, mid_s)
                def ch(inst):
                    vs.inc(inst); vector.wait_ge(vs.h, vs.n)
                ch(vector.reciprocal(rstdt[:], rstdt[:]))
                ch(vector.tensor_tensor(tmpstat[:], s2t[:], rstdt[:], ALU.mult))
                ch(vector.tensor_tensor(tmpstat[:], tmpstat[:], rstdt[:], ALU.mult))
                ch(vector.tensor_scalar(tmpstat[:], tmpstat[:], -0.5, 1.5, ALU.mult, ALU.add))
                ch(vector.tensor_tensor(rstdt[:], rstdt[:], tmpstat[:], ALU.mult))
                ch(vector.tensor_tensor(nmbt[:], s1t[:], rstdt[:], ALU.mult))
                ch(vector.tensor_scalar(nmbt[:], nmbt[:], -1.0, None, ALU.mult))
            block.vector(_)
            return vs.n

        def _(vector):
            vector.wait_ge(ss.h, sq_ms[-1])
        block.vector(_)
        ln1_v = emit_ln_chain(red_ms[-1])

        norm_ms = []
        xl_tr = []
        for t in range(NT):
            slot = t % 2
            def _(sync, t=t, slot=slot):
                if t >= 2:
                    k, v = norm_ms[t - 2]
                    sync.wait_ge(ss.h if k == "s" else vs.h, v)
                else:
                    sync.wait_ge(vs.h, red_ms[NT - 2 + t])
                    sync.wait_ge(ss.h, sq_ms[NT - 2 + t])
                lx[slot].inc(sync.dma_start(out=x2t[slot][:],
                                            in_=x_in[t * 128:(t + 1) * 128, :]), 16)
            block.sync(_)
            if t % 2 == 0:
                def _(scalar, t=t, slot=slot):
                    scalar.wait_ge(lx[slot].h, lx[slot].n)
                    scalar.wait_ge(vs.h, ln1_v)
                    if t >= 2:
                        scalar.wait_ge(ts.h, xl_tr[t - 2])
                    ss.inc(scalar.activation(xlt[slot][:], x2t[slot][:], ACTF.Identity,
                                             bias=nmbt[:, t:t + 1], scale=rstdt[:, t:t + 1]))
                    norm_ms.append(("s", ss.n))
                block.scalar(_)
            else:
                def _(vector, t=t, slot=slot):
                    vector.wait_ge(lx[slot].h, lx[slot].n)
                    vector.wait_ge(vs.h, ln1_v)
                    if t >= 2:
                        vector.wait_ge(ts.h, xl_tr[t - 2])
                    vs.inc(vector.tensor_scalar(xlt[slot][:], x2t[slot][:],
                                                rstdt[:, t:t + 1], nmbt[:, t:t + 1],
                                                ALU.mult, ALU.add))
                    norm_ms.append(("v", vs.n))
                block.vector(_)
            def _(tensor, t=t, slot=slot):
                k, v = norm_ms[t]
                tensor.wait_ge(ss.h if k == "s" else vs.h, v)
                tensor.wait_ge(ss.h, ptg[0])
                tensor.wait_ge(vs.h, ptgv[0])
                for ci in range(NCH):
                    inst = tensor.transpose(pT[:, ci * 128:(ci + 1) * 128],
                                            xlt[slot][:, ci * 128:(ci + 1) * 128], iden_b[:])
                ts.inc(inst)
                xl_tr.append(ts.n)
            block.tensor(_)
            trm = ts.n
            if t % 2 == 0:
                def _(scalar, t=t):
                    scalar.wait_ge(ts.h, trm)
                    for ci in range(NCH):
                        inst = scalar.copy(xlnT[:, ci * T + t * 128: ci * T + t * 128 + 128],
                                           pT[:, ci * 128:(ci + 1) * 128])
                    ss.inc(inst)
                    ptg[0] = ss.n
                block.scalar(_)
            else:
                def _(vector, t=t):
                    vector.wait_ge(ts.h, trm)
                    for ci in range(NCH):
                        inst = vector.tensor_copy(
                            xlnT[:, ci * T + t * 128: ci * T + t * 128 + 128],
                            pT[:, ci * 128:(ci + 1) * 128])
                    vs.inc(inst)
                    ptgv[0] = vs.n
                block.vector(_)
        P1_SS = ss.n
        P1_VS = vs.n

        lat_ms = []
        for f in range(NQ):
            bank = pA if f % 2 == 0 else pB
            def _(tensor, f=f, bank=bank):
                if f < 2:
                    tensor.wait_ge(ss.h, P1_SS)
                    tensor.wait_ge(vs.h, P1_VS)
                    tensor.wait_ge(vs.h, bo_v)
                else:
                    tensor.wait_ge(ss.h, lat_ms[f - 2])
                for ci in range(NCH):
                    inst = tensor.matmul(bank[0:R, :],
                                         WbT[:, ci * R:(ci + 1) * R],
                                         xlnT[:, ci * T + f * 512: ci * T + (f + 1) * 512],
                                         start=(ci == 0), stop=(ci == NCH - 1))
                ts.inc(inst)
            block.tensor(_)
            mm = ts.n
            def _(scalar, f=f, bank=bank):
                scalar.wait_ge(ts.h, mm)
                ss.inc(scalar.copy(LT[:, f * 512:(f + 1) * 512], bank[0:R, :]))
                lat_ms.append(ss.n)
            block.scalar(_)

        v_sb = sb("v_sb", [128, NT * C], BF16, stk2a)
        v_ms = []
        for t in range(NT):
            banks = (pC, pD) if t % 2 == 0 else (pE, pF)
            for h in range(2):
                bank = banks[h]
                def _(tensor, t=t, h=h, bank=bank):
                    if t >= 2:
                        tensor.wait_ge(ss.h, v_ms[(t - 2) * 2 + h])
                    for ci in range(NCH):
                        tensor.matmul(bank[:],
                                      xlnT[:, ci * T + t * 128: ci * T + t * 128 + 128],
                                      WvT[:, ci * C + h * 512: ci * C + (h + 1) * 512],
                                      start=(ci == 0), stop=False)
                    inst = tensor.matmul(bank[:], ones_b[:], bv_b[:, h * 512:(h + 1) * 512],
                                         start=False, stop=True)
                    ts.inc(inst)
                block.tensor(_)
                mm = ts.n
                def _(scalar, t=t, h=h, bank=bank):
                    scalar.wait_ge(ts.h, mm)
                    ss.inc(scalar.copy(v_sb[:, t * C + h * 512: t * C + (h + 1) * 512], bank[:]))
                    v_ms.append(ss.n)
                block.scalar(_)
        P2_SS = ss.n
        stk1.close()

        pTs = sb("pTs", [128, NT * 512], BF16, stk2b)
        p_row = sb("p_row", [128, T], BF16, stk2b)
        yT = sb("yT", [128, NCH * 512], BF16, stk2b)
        maskneg = sb("maskneg", [128, 128], F32, stk2b)
        smsk = sb("smsk", [128, 128], F32, stk2b)
        ymm_last = [0]
        ptz = [0]
        def _(sync):
            sync.wait_ge(ss.h, P2_SS)
            lc.inc(sync.dma_start(out=maskneg[:], in_=maskneg_in), 16)
        block.sync(_)
        MSK_LC = lc.n
        last_exp_ss = [P2_SS]
        last_yev = [P2_SS]
        epi_v = []
        epi_sq = []
        x2st_ms = [0, 0]

        def emit_pass1(qc, qt):
            qp = (qc % 2) * 4 + qt
            g = qc * 4 + qt
            k_hi = (g + 1) * 128
            nkc = (k_hi + 511) // 512
            sbank = (pA, pB)
            red1 = []
            for kc in range(nkc):
                N = min(512, k_hi - kc * 512)
                def _(tensor, kc=kc, N=N, g=g):
                    if kc >= 2:
                        tensor.wait_ge(vs.h, red1[kc - 2])
                    else:
                        tensor.wait_ge(vs.h, p1_mv[-1])
                    ts.inc(tensor.matmul(sbank[kc % 2][:, 0:N],
                                         LT[:, g * 128:(g + 1) * 128],
                                         LT[:, kc * 512: kc * 512 + N],
                                         start=True, stop=True))
                block.tensor(_)
                mm = ts.n
                def _(vector, kc=kc, N=N, nkc=nkc):
                    vector.wait_ge(ts.h, mm)
                    if kc == nkc - 1:
                        vector.wait_ge(lc.h, MSK_LC)
                        off = N - 128
                        vs.inc(vector.tensor_tensor(smsk[:], sbank[kc % 2][:, off:off + 128],
                                                    maskneg[:], ALU.add))
                        vector.wait_ge(vs.h, vs.n)
                        if off > 0:
                            vector.reduce_max(mparts[:, kc:kc + 1],
                                              sbank[kc % 2][:, 0:off], axis=AX.X)
                        else:
                            vector.memset(mparts[:, kc:kc + 1], NEGBIG)
                        vs.inc(vector.reduce_max(mparts[:, kc + 1:kc + 2], smsk[:], axis=AX.X))
                    else:
                        vs.inc(vector.reduce_max(mparts[:, kc:kc + 1],
                                                 sbank[kc % 2][:, 0:N], axis=AX.X))
                    red1.append(vs.n)
                block.vector(_)
            def _(vector, qt=qt, nkc=nkc, qp=qp):
                vector.wait_ge(vs.h, red1[-1])
                vs.inc(vector.reduce_max(tmpc[:, 2:3], mparts[:, 0:nkc + 1], axis=AX.X))
                vector.wait_ge(vs.h, vs.n)
                vs.inc(vector.tensor_scalar(nbias[:, qp:qp + 1], tmpc[:, 2:3],
                                            -0.125, None, ALU.mult))
            block.vector(_)
            p1_mv.append(vs.n)
            return vs.n

        def emit_pass2(qc, qt, m_v):
            qp = (qc % 2) * 4 + qt
            g = qc * 4 + qt
            k_hi = (g + 1) * 128
            nkc = (k_hi + 511) // 512
            sbank = (pD, pE)
            exp_ms = []
            for kc in range(nkc):
                N = min(512, k_hi - kc * 512)
                def _(tensor, kc=kc, N=N, g=g, m_v=m_v):
                    if kc >= 2:
                        tensor.wait_ge(ss.h, exp_ms[kc - 2])
                    else:
                        tensor.wait_ge(vs.h, m_v)
                        tensor.wait_ge(ss.h, last_exp_ss[0])
                        tensor.wait_ge(ss.h, last_yev[0])
                    ts.inc(tensor.matmul(sbank[kc % 2][:, 0:N],
                                         LT[:, g * 128:(g + 1) * 128],
                                         LT[:, kc * 512: kc * 512 + N],
                                         start=True, stop=True))
                block.tensor(_)
                mm2 = ts.n
                last = (kc == nkc - 1)
                if last:
                    off = N - 128
                    def _(vector, kc=kc, N=N, off=off, m_v=m_v):
                        vector.wait_ge(ts.h, mm2)
                        vector.wait_ge(vs.h, m_v)
                        if off == 0:
                            vector.memset(zparts[:, kc:kc + 1], 0.0)
                        vs.inc(vector.tensor_tensor(smsk2[:], sbank[kc % 2][:, off:off + 128],
                                                    maskneg[:], ALU.add))
                    block.vector(_)
                    mk_v = vs.n
                    def _(scalar, kc=kc, N=N, off=off, qp=qp):
                        scalar.wait_ge(vs.h, mk_v)
                        if off > 0:
                            scalar.activation(p_row[:, kc * 512: kc * 512 + off],
                                              sbank[kc % 2][:, 0:off], ACTF.Exp,
                                              bias=nbias[:, qp:qp + 1], scale=0.125,
                                              accum_out=zparts[:, kc:kc + 1])
                        ss.inc(scalar.activation(p_row[:, kc * 512 + off: kc * 512 + N],
                                                 smsk2[:], ACTF.Exp,
                                                 bias=nbias[:, qp:qp + 1], scale=0.125,
                                                 accum_out=zparts[:, kc + 1:kc + 2]))
                        exp_ms.append(ss.n)
                    block.scalar(_)
                else:
                    def _(scalar, kc=kc, N=N, qp=qp, m_v=m_v):
                        scalar.wait_ge(ts.h, mm2)
                        scalar.wait_ge(vs.h, m_v)
                        ss.inc(scalar.activation(p_row[:, kc * 512: kc * 512 + N],
                                                 sbank[kc % 2][:, 0:N], ACTF.Exp,
                                                 bias=nbias[:, qp:qp + 1], scale=0.125,
                                                 accum_out=zparts[:, kc:kc + 1]))
                        exp_ms.append(ss.n)
                    block.scalar(_)
            last_exp_ss[0] = exp_ms[-1]
            def _(vector, qp=qp, nkc=nkc):
                vector.wait_ge(ss.h, exp_ms[-1])
                vs.inc(vector.reduce_sum(zcol[:, qp:qp + 1], zparts[:, 0:nkc + 1], axis=AX.X))
                vector.wait_ge(vs.h, vs.n)
                vs.inc(vector.reciprocal(rzcol[:, qp:qp + 1], zcol[:, qp:qp + 1]))
            block.vector(_)
            return exp_ms[-1]

        def emit_transposes(qc, qt, exp_last):
            g = qc * 4 + qt
            k_hi = (g + 1) * 128
            nblk = k_hi // 128
            for b0 in range(0, nblk, 8):
                bn = min(8, nblk - b0)
                def _(tensor, b0=b0, bn=bn, exp_last=exp_last):
                    tensor.wait_ge(ss.h, exp_last)
                    tensor.wait_ge(ss.h, ptg[0])
                    tensor.wait_ge(vs.h, ptgv[0])
                    for j in range(bn):
                        inst = tensor.transpose(pT[:, j * 128:(j + 1) * 128],
                                                p_row[:, (b0 + j) * 128:(b0 + j + 1) * 128],
                                                iden_b[:])
                    ts.inc(inst)
                block.tensor(_)
                trm = ts.n
                def _(scalar, b0=b0, bn=bn, qt=qt):
                    scalar.wait_ge(ts.h, trm)
                    for j in range(bn):
                        kb = b0 + j
                        inst = scalar.copy(
                            pTs[:, kb * 512 + qt * 128: kb * 512 + qt * 128 + 128],
                            pT[:, j * 128:(j + 1) * 128])
                    ss.inc(inst)
                    ptg[0] = ss.n
                block.scalar(_)

        p1_mv = [0]
        smsk2 = sb("smsk2", [128, 128], F32, stk2b)

        def emit_sm_memset(qc):
            def _(vector, qc=qc):
                vector.wait_ge(ts.h, ymm_last[0])
                if qc == 0:
                    vector.wait_ge(ss.h, P2_SS)
                inst = None
                for j in range(1, 4):
                    kb = qc * 4 + j
                    inst = vector.memset(pTs[:, kb * 512: kb * 512 + j * 128], 0.0)
                if inst is not None:
                    vs.inc(inst)
                    ptz[0] = vs.n
            block.vector(_)

        def emit_softmax_chunk(qc):
            emit_sm_memset(qc)
            mv_next = emit_pass1(qc, 0)
            for qt in range(4):
                e_last = emit_pass2(qc, qt, mv_next)
                if qt < 3:
                    mv_next = emit_pass1(qc, qt + 1)
                emit_transposes(qc, qt, e_last)

        def emit_yacc_group(qc, d, yev_ms, ptr_ss):
            nkb = (qc + 1) * 4
            bank = pC if d % 2 == 0 else pF
            def _(tensor, d=d, bank=bank, nkb=nkb, ptr_ss=ptr_ss):
                tensor.wait_ge(ss.h, ptr_ss)
                tensor.wait_ge(vs.h, ptz[0])
                if d >= 2:
                    tensor.wait_ge(ss.h, yev_ms[d - 2])
                elif epi_v:
                    tensor.wait_ge(vs.h, epi_v[-1])
                for kb in range(nkb):
                    inst = tensor.matmul(bank[:],
                                         v_sb[:, kb * C + d * 128: kb * C + d * 128 + 128],
                                         pTs[:, kb * 512:(kb + 1) * 512],
                                         start=(kb == 0), stop=(kb == nkb - 1))
                ts.inc(inst)
                if d == NCH - 1:
                    ymm_last[0] = ts.n
            block.tensor(_)
            ym = ts.n
            def _(scalar, d=d, bank=bank):
                scalar.wait_ge(ts.h, ym)
                ss.inc(scalar.copy(yT[:, d * 512:(d + 1) * 512], bank[:]))
                yev_ms.append(ss.n)
            block.scalar(_)
            if d == NCH - 1:
                last_yev[0] = yev_ms[-1]

        def emit_oproj_qt(qc, qt, yev_ms):
            if True:
                g = qc * 4 + qt
                slot = g % 2
                def _(sync, g=g, slot=slot):
                    if len(epi_sq) >= 2:
                        sync.wait_ge(ss.h, epi_sq[-2])
                        sync.wait_ge(sp[slot].h, x2st_ms[slot])
                    else:
                        k, v = norm_ms[NT - 2 + g]
                        sync.wait_ge(ss.h if k == "s" else vs.h, v)
                    lx[slot].inc(sync.dma_start(out=x2t[slot][:],
                                                in_=x_in[g * 128:(g + 1) * 128, :]), 16)
                block.sync(_)
                obanks = (pC, pF)
                def _(tensor, qt=qt):
                    tensor.wait_ge(ss.h, yev_ms[-1])
                    if epi_v:
                        tensor.wait_ge(vs.h, epi_v[-1])
                    for h in range(2):
                        for d in range(NCH):
                            inst = tensor.matmul(
                                obanks[h][:],
                                yT[:, d * 512 + qt * 128: d * 512 + qt * 128 + 128],
                                WoT[:, d * C + h * 512: d * C + (h + 1) * 512],
                                start=(d == 0), stop=(d == NCH - 1))
                    ts.inc(inst)
                block.tensor(_)
                om = ts.n
                def _(vector, qt=qt, g=g, slot=slot, qc=qc):
                    qp = (qc % 2) * 4 + qt
                    vector.wait_ge(ts.h, om)
                    vector.wait_ge(lx[slot].h, lx[slot].n)
                    def ch(inst):
                        vs.inc(inst); vector.wait_ge(vs.h, vs.n)
                    ch(vector.tensor_tensor(x2t[slot][:], x2t[slot][:], bo_bc[:], ALU.add))
                    ch(vector.scalar_tensor_tensor(x2t[slot][:, 0:512], obanks[0][:],
                                                   rzcol[:, qp:qp + 1], x2t[slot][:, 0:512],
                                                   ALU.mult, ALU.add))
                    ch(vector.scalar_tensor_tensor(x2t[slot][:, 512:1024], obanks[1][:],
                                                   rzcol[:, qp:qp + 1], x2t[slot][:, 512:1024],
                                                   ALU.mult, ALU.add))
                    vs.inc(vector.reduce_sum(s1t[:, g:g + 1], x2t[slot][:], axis=AX.X))
                    epi_v.append(vs.n)
                block.vector(_)
                def _(scalar, g=g, slot=slot):
                    scalar.wait_ge(vs.h, epi_v[-1])
                    if len(epi_sq) >= 2:
                        scalar.wait_ge(ss.h, epi_sq[-2])
                    elif sq_ms:
                        scalar.wait_ge(ss.h, sq_ms[-2 + len(epi_sq)])
                    ss.inc(scalar.activation(xsq[slot][:], x2t[slot][:], ACTF.Square,
                                             accum_out=s2t[:, g:g + 1]))
                    epi_sq.append(ss.n)
                block.scalar(_)
                def _(vector, g=g, slot=slot):
                    def ch(inst):
                        vs.inc(inst); vector.wait_ge(vs.h, vs.n)
                    vector.wait_ge(vs.h, vs.n)
                    ch(vector.tensor_scalar(tmpc[:, 3:4], s1t[:, g:g + 1], 1.0 / C,
                                            None, ALU.mult))
                    ch(vector.tensor_scalar(tmpc[:, 3:4], tmpc[:, 3:4], -1.0, None, ALU.mult))
                    vector.wait_ge(ss.h, epi_sq[-1])
                    ch(vector.tensor_scalar(xsq[slot][:], x2t[slot][:], tmpc[:, 3:4],
                                            None, ALU.add))
                    vs.inc(vector.reduce_max(statx[:, g:g + 1], xsq[slot][:], axis=AX.X,
                                             apply_absolute_value=True))
                    epi_v.append(vs.n)
                block.vector(_)
                def _(gpsimd, g=g, slot=slot):
                    gpsimd.wait_ge(vs.h, epi_v[-1])
                    gpsimd.wait_ge(ss.h, epi_sq[-1])
                    sp[slot].inc(gpsimd.dma_start(out=x2_hbm[g * 128:(g + 1) * 128, :],
                                                  in_=x2t[slot][:]), 16)
                    x2st_ms[slot] = sp[slot].n
                block.gpsimd(_)

        stk4w = contextlib.ExitStack()
        wrd = [sb("wrd0", [128, I], F32, stk4w)] * 2
        wqd = sb("wqd", [128, I], BF16, stk4w)
        blkd = sb("blkd", [128, 96], F32, stk4w)
        bind = sb("bind", [128, 96], F32, stk4w)
        wstg = sb("wstg", [128, I], BF16, stk4w)
        P3_TS0 = ts.n
        wd_cast, wd_tr = [], []
        def emit_wd_head(rt):
            slot = rt % 2
            def _(sync, rt=rt, slot=slot):
                if wd_cast:
                    sync.wait_ge(vs.h, wd_cast[-1])
                else:
                    sync.wait_ge(ts.h, P3_TS0)
                lw[slot].inc(sync.dma_start(out=wrd[slot][:],
                                            in_=Wd[rt * 128:(rt + 1) * 128, :]), 16)
            block.sync(_)
            def _(vector, rt=rt, slot=slot):
                vector.wait_ge(lw[slot].h, lw[slot].n)
                if wd_tr:
                    vector.wait_ge(ts.h, wd_tr[-1])
                def ch(inst):
                    vs.inc(inst); vector.wait_ge(vs.h, vs.n)
                w3 = wrd[slot][:].rearrange("p (b k) -> p b k", k=32)
                ch(vector.reduce_max(blkd[:], w3, axis=AX.X, apply_absolute_value=True))
                ch(vector.tensor_scalar(blkd[:], blkd[:], INV_WQMAX, 1.0e-12,
                                        ALU.mult, ALU.max))
                ch(vector.reciprocal(bind[:], blkd[:]))
                bmr = bind[:].unsqueeze(2).broadcast_to([128, 96, 32])
                ch(vector.tensor_tensor(w3, w3, bmr, ALU.mult))
                ch(rnd(vector, wrd[slot][:], wrd[slot][:]))
                bms = blkd[:].unsqueeze(2).broadcast_to([128, 96, 32])
                ch(vector.tensor_tensor(wqd[:].rearrange("p (b k) -> p b k", k=32),
                                        w3, bms, ALU.mult))
                wd_cast.append(vs.n)
            block.vector(_)
        def emit_wd_tr(rt):
            for hb in range(3):
                def _(tensor, rt=rt, hb=hb):
                    tensor.wait_ge(vs.h, wd_cast[rt])
                    tensor.wait_ge(ss.h, pt2g[0])
                    for j in range(8):
                        ii = hb * 8 + j
                        inst = tensor.transpose(pT2[:, j * 128:(j + 1) * 128],
                                                wqd[:, ii * 128:(ii + 1) * 128], iden_b[:])
                    ts.inc(inst)
                    wd_tr.append(ts.n)
                block.tensor(_)
                def _(scalar, rt=rt, hb=hb):
                    scalar.wait_ge(ts.h, wd_tr[-1])
                    if hb == 0:
                        scalar.wait_ge(wds.h, 16 * rt)
                    for j in range(8):
                        ii = hb * 8 + j
                        inst = scalar.copy(wstg[:, ii * 128:(ii + 1) * 128],
                                           pT2[:, j * 128:(j + 1) * 128])
                    ss.inc(inst)
                    pt2g[0] = ss.n
                block.scalar(_)
            def _(sync, rt=rt):
                sync.wait_ge(ss.h, pt2g[0])
                wds.inc(sync.dma_start(
                    out=WdnT_hbm.rearrange("p (ii c) -> p ii c", c=C)[:, :,
                                                                     rt * 128:rt * 128 + 128],
                    in_=wstg[:].rearrange("p (ii c) -> p ii c", c=128)), 16)
            block.sync(_)

        emit_softmax_chunk(0)
        wd_emitted = 0
        for qc in range(NQ):
            nxt = qc + 1 < NQ
            ptr_ss = ss.n
            yev = []
            if nxt:
                emit_sm_memset(qc + 1)
                mv = emit_pass1(qc + 1, 0)
            for d in range(4):
                emit_yacc_group(qc, d, yev, ptr_ss)
            if nxt:
                e_last = emit_pass2(qc + 1, 0, mv)
            for d in range(4, 8):
                emit_yacc_group(qc, d, yev, ptr_ss)
            if nxt:
                mv = emit_pass1(qc + 1, 1)
                emit_transposes(qc + 1, 0, e_last)
            for qt in range(4):
                emit_oproj_qt(qc, qt, yev)
                if nxt and qt < 3:
                    e_last = emit_pass2(qc + 1, qt + 1, mv)
                    if qt < 2:
                        mv = emit_pass1(qc + 1, qt + 2)
                    emit_transposes(qc + 1, qt + 1, e_last)
            if wd_emitted < 8:
                if wd_emitted >= 1:
                    emit_wd_tr(wd_emitted - 1)
                emit_wd_head(wd_emitted)
                wd_emitted += 1
        while wd_emitted < 8:
            if wd_emitted >= 1:
                emit_wd_tr(wd_emitted - 1)
            emit_wd_head(wd_emitted)
            wd_emitted += 1
        emit_wd_tr(7)
        stk4w.close()
        X2_SP = (sp[0].n, sp[1].n)
        stk2b.close()
        stk2a.close()

        def _(vector):
            vector.wait_ge(ss.h, epi_sq[-1])
        block.vector(_)
        ln2_v = emit_ln_chain(epi_v[-1])

        def _(vector):
            vector.wait_ge(vs.h, ln2_v)
            vs.inc(vector.tensor_tensor(statx[:], statx[:], rstdt[:], ALU.mult))
        block.vector(_)
        amax_v = vs.n
        pa_done = ss.n

        def fold_stat(stat_ap, col, pre_v, use_abs=True):
            def _(vector):
                if pre_v is not None:
                    vector.wait_ge(vs.h, pre_v)
                vs.inc(vector.memset(padt[:], NEGBIG if not use_abs else 0.0))
                vector.wait_ge(vs.h, vs.n)
                vs.inc(vector.reduce_max(padt[:, 0:1], stat_ap, axis=AX.X,
                                         apply_absolute_value=use_abs))
            block.vector(_)
            fv = vs.n
            def _(tensor):
                tensor.wait_ge(vs.h, fv)
                ts.inc(tensor.transpose(pA[:, 0:128], padt[:], iden_f[:]))
            block.tensor(_)
            ft = ts.n
            def _(vector):
                vector.wait_ge(ts.h, ft)
                vs.inc(vector.reduce_max(srow[0:1, col:col + 1], pA[0:1, 0:128], axis=AX.X))
            block.vector(_)
            return vs.n

        def barrier(tag, pre_v):
            def _(sync):
                sync.wait_ge(vs.h, pre_v)
                lc.inc(sync.dma_start(out=cc_in[tag], in_=srow[0:1, 0:8]), 16)
            block.sync(_)
            ldn = lc.n
            def _(gpsimd):
                gpsimd.wait_ge(lc.h, ldn)
                gs.inc(gpsimd.collective_compute(
                    "AllReduce", ALU.max, replica_groups=[list(range(8))],
                    ins=[cc_in[tag]], outs=[cc_out[tag]]))
            block.gpsimd(_)
            gn = gs.n
            def _(sync):
                sync.wait_ge(gs.h, gn)
                lc.inc(sync.dma_start(out=srow[0:1, 0:8], in_=cc_out[tag]), 16)
            block.sync(_)
            ldn2 = lc.n
            def _(vector):
                vector.wait_ge(lc.h, ldn2)
                vs.inc(vector.nop())
            block.vector(_)
            return vs.n

        def bcast_scols(pairs, pre_v, pre_extra=None):
            def _(tensor):
                tensor.wait_ge(vs.h, pre_v)
                if pre_extra is not None:
                    tensor.wait_ge(vs.h, pre_extra)
                ts.inc(tensor.matmul(pB[:, 0:16], ones_f[:], srow[:],
                                     start=True, stop=True))
            block.tensor(_)
            bt = ts.n
            def _(vector):
                vector.wait_ge(ts.h, bt)
                for (rc, cc_) in pairs:
                    inst = vector.tensor_copy(scols[:, cc_:cc_ + 1], pB[:, rc:rc + 1])
                vs.inc(inst)
            block.vector(_)
            return vs.n

        fx = fold_stat(statx[:, 0:NT], 0, amax_v)
        bx = barrier("X", fx)
        def _(vector):
            vector.wait_ge(vs.h, bx)
            def ch(inst):
                vs.inc(inst); vector.wait_ge(vs.h, vs.n)
            ch(vector.tensor_scalar(srow[0:1, 1:2], srow[0:1, 0:1], INV_QMAX, 1.0e-12,
                                    ALU.mult, ALU.max))
            ch(vector.reciprocal(srow[0:1, 2:3], srow[0:1, 1:2]))
        block.vector(_)
        bcx = bcast_scols([(1, SX), (2, ISX)], vs.n)
        rsdx = sb("rsdx", [128, NT], F32)
        nmbx = sb("nmbx", [128, NT], F32)
        def _(vector):
            vector.wait_ge(vs.h, bcx)
            def ch(inst):
                vs.inc(inst); vector.wait_ge(vs.h, vs.n)
            ch(vector.tensor_scalar(rsdx[:], rstdt[:], scols[:, ISX:ISX + 1], None, ALU.mult))
            ch(vector.tensor_scalar(nmbx[:], nmbt[:], scols[:, ISX:ISX + 1], None, ALU.mult))
        block.vector(_)
        bcx2 = vs.n

        xqT = sb("xqT", [128, NCH * T], BF16, stk3)
        wkt = [sb(f"wkt{i}", [128, C], F32, stk3) for i in range(2)]
        WupT = sb("WupT", [128, NCH * I], BF16, stk3)
        WgT = sb("WgT", [128, NCH * I], BF16, stk3)
        wrs = [sb(f"wrs{i}", [128, C], F32, stk3) for i in range(2)]
        wqb = [sb(f"wqb{i}", [128, C], BF16, stk3) for i in range(2)]
        blkmx = [sb(f"blkmx{i}", [128, 32], F32, stk3) for i in range(2)]
        binx = sb("binx", [128, 32], F32, stk3)

        wp_cast, wp_tr = [], []
        wp_i = [0]
        def emit_wprep_tile(Wsrc, dstT, rt):
            slot = wp_i[0] % 2
            wp_i[0] += 1
            def _(sync, rt=rt, slot=slot):
                if len(wp_cast) >= 2:
                    sync.wait_ge(vs.h, wp_cast[-2])
                elif not wp_cast:
                    sync.wait_ge(vs.h, bcx)
                    sync.wait_ge(wds.h, 128)
                lw[slot].inc(sync.dma_start(out=wrs[slot][:],
                                            in_=Wsrc[rt * 128:(rt + 1) * 128, :]), 16)
            block.sync(_)
            def _(vector, rt=rt, slot=slot):
                vector.wait_ge(lw[slot].h, lw[slot].n)
                if len(wp_tr) >= 2:
                    vector.wait_ge(ts.h, wp_tr[-2])
                def ch(inst):
                    vs.inc(inst); vector.wait_ge(vs.h, vs.n)
                w3 = wrs[slot][:].rearrange("p (b k) -> p b k", k=32)
                ch(vector.reduce_max(blkmx[slot][:], w3, axis=AX.X, apply_absolute_value=True))
                ch(vector.tensor_scalar(blkmx[slot][:], blkmx[slot][:], INV_WQMAX, 1.0e-12,
                                        ALU.mult, ALU.max))
                ch(vector.reciprocal(binx[:], blkmx[slot][:]))
                bmr = binx[:].unsqueeze(2).broadcast_to([128, 32, 32])
                ch(vector.tensor_tensor(w3, w3, bmr, ALU.mult))
                ch(rnd(vector, wrs[slot][:], wrs[slot][:]))
                bms = blkmx[slot][:].unsqueeze(2).broadcast_to([128, 32, 32])
                ch(vector.tensor_tensor(wqb[slot][:].rearrange("p (b k) -> p b k", k=32),
                                        w3, bms, ALU.mult))
                wp_cast.append(vs.n)
            block.vector(_)
            def _(tensor, rt=rt, slot=slot):
                tensor.wait_ge(vs.h, wp_cast[-1])
                tensor.wait_ge(ss.h, pt2g[0])
                for ci in range(NCH):
                    inst = tensor.transpose(pT2[:, ci * 128:(ci + 1) * 128],
                                            wqb[slot][:, ci * 128:(ci + 1) * 128], iden_b[:])
                ts.inc(inst)
                wp_tr.append(ts.n)
            block.tensor(_)
            def _(scalar, rt=rt, first=(wp_i[0] == 1)):
                scalar.wait_ge(ts.h, wp_tr[-1])
                if first:
                    scalar.wait_ge(wds.h, 128)
                for ci in range(NCH):
                    inst = scalar.copy(dstT[:, ci * I + rt * 128: ci * I + rt * 128 + 128],
                                       pT2[:, ci * 128:(ci + 1) * 128])
                ss.inc(inst)
                pt2g[0] = ss.n
            block.scalar(_)

        qdq_ev = []
        wtiles = [(Wu, WupT, rt) for rt in range(NI)] + [(Wg, WgT, rt) for rt in range(NI)]
        wi = 0
        for t in range(NT):
            slot = t % 2
            def _(sync, t=t, slot=slot):
                if t >= 2:
                    sync.wait_ge(ss.h, qdq_ev[-1])
                    sync.wait_ge(ss.h, ptg[0])
                else:
                    sync.wait_ge(sp[slot].h, X2_SP[slot])
                lx[slot].inc(sync.dma_start(out=x2t[slot][:],
                                            in_=x2_hbm[t * 128:(t + 1) * 128, :]), 16)
            block.sync(_)
            def _(scalar, t=t, slot=slot):
                scalar.wait_ge(lx[slot].h, lx[slot].n)
                scalar.wait_ge(vs.h, bcx2)
                if len(qdq_ev) >= 2:
                    scalar.wait_ge(ss.h, qdq_ev[-2])
                ss.inc(scalar.activation(wkt[slot][:], x2t[slot][:], ACTF.Identity,
                                         bias=nmbx[:, t:t + 1], scale=rsdx[:, t:t + 1]))
            block.scalar(_)
            y_s = ss.n
            def _(vector, t=t, slot=slot):
                vector.wait_ge(ss.h, y_s)
                vs.inc(rnd(vector, wkt[slot][:], wkt[slot][:]))
            block.vector(_)
            k_v = vs.n
            def _(scalar, t=t, slot=slot):
                scalar.wait_ge(vs.h, k_v)
                ss.inc(scalar.activation(xlt[slot][:], wkt[slot][:], ACTF.Copy,
                                         scale=scols[:, SX:SX + 1]))
            block.scalar(_)
            xq_s = ss.n
            def _(tensor, t=t, slot=slot):
                tensor.wait_ge(ss.h, xq_s)
                tensor.wait_ge(ss.h, ptg[0])
                tensor.wait_ge(vs.h, ptgv[0])
                for ci in range(NCH):
                    inst = tensor.transpose(pT[:, ci * 128:(ci + 1) * 128],
                                            xlt[slot][:, ci * 128:(ci + 1) * 128], iden_b[:])
                ts.inc(inst)
            block.tensor(_)
            trm = ts.n
            def _(scalar, t=t):
                scalar.wait_ge(ts.h, trm)
                if t == 0:
                    scalar.wait_ge(wds.h, 128)
                for ci in range(NCH):
                    inst = scalar.copy(xqT[:, ci * T + t * 128: ci * T + t * 128 + 128],
                                       pT[:, ci * 128:(ci + 1) * 128])
                ss.inc(inst)
                ptg[0] = ss.n
            block.scalar(_)
            qdq_ev.append(ss.n)
            nw = 2 if (t % 2 == 0) else 1
            for _k in range(nw):
                if wi < len(wtiles):
                    Ws, dT, rt = wtiles[wi]
                    emit_wprep_tile(Ws, dT, rt)
                    wi += 1
        while wi < len(wtiles):
            Ws, dT, rt = wtiles[wi]
            emit_wprep_tile(Ws, dT, rt)
            wi += 1
        P4_done = ss.n
        P4_wv = vs.n

        ustg = [sb("ustg0", [128, T], BF16, stk3)] * 2
        f1_ev = []
        f1_st = [0, 0]
        f1_stat_v = []

        f1_lo = []
        f1_hi = []
        def emit_big_mm(dstT_hbm, WT, stat, first, stat2=None):
            H = T // 2
            for ic in range(NI):
                slot = ic % 2
                banks = (pA, pB) if ic % 2 == 0 else (pC, pD)
                ev_here = []
                for f in range(NQ):
                    bank = banks[f % 2]
                    def _(tensor, ic=ic, f=f, bank=bank):
                        if first and ic == 0 and f == 0:
                            tensor.wait_ge(ss.h, P4_done)
                            tensor.wait_ge(vs.h, bcx)
                        if f >= 2:
                            tensor.wait_ge(ss.h, ev_here[f - 2])
                        elif f1_ev:
                            tensor.wait_ge(ss.h, f1_ev[-1])
                        for ci in range(NCH):
                            inst = tensor.matmul(
                                bank[:],
                                WT[:, ci * I + ic * 128: ci * I + ic * 128 + 128],
                                xqT[:, ci * T + f * 512: ci * T + (f + 1) * 512],
                                start=(ci == 0), stop=(ci == NCH - 1))
                        ts.inc(inst)
                    block.tensor(_)
                    mm = ts.n
                    def _(scalar, ic=ic, f=f, bank=bank, slot=slot):
                        scalar.wait_ge(ts.h, mm)
                        if f == 0 and NQ >= 2 and f1_lo:
                            psl, pv, pvs = f1_lo[-1]
                            scalar.wait_ge(sq[psl].h, pv)
                            scalar.wait_ge(vs.h, pvs)
                        if f == 0 and NQ < 2 and f1_hi:
                            psl, pv, pvs = f1_hi[-1]
                            scalar.wait_ge(sp[psl].h, pv)
                            scalar.wait_ge(vs.h, pvs)
                        if NQ >= 2 and f == NQ // 2 and f1_hi:
                            psl, pv, pvs = f1_hi[-1]
                            scalar.wait_ge(sp[psl].h, pv)
                            scalar.wait_ge(vs.h, pvs)
                        ss.inc(scalar.copy(ustg[slot][:, f * 512:(f + 1) * 512], bank[:]))
                        ev_here.append(ss.n)
                    block.scalar(_)
                    if NQ >= 2 and f == NQ // 2 - 1:
                        def _(vector, ic=ic, slot=slot):
                            vector.wait_ge(ss.h, ev_here[-1])
                            if stat2 is not None:
                                vector.reduce_max(stat2[:, 2 * ic:2 * ic + 1],
                                                  ustg[slot][:, 0:H], axis=AX.X)
                            vs.inc(vector.reduce_max(stat[:, 2 * ic:2 * ic + 1],
                                                     ustg[slot][:, 0:H], axis=AX.X,
                                                     apply_absolute_value=True))
                        block.vector(_)
                        vlo = vs.n
                        def _(gpsimd, ic=ic, slot=slot, evv=ev_here[-1]):
                            gpsimd.wait_ge(ss.h, evv)
                            sq[slot].inc(gpsimd.dma_start(
                                out=dstT_hbm[ic * 128:(ic + 1) * 128, 0:H],
                                in_=ustg[slot][:, 0:H]), 16)
                        block.gpsimd(_)
                        f1_lo.append((slot, sq[slot].n, vlo))
                f1_ev.append(ev_here[-1])
                lo = 0 if NQ >= 2 else None
                def _(vector, ic=ic, slot=slot, lo=lo):
                    vector.wait_ge(ss.h, ev_here[-1])
                    if lo is not None:
                        src = ustg[slot][:, H:T]
                        c0 = 2 * ic + 1
                    else:
                        src = ustg[slot][:]
                        c0 = 2 * ic
                    if stat2 is not None:
                        vector.reduce_max(stat2[:, c0:c0 + 1], src, axis=AX.X)
                        if lo is None:
                            vector.reduce_max(stat2[:, c0 + 1:c0 + 2], src, axis=AX.X)
                    if lo is None:
                        vector.reduce_max(stat[:, c0 + 1:c0 + 2], src, axis=AX.X,
                                          apply_absolute_value=True)
                    vs.inc(vector.reduce_max(stat[:, c0:c0 + 1], src, axis=AX.X,
                                             apply_absolute_value=True))
                    f1_stat_v.append(vs.n)
                block.vector(_)
                def _(gpsimd, ic=ic, slot=slot, lo=lo):
                    gpsimd.wait_ge(ss.h, ev_here[-1])
                    if lo is not None:
                        sp[slot].inc(gpsimd.dma_start(
                            out=dstT_hbm[ic * 128:(ic + 1) * 128, H:T],
                            in_=ustg[slot][:, H:T]), 16)
                    else:
                        sp[slot].inc(gpsimd.dma_start(
                            out=dstT_hbm[ic * 128:(ic + 1) * 128, :],
                            in_=ustg[slot][:]), 16)
                    f1_st[slot] = sp[slot].n
                block.gpsimd(_)
                f1_hi.append((slot, sp[slot].n, f1_stat_v[-1]))

        emit_big_mm(U_hbm, WupT, ustat, True)
        emit_big_mm(G_hbm, WgT, gstat, False, stat2=gstat2)
        F1_V = vs.n
        F1_SP = (sp[0].n, sp[1].n)
        F1_SQ = (sq[0].n, sq[1].n)

        fa1 = fold_stat(ustat[:, 0:2 * NI], 0, F1_V)
        fa2 = fold_stat(gstat[:, 0:2 * NI], 1, None)
        fa3 = fold_stat(gstat2[:, 0:2 * NI], 2, None, use_abs=False)
        ba = barrier("A", fa3)
        def _(vector):
            vector.wait_ge(vs.h, ba)
            def ch(inst):
                vs.inc(inst); vector.wait_ge(vs.h, vs.n)
            ch(vector.tensor_scalar(srow[0:1, 8:9], srow[0:1, 0:1], INV_QMAX, 1.0e-12,
                                    ALU.mult, ALU.max))
            ch(vector.reciprocal(srow[0:1, 9:10], srow[0:1, 8:9]))
            ch(vector.tensor_scalar(srow[0:1, 10:11], srow[0:1, 1:2], INV_QMAX, 1.0e-12,
                                    ALU.mult, ALU.max))
            ch(vector.reciprocal(srow[0:1, 11:12], srow[0:1, 10:11]))
            ch(vector.tensor_scalar(srow[0:1, 12:13], srow[0:1, 10:11], 1.0 / 65536.0,
                                    None, ALU.mult))
            ch(vector.tensor_tensor(srow[0:1, 3:4], srow[0:1, 2:3], srow[0:1, 11:12],
                                    ALU.mult))
            ch(vector.tensor_scalar(srow[0:1, 3:4], srow[0:1, 3:4], mcol[0:1, :],
                                    nmcol[0:1, :], ALU.add, ALU.add))
            ch(vector.tensor_tensor(srow[0:1, 3:4], srow[0:1, 3:4], srow[0:1, 10:11],
                                    ALU.mult))
        block.vector(_)
        smA = vs.n
        def _(scalar):
            scalar.wait_ge(vs.h, smA)
            ss.inc(scalar.activation(srow[0:1, 6:7], srow[0:1, 3:4], ACTF.Sigmoid))
        block.scalar(_)
        smAs = ss.n
        def _(vector):
            vector.wait_ge(ss.h, smAs)
            def ch(inst):
                vs.inc(inst); vector.wait_ge(vs.h, vs.n)
            ch(vector.tensor_scalar(srow[0:1, 6:7], srow[0:1, 6:7], 65536.0, U16_CAP,
                                    ALU.mult, ALU.min))
            ch(vector.tensor_scalar(srow[0:1, 6:7], srow[0:1, 6:7], mcol[0:1, :],
                                    nmcol[0:1, :], ALU.add, ALU.add))
            ch(vector.tensor_tensor(srow[0:1, 6:7], srow[0:1, 6:7], srow[0:1, 3:4],
                                    ALU.mult))
            ch(vector.tensor_scalar(srow[0:1, 6:7], srow[0:1, 6:7], 1.0 / 65536.0,
                                    None, ALU.mult))
            ch(vector.tensor_scalar(srow[0:1, 4:5], srow[0:1, 6:7], INV_QMAX, 1.0e-12,
                                    ALU.mult, ALU.max))
            ch(vector.reciprocal(srow[0:1, 5:6], srow[0:1, 4:5]))
            ch(vector.tensor_tensor(srow[0:1, 7:8], srow[0:1, 12:13], srow[0:1, 5:6],
                                    ALU.mult))
        block.vector(_)
        bca = bcast_scols([(8, SU), (9, ISU), (10, SG), (11, ISG), (12, SGQ),
                           (7, ISB)], vs.n)

        stk3.close()

        WdnT = sb("WdnT", [128, NI * C], BF16, stk4o)
        def _(sync):
            sync.wait_ge(sp[0].h, F1_SP[0])
            sync.wait_ge(sp[1].h, F1_SP[1])
            sync.wait_ge(sq[0].h, F1_SQ[0])
            sync.wait_ge(sq[1].h, F1_SQ[1])
            sync.wait_ge(wds.h, 16 * 8)
            for q in range(4):
                wl.inc(sync.dma_start(
                    out=WdnT[:, q * (NI * C // 4):(q + 1) * (NI * C // 4)],
                    in_=WdnT_hbm[:, q * (NI * C // 4):(q + 1) * (NI * C // 4)]), 16)
        block.sync(_)
        gbuf = [sb("gbuf0", [128, T], BF16, stk4a)] * 2
        ubuf = [sb(f"ubuf{i}", [128, T], BF16, stk4a) for i in range(2)]
        A1 = [sb(f"A1{i}", [128, T], F32, stk4a) for i in range(3)]
        A2 = [sb(f"A2{i}", [128, T], F32, stk4a) for i in range(2)]
        A3 = [sb(f"A3{i}", [128, T], F32, stk4a) for i in range(2)]
        f23_v, f23_pp, f23_yg, f23_sg, f23_yu = [], [], [], [], []
        qq_v = {}
        spq = {}
        def emit_t1(ic):
            slot = ic % 2
            s3 = ic % 3
            def _(vector, ic=ic, slot=slot, s3=s3):
                vector.wait_ge(ss.h, f23_sg[ic])
                def ch(inst):
                    vs.inc(inst); vector.wait_ge(vs.h, vs.n)
                ch(vector.tensor_scalar(A1[s3][:], A1[s3][:], 65536.0, U16_CAP,
                                        ALU.mult, ALU.min))
                ch(rnd(vector, A1[s3][:], A1[s3][:]))
                if ic >= 2:
                    vector.wait_ge(sp[slot].h, spq[ic - 2])
                ch(vector.tensor_tensor(A3[slot][:], A2[slot][:], A1[s3][:], ALU.mult))
                ch(vector.tensor_scalar(A1[s3][:], A3[slot][:], scols[:, ISB:ISB + 1],
                                        mcol[:], ALU.mult, ALU.add))
                ch(vector.tensor_scalar(A1[s3][:], A1[s3][:], nmcol[:], None, ALU.add))
                f23_pp.append(vs.n)
            block.vector(_)
        def emit_t2(ic):
            slot = ic % 2
            s3 = ic % 3
            def _(vector, ic=ic, slot=slot, s3=s3):
                vector.wait_ge(vs.h, f23_pp[ic])
                vector.wait_ge(lu[slot].h, 16 * (ic // 2 + 1))
                def ch(inst):
                    vs.inc(inst); vector.wait_ge(vs.h, vs.n)
                ch(vector.tensor_scalar(A2[slot][:], ubuf[slot][:], scols[:, ISU:ISU + 1],
                                        mcol[:], ALU.mult, ALU.add))
                ch(vector.tensor_scalar(A2[slot][:], A2[slot][:], nmcol[:], None, ALU.add))
                ch(vector.tensor_tensor(A3[slot][:], A1[s3][:], A2[slot][:], ALU.mult))
                qq_v[ic] = vs.n
                vs.inc(vector.reduce_max(qstat[:, ic:ic + 1], A3[slot][:], axis=AX.X,
                                         apply_absolute_value=True))
                f23_v.append(vs.n)
            block.vector(_)
            def _(gpsimd, ic=ic, slot=slot):
                gpsimd.wait_ge(vs.h, f23_v[-1])
                sp[slot].inc(gpsimd.dma_start(out=QQ_hbm[ic * 128:(ic + 1) * 128, :],
                                              in_=A3[slot][:]), 16)
                spq[ic] = sp[slot].n
            block.gpsimd(_)

        for ic in range(NI):
            slot = ic % 2
            if ic >= 2:
                emit_t2(ic - 2)
            def _(sync, ic=ic, slot=slot):
                if ic < 2:
                    sync.wait_ge(sp[0].h, F1_SP[0])
                    sync.wait_ge(sp[1].h, F1_SP[1])
                    sync.wait_ge(sq[0].h, F1_SQ[0])
                    sync.wait_ge(sq[1].h, F1_SQ[1])
                if f23_yg:
                    sync.wait_ge(ss.h, f23_yg[-1])
                lr[slot].inc(sync.dma_start(out=gbuf[slot][:],
                                            in_=G_hbm[ic * 128:(ic + 1) * 128, :]), 16)
            block.sync(_)
            s3 = ic % 3
            def _(scalar, ic=ic, slot=slot, s3=s3):
                scalar.wait_ge(lr[slot].h, lr[slot].n)
                scalar.wait_ge(vs.h, bca)
                if ic >= 3:
                    scalar.wait_ge(vs.h, qq_v[ic - 3])
                ss.inc(scalar.activation(A1[s3][:], gbuf[slot][:], ACTF.Copy,
                                         scale=scols[:, ISG:ISG + 1]))
                f23_yg.append(ss.n)
            block.scalar(_)
            yg_s = ss.n
            def _(vector, ic=ic, slot=slot, s3=s3):
                vector.wait_ge(ss.h, yg_s)
                if ic >= 2:
                    vector.wait_ge(vs.h, qq_v[ic - 2])
                vs.inc(rnd(vector, A2[slot][:], A1[s3][:]))
            block.vector(_)
            kg_v = vs.n
            def _(scalar, ic=ic, slot=slot, s3=s3):
                scalar.wait_ge(vs.h, kg_v)
                ss.inc(scalar.activation(A1[s3][:], A2[slot][:], ACTF.Sigmoid,
                                         scale=scols[:, SG:SG + 1]))
                f23_sg.append(ss.n)
            block.scalar(_)
            if ic >= 1:
                emit_t1(ic - 1)
            def _(sync, ic=ic, slot=slot):
                if ic < 2:
                    sync.wait_ge(sp[0].h, F1_SP[0])
                    sync.wait_ge(sp[1].h, F1_SP[1])
                    sync.wait_ge(sq[0].h, F1_SQ[0])
                    sync.wait_ge(sq[1].h, F1_SQ[1])
                else:
                    sync.wait_ge(vs.h, f23_v[ic - 2])
                lu[slot].inc(sync.dma_start(out=ubuf[slot][:],
                                              in_=U_hbm[ic * 128:(ic + 1) * 128, :]), 16)
            block.sync(_)
        emit_t2(NI - 2)
        emit_t1(NI - 1)
        emit_t2(NI - 1)
        F3_V = vs.n
        F3_SP = (sp[0].n, sp[1].n)

        fc = fold_stat(qstat[:, 0:NI], 0, F3_V)
        bc = barrier("Cc", fc)
        def _(vector):
            vector.wait_ge(vs.h, bc)
            def ch(inst):
                vs.inc(inst); vector.wait_ge(vs.h, vs.n)
            ch(vector.tensor_tensor(srow[0:1, 6:7], srow[0:1, 4:5], srow[0:1, 8:9],
                                    ALU.mult))
            ch(vector.tensor_tensor(srow[0:1, 7:8], srow[0:1, 0:1], srow[0:1, 6:7],
                                    ALU.mult))
            ch(vector.tensor_scalar(srow[0:1, 13:14], srow[0:1, 7:8], INV_QMAX, 1.0e-12,
                                    ALU.mult, ALU.max))
            ch(vector.reciprocal(srow[0:1, 14:15], srow[0:1, 13:14]))
            ch(vector.tensor_tensor(srow[0:1, 15:16], srow[0:1, 6:7], srow[0:1, 14:15],
                                    ALU.mult))
        block.vector(_)
        bcc = bcast_scols([(15, ISCQ), (13, SC)], vs.n)

        stk4a.close()
        oT = [sb(f"oT{i}", [128, NI * TQ], BF16, stk4b) for i in range(2)]
        qqb = [sb(f"qqb{i}", [128, TQ], F32, stk4b) for i in range(4)]
        lr4 = [lr[0], lr[1], mk("lr2"), mk("lr3")]
        oki = [sb(f"oki{i}", [128, TQ], F32, stk4b) for i in range(2)]

        out_v = []
        mmq_ms = {}
        dq_ms = {}
        row_lists = {}
        bank_user = {0: None, 1: None}

        def emit_dq_ic(tq, ic):
            qslot = tq % 2
            dq_prev = row_lists.get(tq - 1)
            row_ms = row_lists[tq]
            slot = ic % 2
            qs = ic % 4
            def _(sync, tq=tq, ic=ic, qs=qs):
                if tq == 0 and ic < 4:
                    sync.wait_ge(sp[0].h, F3_SP[0])
                    sync.wait_ge(sp[1].h, F3_SP[1])
                elif ic >= 4:
                    sync.wait_ge(vs.h, row_ms[ic - 4])
                else:
                    sync.wait_ge(vs.h, dq_prev[ic - 4])
                lr4[qs].inc(sync.dma_start(
                    out=qqb[qs][:],
                    in_=QQ_hbm[ic * 128:(ic + 1) * 128, tq * TQ:(tq + 1) * TQ]), 16)
            block.sync(_)
            def _(scalar, tq=tq, ic=ic, slot=slot, qs=qs):
                scalar.wait_ge(lr4[qs].h, lr4[qs].n)
                scalar.wait_ge(vs.h, bcc)
                if ic >= 2:
                    scalar.wait_ge(vs.h, row_ms[ic - 2])
                elif dq_prev is not None:
                    scalar.wait_ge(vs.h, dq_prev[ic - 2])
                ss.inc(scalar.activation(oki[slot][:], qqb[qs][:], ACTF.Copy,
                                         scale=scols[:, ISCQ:ISCQ + 1]))
            block.scalar(_)
            yq_s = ss.n
            def _(vector, tq=tq, ic=ic, qslot=qslot, slot=slot):
                vector.wait_ge(ss.h, yq_s)
                if tq >= 2 and ic == 0:
                    vector.wait_ge(ts.h, mmq_ms[tq - 2])
                vs.inc(rnd(vector, oki[slot][:], oki[slot][:]))
                vector.wait_ge(vs.h, vs.n)
                vs.inc(vector.tensor_scalar(oT[qslot][:, ic * TQ:(ic + 1) * TQ],
                                            oki[slot][:], scols[:, SC:SC + 1], None,
                                            ALU.mult))
                row_ms.append(vs.n)
            block.vector(_)
            if ic == NI - 1:
                dq_ms[tq] = vs.n

        def emit_mm_tile(tq, tt):
            qslot = tq % 2
            if True:
                g = tq * NTQ + tt
                banks = (pA, pB) if tt % 2 == 0 else (pC, pD)
                slot = g % 2
                def _(gpsimd, g=g, slot=slot):
                    if len(out_v) >= 2:
                        gpsimd.wait_ge(vs.h, out_v[-2])
                        gpsimd.wait_ge(so[slot].h, so[slot].n)
                    else:
                        gpsimd.wait_ge(vs.h, bcc)
                    lg[slot].inc(gpsimd.dma_start(out=x2t[slot][:],
                                                  in_=x2_hbm[g * 128:(g + 1) * 128, :]), 16)
                block.gpsimd(_)
                def _(tensor, tq=tq, tt=tt, banks=banks, qslot=qslot):
                    if tq == 0 and tt == 0:
                        tensor.wait_ge(wl.h, 64)
                    tensor.wait_ge(vs.h, dq_ms[tq])
                    bu = bank_user[tt % 2]
                    if bu is not None:
                        tensor.wait_ge(vs.h, out_v[bu])
                    else:
                        tensor.wait_ge(vs.h, bcc)
                    for h in range(2):
                        for ic in range(NI):
                            inst = tensor.matmul(
                                banks[h][:],
                                oT[qslot][:, ic * TQ + tt * 128: ic * TQ + tt * 128 + 128],
                                WdnT[:, ic * C + h * 512: ic * C + (h + 1) * 512],
                                start=(ic == 0), stop=(ic == NI - 1))
                    ts.inc(inst)
                block.tensor(_)
                mmf = ts.n
                if tt == NTQ - 1:
                    mmq_ms[tq] = ts.n
                def _(vector, g=g, banks=banks, slot=slot):
                    vector.wait_ge(ts.h, mmf)
                    vector.wait_ge(lg[slot].h, lg[slot].n)
                    vector.tensor_tensor(x2t[slot][:, 0:512], banks[0][:],
                                         x2t[slot][:, 0:512], ALU.add)
                    vs.inc(vector.tensor_tensor(x2t[slot][:, 512:1024], banks[1][:],
                                                x2t[slot][:, 512:1024], ALU.add))
                    out_v.append(vs.n)
                    bank_user[tt % 2] = len(out_v) - 1
                block.vector(_)
                def _(gpsimd, g=g, slot=slot):
                    gpsimd.wait_ge(vs.h, out_v[-1])
                    so[slot].inc(gpsimd.dma_start(out=out[g * 128:(g + 1) * 128, :],
                                                  in_=x2t[slot][:]), 16)
                block.gpsimd(_)

        kpt = -(-NI // NTQ)
        row_lists[0] = []
        for ic in range(NI):
            emit_dq_ic(0, ic)
        for tq in range(4):
            if tq < 3:
                row_lists[tq + 1] = []
            for tt in range(NTQ):
                if tq < 3:
                    for ic in range(tt * kpt, min((tt + 1) * kpt, NI)):
                        emit_dq_ic(tq + 1, ic)
                emit_mm_tile(tq, tt)

        def _(gpsimd):
            gpsimd.wait_ge(so[0].h, so[0].n)
            gpsimd.wait_ge(so[1].h, so[1].n)
        block.gpsimd(_)

    with nc.Block() as rb:
        def _replay(eng, ops):
            for attr, a, k, inst in ops:
                real = getattr(eng, attr)(*a, **k)
                for sem, v in inst.incs:
                    real.then_inc(sem, v)
        rb.sync(lambda e: _replay(e, block.eng["sync"].ops))
        rb.scalar(lambda e: _replay(e, block.eng["scalar"].ops))
        rb.vector(lambda e: _replay(e, block.eng["vector"].ops))
        rb.tensor(lambda e: _replay(e, block.eng["tensor"].ops))
        rb.gpsimd(lambda e: _replay(e, block.eng["gpsimd"].ops))
    stk4b.close()
    stk4o.close()

    return nc


_NC_CACHE = {}


def _consts():
    iden = np.eye(128, dtype=np.float32)
    tril = np.tril(np.ones((128, 128), dtype=bool))
    maskneg = np.where(tril, np.float32(0.0), np.float32(NEGBIG)).astype(np.float32)
    ones = np.ones((1, 128), dtype=np.float32)
    return iden, maskneg, ones


def make_in_maps(x, Wb, Wv, bv, Wo, bo, Wg, Wu, Wd):
    iden, maskneg, ones = _consts()
    base = {
        "Wb": np.ascontiguousarray(np.asarray(Wb, np.float32)),
        "Wv": np.ascontiguousarray(np.asarray(Wv, np.float32)),
        "bv": np.asarray(bv, np.float32).reshape(1, C),
        "Wo": np.ascontiguousarray(np.asarray(Wo, np.float32)),
        "bo": np.asarray(bo, np.float32).reshape(1, C),
        "Wg": np.ascontiguousarray(np.asarray(Wg, np.float32)),
        "Wu": np.ascontiguousarray(np.asarray(Wu, np.float32)),
        "Wd": np.ascontiguousarray(np.asarray(Wd, np.float32)),
        "iden_in": iden, "maskneg_in": maskneg, "ones_in": ones,
    }
    return [dict(base, x_in=np.ascontiguousarray(np.asarray(xb, np.float32)))
            for xb in x]


def kernel(L0_attn_hidden_states, L0_attn_W_basis, L0_attn_W_v, L0_attn_b_v,
           L0_attn_W_o, L0_attn_b_o, L0_ffn_w_gate, L0_ffn_w_up, L0_ffn_w_down):
    x = np.ascontiguousarray(np.asarray(L0_attn_hidden_states, dtype=np.float32))
    B, T, C_ = x.shape
    assert C_ == C and B == 8
    if T not in _NC_CACHE:
        _NC_CACHE[T] = build_nc(T)
    nc = _NC_CACHE[T]
    in_maps = make_in_maps(x, L0_attn_W_basis, L0_attn_W_v, L0_attn_b_v,
                           L0_attn_W_o, L0_attn_b_o, L0_ffn_w_gate, L0_ffn_w_up,
                           L0_ffn_w_down)
    res = run_bass_kernel_spmd(nc, in_maps, core_ids=list(range(B)))
    return np.stack([np.asarray(res.results[b]["out"]) for b in range(B)], axis=0)
